# version 03
# baseline (speedup 1.0000x reference)
import os
import numpy as np
from contextlib import ExitStack
import concourse.bass as bass
import concourse.mybir as mybir
from concourse.bass_utils import run_bass_kernel_spmd

F32 = mybir.dt.float32
BF16 = mybir.dt.bfloat16
AF = mybir.ActivationFunctionType
ALU = mybir.AluOpType
AX = mybir.AxisListType

T = 2048
D = 1024
NT = 16
ND = 8
KSEL = 1536
GC1 = 0.7978845608028654
GC2 = 0.044715
EPS = 1e-5
NEG = -1e30


def f32(v):
    return np.float32(v)


class Sem:
    def __init__(self, h, is_dma=False):
        self.h = h
        self.n = 0
        self.is_dma = is_dma


class Em:
    ENGS = ("t", "v", "s", "g", "y")

    def __init__(self, nc):
        self.nc = nc
        self.L = {e: [] for e in self.ENGS}
        self._stat = None
        self._engsem = {}

    def set_fences(self, stat, sems_by_eng):
        self._stat = stat
        self._engsem = sems_by_eng

    def _fence(self, peng):
        stat = self._stat
        sem = self._engsem[peng]
        if peng == "v":
            self.op("v", lambda v: v.tensor_copy(stat[:, 61:62],
                                                 stat[:, 60:61]), sem)
        elif peng == "s":
            self.op("s", lambda s: s.copy(stat[:, 62:63],
                                          stat[:, 60:61]), sem)
        elif peng == "g":
            self.op("g", lambda g: g.tensor_copy(stat[:, 63:64],
                                                 stat[:, 60:61]), sem)

    def op(self, eng, fn, sem=None, k=1):
        if sem is not None:
            sem.n += k
            kk = k
            hh = sem.h
            self.L[eng].append(lambda e: fn(e).then_inc(hh, kk))
        else:
            self.L[eng].append(fn)

    DMA_SETTLE_CYC = 800

    def wait(self, eng, sem, val=None):
        if val is None and self._stat is not None:
            for peng, psem in self._engsem.items():
                if sem is psem and eng != peng:
                    self._fence(peng)
                    break
        v = sem.n if val is None else val
        hh = sem.h
        self.L[eng].append(lambda e: e.wait_ge(hh, v))
        if sem.is_dma:
            cyc = self.DMA_SETTLE_CYC
            self.L[eng].append(lambda e: e.nop(cycle_cnt=cyc))

    def flush(self, blk):
        if self._stat is not None:
            for peng in ("v", "s", "g"):
                if self.L[peng]:
                    self._fence(peng)
        names = {"t": "tensor", "v": "vector", "s": "scalar",
                 "g": "gpsimd", "y": "sync"}
        for e in self.ENGS:
            lst = self.L[e]
            if not lst:
                continue

            def body(eng, lst=lst):
                for f in lst:
                    f(eng)
            getattr(blk, names[e])(body)
        self.L = {e: [] for e in self.ENGS}


def mk_em(env):
    em = Em(env["nc"])
    em.set_fences(env["STAT"], {"v": env["sv"], "s": env["ss"],
                                "g": env["sg"]})
    return em


def build(do_l0=True, do_l1=True):
    nc = bass.Bass("TRN2", target_bir_lowering=False)

    xin = nc.dram_tensor("x", [T, D], F32, kind="ExternalInput").ap()
    out = nc.dram_tensor("out", [T, D], F32, kind="ExternalOutput").ap()
    W = {}
    if do_l0:
        W["wq0"] = nc.dram_tensor("wq0", [D, D], F32, kind="ExternalInput").ap()
        W["bq0"] = nc.dram_tensor("bq0", [D], F32, kind="ExternalInput").ap()
        if not do_l1:
            W["bq0t"] = nc.dram_tensor("bq0t", [128, 8], F32,
                                       kind="ExternalInput").ap()
        W["wr0"] = nc.dram_tensor("wr0", [D], F32, kind="ExternalInput").ap()
        W["wb0"] = nc.dram_tensor("wb0", [D, D], F32, kind="ExternalInput").ap()
    if do_l1:
        W["wq1"] = nc.dram_tensor("wq1", [D, D], F32, kind="ExternalInput").ap()
        W["bq1"] = nc.dram_tensor("bq1", [128, 8], F32, kind="ExternalInput").ap()
        W["wk1"] = nc.dram_tensor("wk1", [128, D], F32, kind="ExternalInput").ap()
        W["bk1"] = nc.dram_tensor("bk1", [128, 1], F32, kind="ExternalInput").ap()
        W["wv1"] = nc.dram_tensor("wv1", [128, D], F32, kind="ExternalInput").ap()
        W["bv1"] = nc.dram_tensor("bv1", [128, 1], F32, kind="ExternalInput").ap()
        W["wo1"] = nc.dram_tensor("wo1", [D, D], F32, kind="ExternalInput").ap()
        W["bo1"] = nc.dram_tensor("bo1", [D], F32, kind="ExternalInput").ap()
        W["mr1"] = nc.dram_tensor("mr1", [128, 64], F32, kind="ExternalInput").ap()
        W["bq0t"] = nc.dram_tensor("bq0t", [128, 8], F32, kind="ExternalInput").ap()
        W["w13"] = nc.dram_tensor("w13", [8, 32, 128, 1024], F32,
                                  kind="ExternalInput").ap()
        W["w2"] = nc.dram_tensor("w2", [8, T, D], F32, kind="ExternalInput").ap()
        dbg = "ExternalOutput" if os.environ.get("MK_DBG") else "Internal"
        W["x1d"] = nc.dram_tensor("x1d", [T, D], F32, kind=dbg).ap()
        W["s1d"] = nc.dram_tensor("s1d", [T, D], F32, kind=dbg).ap()

    with ExitStack() as _es1:
        ent = _es1.enter_context
        A = ent(nc.sbuf_tensor("A", [128, NT * D], BF16))
        WB1 = ent(nc.sbuf_tensor("WB1", [128, ND * D], BF16))
        STG0 = ent(nc.sbuf_tensor("STG0", [128, D], F32))
        STG1 = ent(nc.sbuf_tensor("STG1", [128, D], F32))
        idb = ent(nc.sbuf_tensor("idb", [128, 128], BF16))
        idf = ent(nc.sbuf_tensor("idf", [128, 128], F32))
        STAT = ent(nc.sbuf_tensor("STAT", [128, 64], F32))
        GATES = ent(nc.sbuf_tensor("GATES", [128, 128], F32))
        MASK = ent(nc.sbuf_tensor("MASK", [128, 16], F32))
        BB = ent(nc.sbuf_tensor("BB", [128, D], F32))
        bqp = ent(nc.sbuf_tensor("bqp", [128, 8], F32))
        ROW = ent(nc.sbuf_tensor("ROW", [1, D], F32))
        ones1 = ent(nc.sbuf_tensor("ones1", [1, 128], F32))
        _sd = ent(nc.semaphore("sd"))
        _sd2 = ent(nc.semaphore("sd2"))
        _sdo = ent(nc.semaphore("sdo"))
        _sv = ent(nc.semaphore("sv"))
        _ss = ent(nc.semaphore("ss"))
        _st = ent(nc.semaphore("st"))
        _sg = ent(nc.semaphore("sg"))
        _sd3 = ent(nc.semaphore("sd3"))
        _sd4 = ent(nc.semaphore("sd4"))
        env = dict(nc=nc, A=A, WB1=WB1, STG=(STG0, STG1), idb=idb, idf=idf,
                   STAT=STAT, GATES=GATES, MASK=MASK, BB=BB, bqp=bqp,
                   ROW=ROW, ones1=ones1, W=W, xin=xin, out=out,
                   sd=Sem(_sd, True), sd2=Sem(_sd2, True), sdo=Sem(_sdo, True),
                   sv=Sem(_sv), ss=Sem(_ss), st=Sem(_st), sg=Sem(_sg),
                   sd3=Sem(_sd3, True), sd4=Sem(_sd4, True))

        def fence_v(em):
            em.op("v", lambda v: v.tensor_copy(STAT[:, 61:62],
                                               STAT[:, 60:61]), env["sv"])

        def fence_s(em):
            em.op("s", lambda s: s.copy(STAT[:, 62:63], STAT[:, 60:61]),
                  env["ss"])
        env["fence_v"] = fence_v
        env["fence_s"] = fence_s

        em = mk_em(env)
        em.op("g", lambda g: g.memset(idb[:, :], 0.0))
        em.op("g", lambda g: g.affine_select(
            out=idb[:, :], in_=idb[:, :], compare_op=ALU.not_equal, fill=1.0,
            base=0, pattern=[[-1, 128]], channel_multiplier=1))
        em.op("g", lambda g: g.memset(idf[:, :], 0.0))
        em.op("g", lambda g: g.affine_select(
            out=idf[:, :], in_=idf[:, :], compare_op=ALU.not_equal, fill=1.0,
            base=0, pattern=[[-1, 128]], channel_multiplier=1))
        em.op("g", lambda g: g.memset(ones1[:, :], 1.0))
        with nc.Block() as blk:
            em.flush(blk)

        with (
            nc.sbuf_tensor("X", [128, NT * D], F32) as X,
            nc.sbuf_tensor("B", [128, NT * D], BF16) as B,
        ):
            env["X"] = X
            env["B"] = B
            em = mk_em(env)
            em.op("y", lambda y: y.dma_start(
                out=X[:, :].rearrange("p (n m) -> p n m", n=NT),
                in_=xin.rearrange("(n p) m -> p n m", p=128)), env["sd"], 16)
            em.wait("y", env["sd"])
            with nc.Block() as blk:
                em.flush(blk)

            if do_l0:
                build_l0(env)
            if do_l1:
                build_l1_part1(env)
            else:
                em = mk_em(env)
                em.op("y", lambda y: y.dma_start(
                    out=out.rearrange("(n p) m -> p n m", p=128),
                    in_=X[:, :].rearrange("p (n m) -> p n m", n=NT)),
                    env["sdo"], 16)
                with nc.Block() as blk:
                    em.flush(blk)

        if do_l1:
            build_l1_moe(env)

        em = mk_em(env)
        em.wait("y", env["sdo"])
        with nc.Block() as blk:
            em.flush(blk)

    return nc


def load_weight_dxe(env, em, src_ap, ncols, oview=None):
    nc = env["nc"]
    WB1, (STG0, STG1) = env["WB1"], env["STG"]
    sd, sv, ss = env["sd"], env["sv"], env["ss"]
    for dc in range(ND):
        stg = (STG0, STG1)[dc % 2]
        if dc >= 2:
            em.wait("y", sv if dc % 2 == 0 else ss)
        ov = (lambda a: a) if oview is None else oview
        em.op("y", lambda y, s=src_ap(dc), g=stg, nco=ncols, o=ov:
              y.dma_start(out=o(g[:, 0:nco]), in_=s), sd, 16)
        em.wait("v" if dc % 2 == 0 else "s", sd)
        eng = "v" if dc % 2 == 0 else "s"
        if eng == "v":
            em.op("v", lambda v, g=stg, dcc=dc, nco=ncols:
                  v.tensor_copy(WB1[:, dcc * nco:(dcc + 1) * nco], g[:, 0:nco]), sv)
        else:
            em.op("s", lambda s, g=stg, dcc=dc, nco=ncols:
                  s.copy(WB1[:, dcc * nco:(dcc + 1) * nco], g[:, 0:nco]), ss)


def emit_ln_stats(env, em, c, xn32, sq):
    X, STAT = env["X"], env["STAT"]
    sv, ss = env["sv"], env["ss"]
    xc = X[:, c * D:(c + 1) * D]
    em.op("v", lambda v: v.reduce_sum(STAT[:, 48:49], xc, axis=AX.X), sv)
    em.wait("v", sv)
    em.op("v", lambda v: v.tensor_scalar(
        STAT[:, c:c + 1], STAT[:, 48:49], 1.0 / D, None, op0=ALU.mult), sv)
    em.wait("v", sv)
    em.op("v", lambda v: v.tensor_scalar(
        xn32[:, :], xc, STAT[:, c:c + 1], None, op0=ALU.subtract), sv)
    em.wait("s", sv)
    em.op("s", lambda s: s.activation(sq[:, 0:D], xn32[:, :], AF.Square,
                                      accum_out=STAT[:, 50:51]), ss)
    em.wait("v", ss)
    em.op("v", lambda v: v.tensor_scalar(
        STAT[:, 51:52], STAT[:, 50:51], 1.0 / D, None, op0=ALU.mult), sv)
    em.wait("s", sv)
    em.op("s", lambda s: s.activation(STAT[:, 52:53], STAT[:, 51:52], AF.Ln,
                                      bias=f32(EPS)), ss)
    em.wait("s", ss)
    em.op("s", lambda s: s.activation(STAT[:, 16 + c:17 + c], STAT[:, 52:53],
                                      AF.Exp, scale=f32(-0.5)), ss)
    em.wait("v", ss)
    em.op("v", lambda v: v.tensor_scalar(
        xn32[:, :], xn32[:, :], STAT[:, 16 + c:17 + c], None, op0=ALU.mult), sv)
    em.wait("v", sv)


def build_l0(env):
    nc = env["nc"]
    A, B, X, WB1, STAT, MASK, BB, bqp, ROW, ones1, idf, idb = (
        env["A"], env["B"], env["X"], env["WB1"], env["STAT"], env["MASK"],
        env["BB"], env["bqp"], env["ROW"], env["ones1"], env["idf"], env["idb"])
    W = env["W"]
    sd, sv, ss, st, sg = env["sd"], env["sv"], env["ss"], env["st"], env["sg"]

    with ExitStack() as _es2:
        ent = _es2.enter_context
        srow = ent(nc.sbuf_tensor("srow", [1, T], F32))
        s_row = ent(nc.sbuf_tensor("s_row", [128, T], F32))
        XN32 = ent(nc.sbuf_tensor("XN32", [128, D], F32))
        SC = ent(nc.sbuf_tensor("SC", [128, D], F32))
        wrb = ent(nc.sbuf_tensor("wrb", [128, D], F32))
        em = mk_em(env)
        load_weight_dxe(env, em, lambda dc: W["wq0"][dc * 128:(dc + 1) * 128, :], D)
        em.op("y", lambda y: y.dma_start(out=bqp[:, :], in_=W["bq0t"]),
              sd, 16)
        em.op("y", lambda y: y.dma_start(out=ROW[:, :], in_=W["bq0"].rearrange("(a d) -> a d", a=1)),
              sd, 16)
        em.wait("y", sd)
        with nc.Block() as blk:
            em.flush(blk)

        with nc.psum_tensor("bc_ps", [128, 512], F32) as bc_ps:
            em = mk_em(env)
            for sl in range(2):
                em.op("t", lambda t, s=sl: t.matmul(
                    bc_ps[:, :], ones1[:, :], ROW[:, s * 512:(s + 1) * 512],
                    start=True, stop=True), st)
                em.wait("v", st)
                em.op("v", lambda v, s=sl: v.tensor_copy(
                    BB[:, s * 512:(s + 1) * 512], bc_ps[:, :]), sv)
                em.wait("t", sv)
            with nc.Block() as blk:
                em.flush(blk)

            em = mk_em(env)
            em.op("y", lambda y: y.dma_start(out=ROW[:, :], in_=W["wr0"].rearrange("(a d) -> a d", a=1)),
                  sd, 16)
            em.wait("t", sd)
            for sl in range(2):
                em.op("t", lambda t, s=sl: t.matmul(
                    bc_ps[:, :], ones1[:, :], ROW[:, s * 512:(s + 1) * 512],
                    start=True, stop=True), st)
                em.wait("v", st)
                em.op("v", lambda v, s=sl: v.tensor_copy(
                    wrb[:, s * 512:(s + 1) * 512], bc_ps[:, :]), sv)
                em.wait("t", sv)
            with nc.Block() as blk:
                em.flush(blk)

        with ExitStack() as _es3:
            ent = _es3.enter_context
            XNF = ent(nc.sbuf_tensor("XNF", [128, D], BF16))
            tp0 = ent(nc.psum_tensor("tp0", [128, 128], F32))
            tp1 = ent(nc.psum_tensor("tp1", [128, 128], F32))
            qf0 = ent(nc.psum_tensor("qf0", [128, 128], F32))
            qf1 = ent(nc.psum_tensor("qf1", [128, 128], F32))
            qt0 = ent(nc.psum_tensor("qt0", [128, 512], F32))
            qt1 = ent(nc.psum_tensor("qt1", [128, 512], F32))
            for c in range(NT):
                em = mk_em(env)
                emit_ln_stats(env, em, c, XN32, s_row)
                em.op("v", lambda v: v.tensor_tensor(
                    s_row[:, 0:D], XN32[:, :], wrb[:, :], op=ALU.mult), sv)
                em.op("v", lambda v, cc=c: v.reduce_sum(
                    STAT[:, 32 + cc:33 + cc], s_row[:, 0:D], axis=AX.X), sv)
                em.wait("t", sv)
                tps = (tp0, tp1)
                for dc in range(ND):
                    if dc >= 2:
                        em.wait("t", sv)
                    em.op("t", lambda t, d=dc, p=tps[dc % 2]: t.transpose(
                        p[:, :], XN32[:, d * 128:(d + 1) * 128], idf[:, :]), st)
                    em.wait("v", st)
                    em.op("v", lambda v, d=dc, p=tps[dc % 2]: v.tensor_copy(
                        XNF[:, d * 128:(d + 1) * 128], p[:, :]), sv)
                env["fence_v"](em)
                em.wait("t", sv)
                qfs = (qf0, qf1)
                for i in range(ND):
                    ps = qfs[i % 2]
                    if i >= 2:
                        em.wait("t", sv, sv.n)
                    for dc in range(ND):
                        em.op("t", lambda t, ii=i, d=dc, p=ps: t.matmul(
                            p[:, :],
                            WB1[:, d * D + ii * 128:d * D + (ii + 1) * 128],
                            XNF[:, d * 128:(d + 1) * 128],
                            start=(d == 0), stop=(d == ND - 1)),
                            st if dc == ND - 1 else None)
                    em.wait("v", st)
                    em.op("v", lambda v, ii=i, cc=c, p=ps: v.tensor_scalar(
                        A[:, ii * T + cc * 128:ii * T + (cc + 1) * 128],
                        p[:, :], bqp[:, ii:ii + 1], None, op0=ALU.add), sv)
                qts = (qt0, qt1)
                for h in range(2):
                    ps = qts[h]
                    for dc in range(ND):
                        em.op("t", lambda t, hh=h, d=dc, p=ps: t.matmul(
                            p[:, :],
                            XNF[:, d * 128:(d + 1) * 128],
                            WB1[:, d * D + hh * 512:d * D + (hh + 1) * 512],
                            start=(d == 0), stop=(d == ND - 1)),
                            st if dc == ND - 1 else None)
                    em.wait("v", st)
                    em.op("v", lambda v, hh=h, cc=c, p=ps: v.tensor_tensor(
                        B[:, cc * D + hh * 512:cc * D + (hh + 1) * 512],
                        p[:, :], BB[:, hh * 512:(hh + 1) * 512], op=ALU.add), sv)
                with nc.Block() as blk:
                    em.flush(blk)

        with (
            nc.psum_tensor("trp", [1, 128], F32) as trp,
            nc.psum_tensor("bcp", [128, 512], F32) as bcp,
        ):
            em = mk_em(env)
            for c in range(NT):
                em.op("t", lambda t, cc=c: t.transpose(
                    trp[:, :], STAT[:, 32 + cc:33 + cc], idf[:, :]), st)
                em.wait("v", st)
                em.op("v", lambda v, cc=c: v.tensor_copy(
                    srow[:, cc * 128:(cc + 1) * 128], trp[:, :]), sv)
                em.wait("t", sv)
            with nc.Block() as blk:
                em.flush(blk)
            em = mk_em(env)
            for sl in range(4):
                em.op("t", lambda t, s=sl: t.matmul(
                    bcp[:, :], ones1[:, :], srow[:, s * 512:(s + 1) * 512],
                    start=True, stop=True), st)
                em.wait("v", st)
                em.op("v", lambda v, s=sl: v.tensor_copy(
                    s_row[:, s * 512:(s + 1) * 512], bcp[:, :]), sv)
                em.wait("t", sv)
            with nc.Block() as blk:
                em.flush(blk)
            em = mk_em(env)
            for c in range(NT):
                for sl in range(4):
                    em.op("v", lambda v, cc=c, s=sl: v.tensor_scalar(
                        SC[:, 0:512], s_row[:, s * 512:(s + 1) * 512],
                        STAT[:, 32 + cc:33 + cc], 0.0, op0=ALU.is_gt,
                        op1=ALU.add, accum_out=STAT[:, 53 + s:54 + s]), sv)
                    em.wait("v", sv)
                em.op("v", lambda v: v.tensor_tensor(
                    STAT[:, 53:54], STAT[:, 53:54], STAT[:, 54:55], op=ALU.add), sv)
                em.wait("v", sv)
                em.op("v", lambda v: v.tensor_tensor(
                    STAT[:, 55:56], STAT[:, 55:56], STAT[:, 56:57], op=ALU.add), sv)
                em.wait("v", sv)
                em.op("v", lambda v: v.tensor_tensor(
                    STAT[:, 53:54], STAT[:, 53:54], STAT[:, 55:56], op=ALU.add), sv)
                em.wait("v", sv)
                em.op("v", lambda v, cc=c: v.tensor_scalar(
                    MASK[:, cc:cc + 1], STAT[:, 53:54], float(KSEL), None,
                    op0=ALU.is_lt), sv)
                em.wait("v", sv)
            with nc.Block() as blk:
                em.flush(blk)

        if os.environ.get("MK_DBG6"):
            qdbg6 = nc.dram_tensor("qdbg6", [128, NT * D], F32,
                                   kind="ExternalOutput").ap()
            bdbg6 = nc.dram_tensor("bdbg6", [128, NT * D], F32,
                                   kind="ExternalOutput").ap()
            wdbg6 = nc.dram_tensor("wdbg6", [128, ND * D], F32,
                                   kind="ExternalOutput").ap()
            em = mk_em(env)
            for seg in range(ND):
                em.op("v", lambda v, s2=seg: v.tensor_copy(
                    XN32[:, :], WB1[:, s2 * D:(s2 + 1) * D]), sv)
                em.wait("y", sv)
                em.op("y", lambda y, s2=seg: y.dma_start(
                    out=wdbg6[:, s2 * D:(s2 + 1) * D], in_=XN32[:, :]),
                    env["sdo"], 16)
                em.wait("v", env["sdo"])
            for seg in range(NT):
                em.op("v", lambda v, s2=seg: v.tensor_copy(
                    XN32[:, :], A[:, s2 * D:(s2 + 1) * D]), sv)
                em.wait("y", sv)
                em.op("y", lambda y, s2=seg: y.dma_start(
                    out=qdbg6[:, s2 * D:(s2 + 1) * D], in_=XN32[:, :]),
                    env["sdo"], 16)
                em.wait("v", env["sdo"])
                em.op("v", lambda v, s2=seg: v.tensor_copy(
                    XN32[:, :], B[:, s2 * D:(s2 + 1) * D]), sv)
                em.wait("y", sv)
                em.op("y", lambda y, s2=seg: y.dma_start(
                    out=bdbg6[:, s2 * D:(s2 + 1) * D], in_=XN32[:, :]),
                    env["sdo"], 16)
                em.wait("v", env["sdo"])
            with nc.Block() as blk:
                em.flush(blk)

        em = mk_em(env)
        load_weight_dxe(env, em, lambda dc: W["wb0"][dc * 128:(dc + 1) * 128, :], D)
        em.wait("y", env["sd"])
        with nc.Block() as blk:
            em.flush(blk)

        with ExitStack() as _es4:
            ent = _es4.enter_context
            Pb = ent(nc.sbuf_tensor("Pb", [128, T], BF16))
            PT = ent(nc.sbuf_tensor("PT", [128, 512], BF16))
            Pg = ent(nc.sbuf_tensor("Pg", [128, D], F32))
            XNF3 = ent(nc.sbuf_tensor("XNF3", [128, D], BF16))
            scp0 = ent(nc.psum_tensor("scp0", [128, 512], F32))
            scp1 = ent(nc.psum_tensor("scp1", [128, 512], F32))
            ptp0 = ent(nc.psum_tensor("ptp0", [128, 128], BF16))
            ptp1 = ent(nc.psum_tensor("ptp1", [128, 128], BF16))
            pvp0 = ent(nc.psum_tensor("pvp0", [128, 512], F32))
            pvp1 = ent(nc.psum_tensor("pvp1", [128, 512], F32))
            tpf0 = ent(nc.psum_tensor("tpf0", [128, 128], F32))
            tpf1 = ent(nc.psum_tensor("tpf1", [128, 128], F32))
            pgp0, pgp1 = scp0, scp1
            for c in range(NT):
                em = mk_em(env)
                scs = (scp0, scp1)
                AB2 = os.environ.get("MK_AB", "") == "2"
                for sl in (range(4) if not AB2 else []):
                    ps = scs[sl % 2]
                    if sl >= 2:
                        em.wait("t", sv)
                    for i in range(ND):
                        em.op("t", lambda t, s=sl, ii=i, p=ps: t.matmul(
                            p[:, :],
                            A[:, ii * T + c * 128:ii * T + (c + 1) * 128],
                            A[:, ii * T + s * 512:ii * T + (s + 1) * 512],
                            start=(ii == 0), stop=(ii == ND - 1)),
                            st if i == ND - 1 else None)
                    em.wait("v", st)
                    em.op("v", lambda v, s=sl, p=ps: v.tensor_copy(
                        s_row[:, s * 512:(s + 1) * 512], p[:, :]), sv)
                if AB2:
                    em.op("v", lambda v: v.memset(SC[:, :], 1.0), sv)
                em.wait("v", sv)
                if not AB2:
                    em.op("v", lambda v: v.reduce_max(
                        STAT[:, 48:49], s_row[:, :], axis=AX.X, negate=True), sv)
                    em.wait("v", sv)
                    em.op("v", lambda v: v.tensor_scalar(
                        STAT[:, 49:50], STAT[:, 48:49], 1.0 / 32.0, None,
                        op0=ALU.mult), sv)
                    em.wait("s", sv)
                    em.op("s", lambda s: s.activation(
                        Pb[:, :], s_row[:, :], AF.Exp, bias=STAT[:, 49:50],
                        scale=f32(1.0 / 32.0), accum_out=STAT[:, 50:51]), ss)
                    em.wait("v", ss)
                    em.op("v", lambda v: v.reciprocal(STAT[:, 51:52], STAT[:, 50:51]),
                          sv)
                    pts = (ptp0, ptp1)
                    env["fence_s"](em)
                    em.wait("t", ss)
                    for jb in range(4):
                        for jo in range(4):
                            j = jb * 4 + jo
                            if j >= 2:
                                em.wait("t", ss)
                            em.op("t", lambda t, jj=j, p=pts[j % 2]: t.transpose(
                                p[:, :], Pb[:, jj * 128:(jj + 1) * 128],
                                idb[:, :]), st)
                            em.wait("s", st)
                            em.op("s", lambda s, jj=j, o=jo, p=pts[j % 2]: s.copy(
                                PT[:, o * 128:(o + 1) * 128], p[:, :]), ss)
                        env["fence_s"](em)
                        em.wait("t", ss)
                        for jo in range(4):
                            j = jb * 4 + jo
                            for h in range(2):
                                em.op("t", lambda t, jj=j, o=jo, hh=h,
                                      p=(pvp0, pvp1)[h]: t.matmul(
                                    p[:, :],
                                    PT[:, o * 128:(o + 1) * 128],
                                    B[:, jj * D + hh * 512:jj * D + (hh + 1) * 512],
                                    start=(jj == 0), stop=(jj == NT - 1)),
                                    st if (j == NT - 1 and h == 1) else None)
                    em.wait("v", st)
                    em.wait("v", sv, sv.n)
                    for h in range(2):
                        em.op("v", lambda v, hh=h, p=(pvp0, pvp1)[h]: v.tensor_scalar(
                            SC[:, hh * 512:(hh + 1) * 512], p[:, :],
                            STAT[:, 51:52], None, op0=ALU.mult), sv)
                em.op("v", lambda v: v.tensor_scalar(
                    XN32[:, :], X[:, c * D:(c + 1) * D], STAT[:, c:c + 1], None,
                    op0=ALU.subtract), sv)
                em.wait("v", sv)
                em.op("v", lambda v: v.tensor_scalar(
                    XN32[:, :], XN32[:, :], STAT[:, 16 + c:17 + c], None,
                    op0=ALU.mult), sv)
                AB = os.environ.get("MK_AB", "")
                em.wait("t", sv)
                tfs = (tpf0, tpf1)
                for dc in (range(ND) if AB not in ("1", "2") else []):
                    if dc >= 2:
                        em.wait("t", ss)
                    em.op("t", lambda t, d=dc, p=tfs[dc % 2]: t.transpose(
                        p[:, 0:128], XN32[:, d * 128:(d + 1) * 128],
                        idf[:, :]), st)
                    em.wait("s", st)
                    em.op("s", lambda s, d=dc, p=tfs[dc % 2]: s.copy(
                        XNF3[:, d * 128:(d + 1) * 128], p[:, 0:128]), ss)
                env["fence_s"](em)
                em.wait("t", ss)
                pgs = (pgp0, pgp1)
                for h in (range(2) if AB not in ("1", "2") else []):
                    for dc in range(ND):
                        em.op("t", lambda t, hh=h, d=dc, p=pgs[h]: t.matmul(
                            p[:, :],
                            XNF3[:, d * 128:(d + 1) * 128],
                            WB1[:, d * D + hh * 512:d * D + (hh + 1) * 512],
                            start=(d == 0), stop=(d == ND - 1)),
                            st if dc == ND - 1 else None)
                    em.wait("s", st)
                    em.op("s", lambda s, hh=h, p=pgs[h]: s.copy(
                        Pg[:, hh * 512:(hh + 1) * 512], p[:, :]), ss)
                if AB in ("1", "2"):
                    em.op("v", lambda v: v.tensor_copy(
                        s_row[:, D:2 * D], XN32[:, :]), sv)
                em.wait("s", ss)
                if AB not in ("1", "2"):
                    em.op("s", lambda s: s.activation(
                        s_row[:, 0:D], Pg[:, :], AF.Square), ss)
                em.wait("v", ss)
                if AB not in ("1", "2"):
                    em.op("v", lambda v: v.tensor_scalar(
                        s_row[:, 0:D], s_row[:, 0:D], GC1 * GC2, GC1,
                        op0=ALU.mult, op1=ALU.add), sv)
                    em.op("v", lambda v: v.tensor_tensor(
                        s_row[:, D:2 * D], s_row[:, 0:D], Pg[:, :], op=ALU.mult), sv)
                em.wait("s", sv)
                if AB not in ("1", "2"):
                    em.op("s", lambda s: s.activation(
                        s_row[:, 0:D], s_row[:, D:2 * D], AF.Tanh), ss)
                em.wait("v", ss)
                if AB not in ("1", "2"):
                    em.op("v", lambda v: v.scalar_tensor_tensor(
                        s_row[:, D:2 * D], s_row[:, 0:D], 1.0, Pg[:, :],
                        op0=ALU.add, op1=ALU.mult), sv)
                    em.op("v", lambda v: v.tensor_scalar(
                        s_row[:, D:2 * D], s_row[:, D:2 * D], 0.5, None,
                        op0=ALU.mult), sv)
                    em.op("v", lambda v: v.tensor_tensor(
                        Pg[:, :], s_row[:, D:2 * D], XN32[:, :], op=ALU.subtract), sv)
                    em.op("v", lambda v, cc=c: v.tensor_scalar(
                        Pg[:, :], Pg[:, :], MASK[:, cc:cc + 1], None,
                        op0=ALU.mult), sv)
                    em.op("v", lambda v: v.tensor_tensor(
                        s_row[:, D:2 * D], XN32[:, :], Pg[:, :], op=ALU.add), sv)
                em.wait("s", sv)
                em.op("s", lambda s: s.activation(
                    s_row[:, 0:D], s_row[:, D:2 * D], AF.Tanh,
                    scale=f32(0.5)), ss)
                em.wait("v", ss)
                em.op("v", lambda v: v.tensor_scalar(
                    s_row[:, 0:D], s_row[:, 0:D], 0.5, 0.5,
                    op0=ALU.mult, op1=ALU.add), sv)
                em.op("v", lambda v: v.tensor_tensor(
                    s_row[:, 0:D], s_row[:, 0:D], s_row[:, D:2 * D],
                    op=ALU.mult), sv)
                em.op("v", lambda v: v.tensor_tensor(
                    SC[:, :], SC[:, :], s_row[:, 0:D], op=ALU.mult), sv)
                em.op("v", lambda v, cc=c: v.tensor_tensor(
                    X[:, cc * D:(cc + 1) * D], X[:, cc * D:(cc + 1) * D],
                    SC[:, :], op=ALU.add), sv)
                with nc.Block() as blk:
                    em.flush(blk)


def build_l1_part1(env):
    nc = env["nc"]
    A, B, X, WB1, STAT, GATES, BB, bqp, ROW, ones1, idf, idb = (
        env["A"], env["B"], env["X"], env["WB1"], env["STAT"], env["GATES"],
        env["BB"], env["bqp"], env["ROW"], env["ones1"], env["idf"], env["idb"])
    W = env["W"]
    sd, sd2, sdo, sv, ss, st, sg = (env["sd"], env["sd2"], env["sdo"],
                                    env["sv"], env["ss"], env["st"], env["sg"])

    with ExitStack() as _es5:
        ent = _es5.enter_context
        Kp = ent(nc.sbuf_tensor("Kp", [128, 2304], BF16))
        Vt = ent(nc.sbuf_tensor("Vt", [128, 2304], BF16))
        WKB = ent(nc.sbuf_tensor("WKB", [128, D], BF16))
        WVB = ent(nc.sbuf_tensor("WVB", [128, D], BF16))
        MRH = ent(nc.sbuf_tensor("MRH", [128, 64], BF16))
        MRL = ent(nc.sbuf_tensor("MRL", [128, 64], BF16))
        MRS = ent(nc.sbuf_tensor("MRS", [128, 64], F32))
        bqp1 = ent(nc.sbuf_tensor("bqp1", [128, 8], F32))
        bkvp = ent(nc.sbuf_tensor("bkvp", [128, 2], F32))
        em = mk_em(env)
        load_weight_dxe(env, em,
                        lambda dc: W["wq1"][dc * 128:(dc + 1) * 128, :], D)
        em.op("y", lambda y: y.dma_start(out=bqp1[:, :], in_=W["bq1"]), sd, 16)
        em.op("y", lambda y: y.dma_start(out=bkvp[:, 0:1], in_=W["bk1"]),
              sd, 16)
        em.op("y", lambda y: y.dma_start(out=bkvp[:, 1:2], in_=W["bv1"]),
              sd, 16)
        em.wait("g", sv)
        em.wait("g", ss)
        em.op("g", lambda g: g.dma_start(
            out=env["STG"][0][:, :], in_=W["wk1"]), sd2, 16)
        em.wait("v", sd2)
        em.op("v", lambda v: v.tensor_copy(WKB[:, :], env["STG"][0][:, :]), sv)
        em.op("g", lambda g: g.dma_start(
            out=env["STG"][1][:, :], in_=W["wv1"]), sd2, 16)
        em.wait("v", sd2)
        em.op("v", lambda v: v.tensor_copy(WVB[:, :], env["STG"][1][:, :]), sv)
        em.op("g", lambda g: g.dma_start(out=MRS[:, :], in_=W["mr1"]), sd2, 16)
        em.wait("v", sd2)
        em.op("v", lambda v: v.tensor_copy(MRH[:, :], MRS[:, :]), sv)
        em.wait("v", sv)
        em.op("v", lambda v: v.tensor_copy(env["STG"][0][:, 0:64], MRH[:, :]), sv)
        em.wait("v", sv)
        em.op("v", lambda v: v.tensor_tensor(
            MRS[:, :], MRS[:, :], env["STG"][0][:, 0:64], op=ALU.subtract), sv)
        em.wait("v", sv)
        em.op("v", lambda v: v.tensor_copy(MRL[:, :], MRS[:, :]), sv)
        em.op("g", lambda g: g.memset(Kp[:, 0:128], 0.0))
        em.op("g", lambda g: g.memset(Kp[:, 2176:2304], 0.0))
        em.op("g", lambda g: g.memset(Vt[:, 0:128], 0.0))
        em.op("g", lambda g: g.memset(Vt[:, 2176:2304], 0.0))
        em.wait("y", sd)
        em.wait("g", sd2)
        with nc.Block() as blk:
            em.flush(blk)

        with ExitStack() as _es6:
            ent = _es6.enter_context
            XN32 = ent(nc.sbuf_tensor("XN32L1", [128, D], F32))
            XNB = ent(nc.sbuf_tensor("XNB", [128, D], BF16))
            HI32 = ent(nc.sbuf_tensor("HI32", [128, D], F32))
            LOB = ent(nc.sbuf_tensor("LOB", [128, D], BF16))
            LOF = ent(nc.sbuf_tensor("LOF", [128, D], BF16))
            VFC = ent(nc.sbuf_tensor("VFC", [128, 128], BF16))
            GS = ent(nc.sbuf_tensor("GS", [128, 48], F32))
            txp0 = ent(nc.psum_tensor("txp0", [128, 128], BF16))
            txp1 = ent(nc.psum_tensor("txp1", [128, 128], BF16))
            lgp = ent(nc.psum_tensor("lgp", [128, 8], F32))
            qfp0 = ent(nc.psum_tensor("qfp0", [128, 128], F32))
            qfp1 = ent(nc.psum_tensor("qfp1", [128, 128], F32))
            kvp0 = ent(nc.psum_tensor("kvp0", [128, 128], F32))
            kvp1 = ent(nc.psum_tensor("kvp1", [128, 128], F32))
            vtp = ent(nc.psum_tensor("vtp", [128, 128], BF16))
            for c in range(NT):
                em = mk_em(env)
                emit_ln_stats(env, em, c, XN32, HI32)
                em.op("v", lambda v: v.tensor_copy(XNB[:, :], XN32[:, :]), sv)
                em.wait("v", sv)
                em.op("v", lambda v: v.tensor_copy(HI32[:, :], XNB[:, :]), sv)
                em.wait("v", sv)
                em.op("v", lambda v: v.tensor_tensor(
                    XN32[:, :], XN32[:, :], HI32[:, :], op=ALU.subtract), sv)
                em.wait("v", sv)
                em.op("v", lambda v: v.tensor_copy(LOB[:, :], XN32[:, :]), sv)
                em.op("y", lambda y, cc=c: y.dma_start(
                    out=W["x1d"].rearrange("(n p) m -> p n m", p=128)[:, cc, :],
                    in_=X[:, cc * D:(cc + 1) * D]), sdo, 16)
                em.wait("t", sv)
                txs = (txp0, txp1)
                for dc in range(ND):
                    if dc >= 2:
                        em.wait("t", ss)
                    em.op("t", lambda t, d=dc, p=txs[dc % 2]: t.transpose(
                        p[:, :], XNB[:, d * 128:(d + 1) * 128], idb[:, :]), st)
                    em.wait("s", st)
                    em.op("s", lambda s, d=dc, cc=c, p=txs[dc % 2]: s.copy(
                        A[:, d * T + cc * 128:d * T + (cc + 1) * 128], p[:, :]),
                        ss)
                for dc in range(ND):
                    em.wait("t", ss)
                    em.op("t", lambda t, d=dc, p=txs[dc % 2]: t.transpose(
                        p[:, :], LOB[:, d * 128:(d + 1) * 128], idb[:, :]), st)
                    em.wait("s", st)
                    em.op("s", lambda s, d=dc, p=txs[dc % 2]: s.copy(
                        LOF[:, d * 128:(d + 1) * 128], p[:, :]), ss)
                env["fence_s"](em)
                em.wait("t", ss)
                for pi in range(3):
                    for dc in range(ND):
                        if pi == 0:
                            lh, rh = (
                                A[:, dc * T + c * 128:dc * T + (c + 1) * 128],
                                MRH[:, dc * 8:(dc + 1) * 8])
                        elif pi == 1:
                            lh, rh = (LOF[:, dc * 128:(dc + 1) * 128],
                                      MRH[:, dc * 8:(dc + 1) * 8])
                        else:
                            lh, rh = (
                                A[:, dc * T + c * 128:dc * T + (c + 1) * 128],
                                MRL[:, dc * 8:(dc + 1) * 8])
                        em.op("t", lambda t, l=lh, r=rh, pp=pi, d=dc: t.matmul(
                            lgp[:, :], l, r,
                            start=(pp == 0 and d == 0),
                            stop=(pp == 2 and d == ND - 1)),
                            st if (pi == 2 and dc == ND - 1) else None)
                em.wait("v", st)
                em.op("v", lambda v: v.tensor_copy(GS[:, 0:8], lgp[:, :]), sv)
                em.wait("v", sv)
                em.op("v", lambda v: v.reduce_max(GS[:, 8:9], GS[:, 0:8],
                                                  axis=AX.X), sv)
                em.wait("v", sv)
                em.op("v", lambda v: v.tensor_scalar(
                    GS[:, 16:24], GS[:, 0:8], GS[:, 8:9], None, op0=ALU.is_ge),
                    sv)
                em.wait("v", sv)
                em.op("v", lambda v: v.scalar_tensor_tensor(
                    GS[:, 24:32], GS[:, 16:24], NEG, GS[:, 0:8],
                    op0=ALU.mult, op1=ALU.add), sv)
                em.wait("v", sv)
                em.op("v", lambda v: v.reduce_max(GS[:, 9:10], GS[:, 24:32],
                                                  axis=AX.X), sv)
                em.wait("v", sv)
                em.op("v", lambda v: v.tensor_scalar(
                    GS[:, 10:11], GS[:, 8:9], -1.0, None, op0=ALU.mult), sv)
                em.wait("s", sv)
                em.op("s", lambda s: s.activation(
                    GS[:, 32:40], GS[:, 0:8], AF.Exp, bias=GS[:, 10:11]), ss)
                em.op("s", lambda s: s.activation(
                    GS[:, 11:12], GS[:, 9:10], AF.Exp, bias=GS[:, 10:11]), ss)
                em.wait("v", ss)
                em.op("v", lambda v: v.tensor_scalar(
                    GS[:, 12:13], GS[:, 11:12], 1.0, None, op0=ALU.add), sv)
                em.wait("v", sv)
                em.op("v", lambda v: v.reciprocal(GS[:, 13:14], GS[:, 12:13]), sv)
                em.wait("v", sv)
                em.op("v", lambda v: v.tensor_scalar(
                    GS[:, 40:48], GS[:, 0:8], GS[:, 9:10], None, op0=ALU.is_ge),
                    sv)
                em.wait("v", sv)
                em.op("v", lambda v: v.tensor_tensor(
                    GS[:, 40:48], GS[:, 40:48], GS[:, 32:40], op=ALU.mult), sv)
                em.wait("v", sv)
                em.op("v", lambda v, cc=c: v.tensor_scalar(
                    GATES[:, cc * 8:(cc + 1) * 8], GS[:, 40:48], GS[:, 13:14],
                    None, op0=ALU.mult), sv)
                qfs = (qfp0, qfp1)
                for i in range(ND):
                    ps = qfs[i % 2]
                    if i >= 2:
                        em.wait("t", sv)
                    for dc in range(ND):
                        em.op("t", lambda t, ii=i, d=dc, cc=c, p=ps: t.matmul(
                            p[:, :],
                            WB1[:, d * D + ii * 128:d * D + (ii + 1) * 128],
                            A[:, d * T + cc * 128:d * T + (cc + 1) * 128],
                            start=(d == 0), stop=(d == ND - 1)),
                            st if dc == ND - 1 else None)
                    em.wait("v", st)
                    em.op("v", lambda v, ii=i, cc=c, p=ps: v.tensor_scalar(
                        B[:, ii * T + cc * 128:ii * T + (cc + 1) * 128],
                        p[:, :], bqp1[:, ii:ii + 1], None, op0=ALU.add), sv)
                kvs = (kvp0, kvp1)
                for dc in range(ND):
                    em.op("t", lambda t, d=dc, cc=c: t.matmul(
                        kvs[0][:, :], WKB[:, d * 128:(d + 1) * 128],
                        A[:, d * T + cc * 128:d * T + (cc + 1) * 128],
                        start=(d == 0), stop=(d == ND - 1)),
                        st if dc == ND - 1 else None)
                em.wait("v", st)
                em.op("v", lambda v, cc=c: v.tensor_scalar(
                    Kp[:, 128 + cc * 128:128 + (cc + 1) * 128], kvs[0][:, :],
                    bkvp[:, 0:1], None, op0=ALU.add), sv)
                for dc in range(ND):
                    em.op("t", lambda t, d=dc, cc=c: t.matmul(
                        kvs[1][:, :], WVB[:, d * 128:(d + 1) * 128],
                        A[:, d * T + cc * 128:d * T + (cc + 1) * 128],
                        start=(d == 0), stop=(d == ND - 1)),
                        st if dc == ND - 1 else None)
                em.wait("v", st)
                em.op("v", lambda v: v.tensor_scalar(
                    VFC[:, :], kvs[1][:, :], bkvp[:, 1:2], None, op0=ALU.add),
                    sv)
                em.op("v", lambda v: v.tensor_copy(STAT[:, 61:62],
                                                   STAT[:, 60:61]), sv)
                em.wait("t", sv)
                em.op("t", lambda t: t.transpose(vtp[:, :], VFC[:, :],
                                                 idb[:, :]), st)
                em.wait("v", st)
                em.op("v", lambda v, cc=c: v.tensor_copy(
                    Vt[:, 128 + cc * 128:128 + (cc + 1) * 128], vtp[:, :]), sv)
                with nc.Block() as blk:
                    em.flush(blk)

        if os.environ.get("MK_DBG2"):
            qdbg = nc.dram_tensor("qdbg", [128, NT * D], F32,
                                  kind="ExternalOutput").ap()
            kdbg = nc.dram_tensor("kdbg", [128, 2304], F32,
                                  kind="ExternalOutput").ap()
            vdbg = nc.dram_tensor("vdbg", [128, 2304], F32,
                                  kind="ExternalOutput").ap()
            adbg = nc.dram_tensor("adbg", [128, NT * D], F32,
                                  kind="ExternalOutput").ap()
            wdbg = nc.dram_tensor("wdbg", [128, 2 * D], F32,
                                  kind="ExternalOutput").ap()
            em = mk_em(env)
            em.op("v", lambda v: v.tensor_copy(env["STG"][0][:, :],
                                               WKB[:, :]), sv)
            em.op("v", lambda v: v.tensor_copy(env["STG"][1][:, :],
                                               WB1[:, 0:D]), sv)
            em.wait("y", sv)
            em.op("y", lambda y: y.dma_start(out=wdbg[:, 0:D],
                                             in_=env["STG"][0][:, :]),
                  env["sdo"], 16)
            em.op("y", lambda y: y.dma_start(out=wdbg[:, D:2 * D],
                                             in_=env["STG"][1][:, :]),
                  env["sdo"], 16)
            em.wait("y", env["sdo"])
            with nc.Block() as blk:
                em.flush(blk)
            em = mk_em(env)
            for seg in range(NT):
                em.op("v", lambda v, s2=seg: v.tensor_copy(
                    env["STG"][0][:, :], A[:, s2 * D:(s2 + 1) * D]), sv)
                em.wait("y", sv)
                em.op("y", lambda y, s2=seg: y.dma_start(
                    out=adbg[:, s2 * D:(s2 + 1) * D],
                    in_=env["STG"][0][:, :]), env["sdo"], 16)
                em.wait("v", env["sdo"])
            with nc.Block() as blk:
                em.flush(blk)
            em = mk_em(env)
            for seg in range(NT):
                em.op("v", lambda v, s2=seg: v.tensor_copy(
                    env["STG"][0][:, :], B[:, s2 * D:(s2 + 1) * D]), sv)
                em.wait("y", sv)
                em.op("y", lambda y, s2=seg: y.dma_start(
                    out=qdbg[:, s2 * D:(s2 + 1) * D],
                    in_=env["STG"][0][:, :]), env["sdo"], 16)
                em.wait("v", env["sdo"])
            for seg in range(3):
                w = 768
                em.op("v", lambda v, s2=seg, w2=w: v.tensor_copy(
                    env["STG"][0][:, 0:w2], Kp[:, s2 * w2:(s2 + 1) * w2]), sv)
                em.wait("y", sv)
                em.op("y", lambda y, s2=seg, w2=w: y.dma_start(
                    out=kdbg[:, s2 * w2:(s2 + 1) * w2],
                    in_=env["STG"][0][:, 0:w2]), env["sdo"], 16)
                em.wait("v", env["sdo"])
                em.op("v", lambda v, s2=seg, w2=w: v.tensor_copy(
                    env["STG"][0][:, 0:w2], Vt[:, s2 * w2:(s2 + 1) * w2]), sv)
                em.wait("y", sv)
                em.op("y", lambda y, s2=seg, w2=w: y.dma_start(
                    out=vdbg[:, s2 * w2:(s2 + 1) * w2],
                    in_=env["STG"][0][:, 0:w2]), env["sdo"], 16)
                em.wait("v", env["sdo"])
            with nc.Block() as blk:
                em.flush(blk)

        em = mk_em(env)
        load_weight_dxe(env, em,
                        lambda dc: W["wo1"][dc * 128:(dc + 1) * 128, :], D)
        em.op("y", lambda y: y.dma_start(out=ROW[:, :], in_=W["bo1"].rearrange("(a d) -> a d", a=1)),
              sd, 16)
        em.wait("y", sd)
        with nc.Block() as blk:
            em.flush(blk)
        with nc.psum_tensor("bc_psL1", [128, 512], F32) as bc_ps:
            em = mk_em(env)
            for sl in range(2):
                em.op("t", lambda t, s=sl: t.matmul(
                    bc_ps[:, :], ones1[:, :], ROW[:, s * 512:(s + 1) * 512],
                    start=True, stop=True), st)
                em.wait("v", st)
                em.op("v", lambda v, s=sl: v.tensor_copy(
                    BB[:, s * 512:(s + 1) * 512], bc_ps[:, :]), sv)
                em.wait("t", sv)
            with nc.Block() as blk:
                em.flush(blk)

        with ExitStack() as _es7:
            ent = _es7.enter_context
            BAND = ent(nc.sbuf_tensor("BAND", [128, 1024], F32))
            OC = ent(nc.sbuf_tensor("OC", [128, D], BF16))
            S512 = ent(nc.sbuf_tensor("S512", [128, 512], F32))
            E512 = ent(nc.sbuf_tensor("E512", [128, 512], F32))
            Pb5 = ent(nc.sbuf_tensor("Pb5", [128, 512], BF16))
            PT5 = ent(nc.sbuf_tensor("PT5", [128, 512], BF16))
            S1C = ent(nc.sbuf_tensor("S1C", [128, D], F32))
            scp0 = ent(nc.psum_tensor("scp0L1", [128, 512], F32))
            scp1 = ent(nc.psum_tensor("scp1L1", [128, 512], F32))
            ptp0 = ent(nc.psum_tensor("ptp0L1", [128, 128], BF16))
            ptp1 = ent(nc.psum_tensor("ptp1L1", [128, 128], BF16))
            pvp0 = ent(nc.psum_tensor("pvp0L1", [64, 128], F32))
            pvp1 = ent(nc.psum_tensor("pvp1L1", [64, 128], F32))
            s1p0 = ent(nc.psum_tensor("s1p0", [128, 512], F32))
            s1p1 = ent(nc.psum_tensor("s1p1", [128, 512], F32))
            em = mk_em(env)
            for r in range(2):
                em.op("g", lambda g, rr=r: g.memset(
                    BAND[:, rr * 512:(rr + 1) * 512], 0.0))
                em.op("g", lambda g, rr=r: g.affine_select(
                    out=BAND[:, rr * 512:(rr + 1) * 512],
                    in_=BAND[:, rr * 512:(rr + 1) * 512],
                    compare_op=ALU.is_ge, fill=NEG,
                    base=-128 * rr, pattern=[[1, 512]], channel_multiplier=-1))
                em.op("g", lambda g, rr=r: g.affine_select(
                    out=BAND[:, rr * 512:(rr + 1) * 512],
                    in_=BAND[:, rr * 512:(rr + 1) * 512],
                    compare_op=ALU.is_ge, fill=NEG,
                    base=128 * rr + 255, pattern=[[-1, 512]],
                    channel_multiplier=1))
            with nc.Block() as blk:
                em.flush(blk)

            for c in range(NT):
                n = c // 2
                r = c % 2
                em = mk_em(env)
                scs = (scp0, scp1)
                pvs = (pvp0, pvp1)
                for h in range(16):
                    g = h // 8
                    i = h % 8
                    sp = scs[h % 2]
                    if h >= 2:
                        em.wait("t", sv)
                    em.op("t", lambda t, gg=g, ii=i, p=sp: t.matmul(
                        p[:, :],
                        B[64 * gg:64 * (gg + 1),
                          ii * T + c * 128:ii * T + (c + 1) * 128],
                        Kp[64 * gg:64 * (gg + 1), 256 * n:256 * n + 512],
                        start=True, stop=True, tile_position=(64 * gg, 0)), st)
                    em.wait("v", st)
                    em.wait("v", ss)
                    em.op("v", lambda v, p=sp, rr=r: v.tensor_tensor(
                        S512[:, :], p[:, :], BAND[:, rr * 512:(rr + 1) * 512],
                        op=ALU.add), sv)
                    em.wait("v", sv)
                    em.op("v", lambda v: v.reduce_max(
                        STAT[:, 48:49], S512[:, :], axis=AX.X, negate=True), sv)
                    em.wait("s", sv)
                    em.op("s", lambda s: s.activation(
                        E512[:, :], S512[:, :], AF.Exp, bias=STAT[:, 48:49],
                        accum_out=STAT[:, 50:51]), ss)
                    em.wait("v", ss)
                    em.op("v", lambda v: v.reciprocal(
                        STAT[:, 51:52], STAT[:, 50:51]), sv)
                    em.wait("v", sv)
                    em.wait("v", st)
                    em.op("v", lambda v: v.tensor_scalar(
                        Pb5[:, :], E512[:, :], STAT[:, 51:52], None,
                        op0=ALU.mult), sv)
                    env["fence_v"](em)
                    em.wait("t", sv)
                    pts = (ptp0, ptp1)
                    for j in range(4):
                        if j >= 2:
                            em.wait("t", ss)
                        em.op("t", lambda t, jj=j, p=pts[j % 2]: t.transpose(
                            p[:, :], Pb5[:, jj * 128:(jj + 1) * 128],
                            idb[:, :]), st)
                        em.wait("s", st)
                        em.op("s", lambda s, jj=j, p=pts[j % 2]: s.copy(
                            PT5[:, jj * 128:(jj + 1) * 128], p[:, :]), ss)
                    env["fence_s"](em)
                    em.wait("t", ss)
                    for j in range(4):
                        em.op("t", lambda t, jj=j, gg=g, p=pvs[h % 2]:
                              t.matmul(
                            p[:, :],
                            Vt[:, (2 * n + jj) * 128 + 64 * gg:
                               (2 * n + jj) * 128 + 64 * (gg + 1)],
                            PT5[:, jj * 128:(jj + 1) * 128],
                            start=(jj == 0), stop=(jj == 3)),
                            st if j == 3 else None)
                    em.wait("v", st)
                    em.op("v", lambda v, gg=g, ii=i, p=pvs[h % 2]: v.tensor_copy(
                        OC[64 * gg:64 * (gg + 1), ii * 128:(ii + 1) * 128],
                        p[:, :]), sv)
                env["fence_v"](em)
                em.wait("t", sv)
                s1s = (s1p0, s1p1)
                for hh in range(2):
                    for i in range(ND):
                        em.op("t", lambda t, h2=hh, ii=i: t.matmul(
                            s1s[h2][:, :],
                            OC[:, ii * 128:(ii + 1) * 128],
                            WB1[:, ii * D + h2 * 512:ii * D + (h2 + 1) * 512],
                            start=(ii == 0), stop=(ii == ND - 1)),
                            st if i == ND - 1 else None)
                    em.wait("v", st)
                    em.op("v", lambda v, h2=hh: v.tensor_tensor(
                        S1C[:, h2 * 512:(h2 + 1) * 512], s1s[h2][:, :],
                        BB[:, h2 * 512:(h2 + 1) * 512], op=ALU.add), sv)
                env["fence_v"](em)
                em.wait("y", sv)
                em.op("y", lambda y, cc=c: y.dma_start(
                    out=W["s1d"].rearrange("(n p) m -> p n m", p=128)[:, cc, :],
                    in_=S1C[:, :]), sdo, 16)
                em.wait("y", env["sdo"])
                with nc.Block() as blk:
                    em.flush(blk)


def build_l1_moe(env):
    nc = env["nc"]
    A, WB1, STAT, GATES = env["A"], env["WB1"], env["STAT"], env["GATES"]
    W = env["W"]
    sd, sd2, sd3, sd4, sdo, sv, ss, st, sg = (
        env["sd"], env["sd2"], env["sd3"], env["sd4"], env["sdo"],
        env["sv"], env["ss"], env["st"], env["sg"])
    out = env["out"]
    NE = 8
    NM = 16
    TT = 2

    gdbg = None
    hdbg = None
    if os.environ.get("MK_DBG3"):
        gdbg = nc.dram_tensor("gdbg", [T, D], F32, kind="ExternalOutput").ap()
    if os.environ.get("MK_DBG4"):
        hdbg = nc.dram_tensor("hdbg", [128, NM * 1024], F32,
                              kind="ExternalOutput").ap()
    with ExitStack() as _es8:
        ent = _es8.enter_context
        H = ent(nc.sbuf_tensor("H", [128, NM * 1024], BF16))
        OUTA = ent(nc.sbuf_tensor("OUTA", [128, 8 * D], F32))
        W13G0 = ent(nc.sbuf_tensor("W13G0", [128, D], BF16))
        W13G1 = ent(nc.sbuf_tensor("W13G1", [128, D], BF16))
        W13U0 = ent(nc.sbuf_tensor("W13U0", [128, D], BF16))
        W13U1 = ent(nc.sbuf_tensor("W13U1", [128, D], BF16))
        W2B = ent(nc.sbuf_tensor("W2B", [128, 4 * D], BF16))
        GSIL = ent(nc.sbuf_tensor("GSIL", [128, D], F32))
        XC = ent(nc.sbuf_tensor("XC", [128, D], F32))
        S1R = ent(nc.sbuf_tensor("S1R", [128, D], F32))
        STG2 = ent(nc.sbuf_tensor("STG2", [128, D], F32))
        STG3 = ent(nc.sbuf_tensor("STG3", [128, D], F32))
        gp0 = ent(nc.psum_tensor("gp0", [128, 512], F32))
        gp1 = ent(nc.psum_tensor("gp1", [128, 512], F32))
        up0 = ent(nc.psum_tensor("up0", [128, 512], F32))
        up1 = ent(nc.psum_tensor("up1", [128, 512], F32))
        yp0 = ent(nc.psum_tensor("yp0", [128, 512], F32))
        yp1 = ent(nc.psum_tensor("yp1", [128, 512], F32))
        w13g = (W13G0, W13G1)
        w13u = (W13U0, W13U1)
        stg = env["STG"]
        for tt in range(TT):
            em = mk_em(env)
            em.op("g", lambda g: g.memset(OUTA[:, :], 0.0))
            with nc.Block() as blk:
                em.flush(blk)

            for e in range(int(os.environ.get("MK_NEXP", NE))):
                em = mk_em(env)
                for m in range(NM):
                    pgi = m % 2
                    sgg = (stg[0], STG2)[m % 2]
                    sgu = (stg[1], STG3)[m % 2]
                    if m >= 2:
                        em.wait("y", sv)
                        em.wait("g", sv)
                    em.op("y", lambda y, ee=e, mm=m, sg_=sgg: y.dma_start(
                        out=sg_[:, :], in_=W["w13"][ee, mm]), sd, 16)
                    em.op("g", lambda g, ee=e, mm=m, sg_=sgu: g.dma_start(
                        out=sg_[:, :], in_=W["w13"][ee, 16 + mm]), sd2, 16)
                    em.wait("v", sd)
                    if m >= 2:
                        em.wait("v", st)
                    em.op("v", lambda v, p=pgi, b=sgg: v.tensor_copy(
                        w13g[p][:, :], b[:, :]), sv)
                    em.wait("v", sd2)
                    em.op("v", lambda v, p=pgi, b=sgu: v.tensor_copy(
                        w13u[p][:, :], b[:, :]), sv)
                    env["fence_v"](em)
                    em.wait("t", sv)
                    for th in range(2):
                        gps = (gp0, gp1)[th]
                        ups = (up0, up1)[th]
                        for dc in range(ND):
                            em.op("t", lambda t, d=dc, p=pgi, h2=th, q=gps:
                                  t.matmul(
                                q[:, :], w13g[p][:, d * 128:(d + 1) * 128],
                                A[:, d * T + tt * 1024 + h2 * 512:
                                  d * T + tt * 1024 + (h2 + 1) * 512],
                                start=(d == 0), stop=(d == ND - 1)),
                                st if dc == ND - 1 else None)
                        for dc in range(ND):
                            em.op("t", lambda t, d=dc, p=pgi, h2=th, q=ups:
                                  t.matmul(
                                q[:, :], w13u[p][:, d * 128:(d + 1) * 128],
                                A[:, d * T + tt * 1024 + h2 * 512:
                                  d * T + tt * 1024 + (h2 + 1) * 512],
                                start=(d == 0), stop=(d == ND - 1)),
                                st if dc == ND - 1 else None)
                    em.wait("s", st)
                    for th in range(2):
                        em.op("s", lambda s, h2=th, q=(gp0, gp1)[th]:
                              s.activation(
                            GSIL[:, h2 * 512:(h2 + 1) * 512], q[:, :],
                            AF.Silu), ss)
                    em.wait("v", ss)
                    for th in range(2):
                        em.op("v", lambda v, mm=m, h2=th, q=(up0, up1)[th]:
                              v.tensor_tensor(
                            H[:, mm * 1024 + h2 * 512:mm * 1024 + (h2 + 1) * 512],
                            GSIL[:, h2 * 512:(h2 + 1) * 512], q[:, :],
                            op=ALU.mult), sv)
                with nc.Block() as blk:
                    em.flush(blk)

                if hdbg is not None and e == 0 and tt == 0:
                    em = mk_em(env)
                    for seg in range(NM):
                        em.op("v", lambda v, s2=seg: v.tensor_copy(
                            GSIL[:, :], H[:, s2 * 1024:(s2 + 1) * 1024]), sv)
                        em.wait("y", sv)
                        em.op("y", lambda y, s2=seg: y.dma_start(
                            out=hdbg[:, s2 * 1024:(s2 + 1) * 1024],
                            in_=GSIL[:, :]), sdo, 16)
                        em.wait("v", sdo)
                    with nc.Block() as blk:
                        em.flush(blk)

                em = mk_em(env)
                for s4 in range(4):
                    if s4 >= 1:
                        em.wait("v", st)
                    for k in range(4):
                        m = s4 * 4 + k
                        sgb = stg[k % 2]
                        if s4 > 0 or k >= 2:
                            em.wait("y" if k % 2 == 0 else "g", sv)
                        if k % 2 == 0:
                            em.op("y", lambda y, ee=e, mm=m, b=sgb: y.dma_start(
                                out=b[:, :],
                                in_=W["w2"][ee, mm * 128:(mm + 1) * 128, :]),
                                sd, 16)
                            em.wait("v", sd)
                        else:
                            em.op("g", lambda g, ee=e, mm=m, b=sgb: g.dma_start(
                                out=b[:, :],
                                in_=W["w2"][ee, mm * 128:(mm + 1) * 128, :]),
                                sd2, 16)
                            em.wait("v", sd2)
                        em.op("v", lambda v, k2=k, b=sgb: v.tensor_copy(
                            W2B[:, k2 * D:(k2 + 1) * D], b[:, :]), sv)
                    env["fence_v"](em)
                    em.wait("t", sv)
                    for tc in range(8):
                        cidx = tt * 8 + tc
                        for eb in range(2):
                            yps = (yp0, yp1)[(tc * 2 + eb) % 2]
                            if tc * 2 + eb >= 2:
                                em.wait("t", sv)
                            for k in range(4):
                                m = s4 * 4 + k
                                em.op("t", lambda t, mm=m, k2=k, t2=tc, b=eb,
                                      p=yps: t.matmul(
                                    p[:, :],
                                    H[:, mm * 1024 + t2 * 128:
                                      mm * 1024 + (t2 + 1) * 128],
                                    W2B[:, k2 * D + b * 512:
                                        k2 * D + (b + 1) * 512],
                                    start=(k2 == 0), stop=(k2 == 3)),
                                    st if k == 3 else None)
                            em.wait("v", st)
                            em.op("v", lambda v, t2=tc, b=eb, ci=cidx, ee=e,
                                  p=yps: v.scalar_tensor_tensor(
                                OUTA[:, t2 * D + b * 512:t2 * D + (b + 1) * 512],
                                p[:, :], GATES[:, ci * 8 + ee:ci * 8 + ee + 1],
                                OUTA[:, t2 * D + b * 512:t2 * D + (b + 1) * 512],
                                op0=ALU.mult, op1=ALU.add), sv)
                with nc.Block() as blk:
                    em.flush(blk)

            for tc in range(8):
                cidx = tt * 8 + tc
                em = mk_em(env)
                em.wait("s", sdo)
                em.wait("y", sv)
                em.wait("g", sv)
                em.op("y", lambda y, ci=cidx: y.dma_start(
                    out=XC[:, :],
                    in_=W["x1d"].rearrange("(n p) m -> p n m", p=128)[:, ci, :]),
                    sd, 16)
                em.op("g", lambda g, ci=cidx: g.dma_start(
                    out=S1R[:, :],
                    in_=W["s1d"].rearrange("(n p) m -> p n m", p=128)[:, ci, :]),
                    sd2, 16)
                em.op("s", lambda s, t2=tc: s.activation(
                    GSIL[:, :], OUTA[:, t2 * D:(t2 + 1) * D], AF.Silu), ss)
                env["fence_s"](em)
                if gdbg is not None:
                    em.wait("g", ss)
                    gsrc = (lambda t2: OUTA[:, t2 * D:(t2 + 1) * D]) \
                        if os.environ.get("MK_DBG5") else (lambda t2: GSIL[:, :])
                    em.op("g", lambda g, ci=cidx, t2=tc, gs=gsrc: g.dma_start(
                        out=gdbg.rearrange("(n p) m -> p n m", p=128)[:, ci, :],
                        in_=gs(t2)), sdo, 16)
                    em.wait("g", sdo)
                    em.wait("v", sdo)
                em.wait("v", ss)
                em.wait("v", sd)
                em.wait("v", sd2)
                em.op("v", lambda v: v.tensor_tensor(
                    GSIL[:, :], GSIL[:, :], S1R[:, :], op=ALU.mult), sv)
                em.wait("v", sv)
                em.op("v", lambda v: v.tensor_tensor(
                    GSIL[:, :], GSIL[:, :], XC[:, :], op=ALU.add), sv)
                env["fence_v"](em)
                em.wait("y", sv)
                em.op("y", lambda y, ci=cidx: y.dma_start(
                    out=out.rearrange("(n p) m -> p n m", p=128)[:, ci, :],
                    in_=GSIL[:, :]), sdo, 16)
                with nc.Block() as blk:
                    em.flush(blk)


def kernel(**inputs):
    n = 8
    x = np.ascontiguousarray(np.asarray(inputs["L0_attn_inputs"],
                                        dtype=np.float32))
    wmap = {
        "wq0": "L0_attn_Wq", "bq0": "L0_attn_bq", "wr0": "L0_ffn_w_router",
        "wb0": "L0_ffn_w_block", "wq1": "L1_attn_Wq", "bq1": "L1_attn_bq",
        "wk1": "L1_attn_Wk", "bk1": "L1_attn_bk", "wv1": "L1_attn_Wv",
        "bv1": "L1_attn_bv", "wo1": "L1_attn_Wo", "bo1": "L1_attn_bo",
        "mr1": "L1_ffn_moe_router", "w13": "L1_ffn_moe_w13",
        "w2": "L1_ffn_moe_w2",
    }
    ws = {k: np.ascontiguousarray(np.asarray(inputs[v], dtype=np.float32))
          for k, v in wmap.items()}
    idx = (np.arange(2)[None, :, None] * 512 + np.arange(8)[:, None, None] * 64
           + np.arange(64)[None, None, :]).reshape(-1)
    ws["bq0t"] = np.ascontiguousarray(ws["bq0"].reshape(8, 128).T)
    ws["wq1"] = np.ascontiguousarray(ws["wq1"][:, idx])
    ws["bq1"] = np.ascontiguousarray(
        ws["bq1"][idx].reshape(8, 2, 64).transpose(1, 2, 0).reshape(128, 8))
    ws["wo1"] = np.ascontiguousarray(ws["wo1"][idx, :])
    ws["wk1"] = np.ascontiguousarray(
        ws["wk1"].reshape(8, 128, 128).transpose(1, 0, 2).reshape(128, 1024))
    ws["wv1"] = np.ascontiguousarray(
        ws["wv1"].reshape(8, 128, 128).transpose(1, 0, 2).reshape(128, 1024))
    ws["mr1"] = np.ascontiguousarray(
        ws["mr1"].reshape(8, 128, 8).transpose(1, 0, 2).reshape(128, 64))
    ws["bk1"] = np.ascontiguousarray(ws["bk1"].reshape(128, 1))
    ws["bv1"] = np.ascontiguousarray(ws["bv1"].reshape(128, 1))
    ws["w13"] = np.ascontiguousarray(
        ws["w13"].reshape(8, 8, 128, 32, 128).transpose(0, 3, 2, 1, 4)
        .reshape(8, 32, 128, 1024))
    nc = build(do_l0=True, do_l1=True)
    in_maps = []
    for c in range(n):
        m = {"x": np.ascontiguousarray(x[c])}
        m.update(ws)
        in_maps.append(m)
    res = run_bass_kernel_spmd(nc, in_maps, core_ids=list(range(n)))
    outs = [np.asarray(res.results[c]["out"]) for c in range(n)]
    return np.stack(outs, axis=0)


# version 07
# speedup vs baseline: 1.3812x; 1.3812x over previous
import os
import numpy as np
from contextlib import ExitStack
import concourse.bass as bass
import concourse.mybir as mybir
from concourse.bass_utils import run_bass_kernel_spmd

F32 = mybir.dt.float32
BF16 = mybir.dt.bfloat16
AF = mybir.ActivationFunctionType
ALU = mybir.AluOpType
AX = mybir.AxisListType

T = 2048
D = 1024
NT = 16
ND = 8
KSEL = 1536
GC1 = 0.7978845608028654
GC2 = 0.044715
EPS = 1e-5
NEG = -1e30


def f32(v):
    return np.float32(v)


class Sem:
    def __init__(self, h, is_dma=False):
        self.h = h
        self.n = 0
        self.is_dma = is_dma


class Em:
    ENGS = ("t", "v", "s", "g", "y")

    def __init__(self, nc):
        self.nc = nc
        self.L = {e: [] for e in self.ENGS}
        self._stat = None
        self._engsem = {}

    def set_fences(self, stat, sems_by_eng):
        self._stat = stat
        self._engsem = sems_by_eng

    def _fence(self, peng):
        stat = self._stat
        sem = self._engsem[peng]
        if peng == "v":
            self.op("v", lambda v: v.tensor_copy(stat[:, 61:62],
                                                 stat[:, 60:61]), sem)
        elif peng == "s":
            self.op("s", lambda s: s.copy(stat[:, 62:63],
                                          stat[:, 60:61]), sem)
        elif peng == "g":
            self.op("g", lambda g: g.tensor_copy(stat[:, 63:64],
                                                 stat[:, 60:61]), sem)

    def op(self, eng, fn, sem=None, k=1):
        if sem is not None:
            sem.n += k
            kk = k
            hh = sem.h
            self.L[eng].append(lambda e: fn(e).then_inc(hh, kk))
        else:
            self.L[eng].append(fn)

    DMA_SETTLE_CYC = 800

    def wait(self, eng, sem, val=None):
        if val is None and self._stat is not None:
            for peng, psem in self._engsem.items():
                if sem is psem and eng != peng:
                    self._fence(peng)
                    break
        v = sem.n if val is None else val
        hh = sem.h
        self.L[eng].append(lambda e: e.wait_ge(hh, v))
        if sem.is_dma:
            cyc = self.DMA_SETTLE_CYC
            self.L[eng].append(lambda e: e.nop(cycle_cnt=cyc))

    def flush(self, blk):
        if self._stat is not None:
            for peng in ("v", "s", "g"):
                if self.L[peng]:
                    self._fence(peng)
        names = {"t": "tensor", "v": "vector", "s": "scalar",
                 "g": "gpsimd", "y": "sync"}
        for e in self.ENGS:
            lst = self.L[e]
            if not lst:
                continue

            def body(eng, lst=lst):
                for f in lst:
                    f(eng)
            getattr(blk, names[e])(body)
        self.L = {e: [] for e in self.ENGS}


def mk_em(env):
    em = Em(env["nc"])
    em.set_fences(env["STAT"], {"v": env["sv"], "s": env["ss"],
                                "g": env["sg"]})
    return em


def build(do_l0=True, do_l1=True):
    nc = bass.Bass("TRN2", target_bir_lowering=False)

    xin = nc.dram_tensor("x", [T, D], F32, kind="ExternalInput").ap()
    out = nc.dram_tensor("out", [T, D], F32, kind="ExternalOutput").ap()
    W = {}
    if do_l0:
        W["wq0"] = nc.dram_tensor("wq0", [D, D], F32, kind="ExternalInput").ap()
        W["bq0"] = nc.dram_tensor("bq0", [D], F32, kind="ExternalInput").ap()
        if not do_l1:
            W["bq0t"] = nc.dram_tensor("bq0t", [128, 8], F32,
                                       kind="ExternalInput").ap()
        W["wr0"] = nc.dram_tensor("wr0", [D], F32, kind="ExternalInput").ap()
        W["wb0"] = nc.dram_tensor("wb0", [D, D], F32, kind="ExternalInput").ap()
    if do_l1:
        W["wq1"] = nc.dram_tensor("wq1", [D, D], F32, kind="ExternalInput").ap()
        W["bq1"] = nc.dram_tensor("bq1", [128, 8], F32, kind="ExternalInput").ap()
        W["wk1"] = nc.dram_tensor("wk1", [128, D], F32, kind="ExternalInput").ap()
        W["bk1"] = nc.dram_tensor("bk1", [128, 1], F32, kind="ExternalInput").ap()
        W["wv1"] = nc.dram_tensor("wv1", [128, D], F32, kind="ExternalInput").ap()
        W["bv1"] = nc.dram_tensor("bv1", [128, 1], F32, kind="ExternalInput").ap()
        W["wo1"] = nc.dram_tensor("wo1", [D, D], F32, kind="ExternalInput").ap()
        W["bo1"] = nc.dram_tensor("bo1", [D], F32, kind="ExternalInput").ap()
        W["mr1"] = nc.dram_tensor("mr1", [128, 64], F32, kind="ExternalInput").ap()
        W["bq0t"] = nc.dram_tensor("bq0t", [128, 8], F32, kind="ExternalInput").ap()
        W["w13"] = nc.dram_tensor("w13", [8, 32, 128, 1024], F32,
                                  kind="ExternalInput").ap()
        W["w2"] = nc.dram_tensor("w2", [8, T, D], F32, kind="ExternalInput").ap()
        dbg = "ExternalOutput" if os.environ.get("MK_DBG") else "Internal"
        W["x1d"] = nc.dram_tensor("x1d", [T, D], F32, kind=dbg).ap()
        W["s1d"] = nc.dram_tensor("s1d", [T, D], F32, kind=dbg).ap()

    with ExitStack() as _es1:
        ent = _es1.enter_context
        A = ent(nc.sbuf_tensor("A", [128, NT * D], BF16))
        WB1 = ent(nc.sbuf_tensor("WB1", [128, ND * D], BF16))
        STG0 = ent(nc.sbuf_tensor("STG0", [128, D], F32))
        STG1 = ent(nc.sbuf_tensor("STG1", [128, D], F32))
        idb = ent(nc.sbuf_tensor("idb", [128, 128], BF16))
        idf = ent(nc.sbuf_tensor("idf", [128, 128], F32))
        STAT = ent(nc.sbuf_tensor("STAT", [128, 64], F32))
        GATES = ent(nc.sbuf_tensor("GATES", [128, 128], F32))
        MASK = ent(nc.sbuf_tensor("MASK", [128, 16], F32))
        BB = ent(nc.sbuf_tensor("BB", [128, D], F32))
        bqp = ent(nc.sbuf_tensor("bqp", [128, 8], F32))
        ROW = ent(nc.sbuf_tensor("ROW", [1, D], F32))
        ones1 = ent(nc.sbuf_tensor("ones1", [1, 128], F32))
        _sd = ent(nc.semaphore("sd"))
        _sd2 = ent(nc.semaphore("sd2"))
        _sdo = ent(nc.semaphore("sdo"))
        _sv = ent(nc.semaphore("sv"))
        _ss = ent(nc.semaphore("ss"))
        _st = ent(nc.semaphore("st"))
        _sg = ent(nc.semaphore("sg"))
        _sd3 = ent(nc.semaphore("sd3"))
        _sd4 = ent(nc.semaphore("sd4"))
        env = dict(nc=nc, A=A, WB1=WB1, STG=(STG0, STG1), idb=idb, idf=idf,
                   STAT=STAT, GATES=GATES, MASK=MASK, BB=BB, bqp=bqp,
                   ROW=ROW, ones1=ones1, W=W, xin=xin, out=out,
                   sd=Sem(_sd, True), sd2=Sem(_sd2, True), sdo=Sem(_sdo, True),
                   sv=Sem(_sv), ss=Sem(_ss), st=Sem(_st), sg=Sem(_sg),
                   sd3=Sem(_sd3, True), sd4=Sem(_sd4, True))

        def fence_v(em):
            em.op("v", lambda v: v.tensor_copy(STAT[:, 61:62],
                                               STAT[:, 60:61]), env["sv"])

        def fence_s(em):
            em.op("s", lambda s: s.copy(STAT[:, 62:63], STAT[:, 60:61]),
                  env["ss"])
        env["fence_v"] = fence_v
        env["fence_s"] = fence_s

        em = mk_em(env)
        em.op("g", lambda g: g.memset(idb[:, :], 0.0))
        em.op("g", lambda g: g.affine_select(
            out=idb[:, :], in_=idb[:, :], compare_op=ALU.not_equal, fill=1.0,
            base=0, pattern=[[-1, 128]], channel_multiplier=1))
        em.op("g", lambda g: g.memset(idf[:, :], 0.0))
        em.op("g", lambda g: g.affine_select(
            out=idf[:, :], in_=idf[:, :], compare_op=ALU.not_equal, fill=1.0,
            base=0, pattern=[[-1, 128]], channel_multiplier=1))
        em.op("g", lambda g: g.memset(ones1[:, :], 1.0))
        with nc.Block() as blk:
            em.flush(blk)

        with (
            nc.sbuf_tensor("X", [128, NT * D], F32) as X,
            nc.sbuf_tensor("B", [128, NT * D], BF16) as B,
        ):
            env["X"] = X
            env["B"] = B
            em = mk_em(env)
            em.op("y", lambda y: y.dma_start(
                out=X[:, :].rearrange("p (n m) -> p n m", n=NT),
                in_=xin.rearrange("(n p) m -> p n m", p=128)), env["sd"], 16)
            em.wait("y", env["sd"])
            with nc.Block() as blk:
                em.flush(blk)

            if do_l0:
                build_l0(env)
            if do_l1:
                build_l1_part1(env)
            else:
                em = mk_em(env)
                em.op("y", lambda y: y.dma_start(
                    out=out.rearrange("(n p) m -> p n m", p=128),
                    in_=X[:, :].rearrange("p (n m) -> p n m", n=NT)),
                    env["sdo"], 16)
                with nc.Block() as blk:
                    em.flush(blk)

        if do_l1:
            build_l1_moe(env)

        em = mk_em(env)
        em.wait("y", env["sdo"])
        with nc.Block() as blk:
            em.flush(blk)

    return nc


def load_weight_dxe(env, em, src_ap, ncols, oview=None):
    nc = env["nc"]
    WB1, (STG0, STG1) = env["WB1"], env["STG"]
    sd, sv, ss = env["sd"], env["sv"], env["ss"]
    for dc in range(ND):
        stg = (STG0, STG1)[dc % 2]
        if dc >= 2:
            em.wait("y", sv if dc % 2 == 0 else ss)
        ov = (lambda a: a) if oview is None else oview
        em.op("y", lambda y, s=src_ap(dc), g=stg, nco=ncols, o=ov:
              y.dma_start(out=o(g[:, 0:nco]), in_=s), sd, 16)
        em.wait("v" if dc % 2 == 0 else "s", sd)
        eng = "v" if dc % 2 == 0 else "s"
        if eng == "v":
            em.op("v", lambda v, g=stg, dcc=dc, nco=ncols:
                  v.tensor_copy(WB1[:, dcc * nco:(dcc + 1) * nco], g[:, 0:nco]), sv)
        else:
            em.op("s", lambda s, g=stg, dcc=dc, nco=ncols:
                  s.copy(WB1[:, dcc * nco:(dcc + 1) * nco], g[:, 0:nco]), ss)


def emit_ln_stats(env, em, c, xn32, sq):
    X, STAT = env["X"], env["STAT"]
    sv, ss = env["sv"], env["ss"]
    xc = X[:, c * D:(c + 1) * D]
    em.op("v", lambda v: v.reduce_sum(STAT[:, 48:49], xc, axis=AX.X), sv)
    em.wait("v", sv)
    em.op("v", lambda v: v.tensor_scalar(
        STAT[:, c:c + 1], STAT[:, 48:49], 1.0 / D, None, op0=ALU.mult), sv)
    em.wait("v", sv)
    em.op("v", lambda v: v.tensor_scalar(
        xn32[:, :], xc, STAT[:, c:c + 1], None, op0=ALU.subtract), sv)
    em.wait("s", sv)
    em.op("s", lambda s: s.activation(sq[:, 0:D], xn32[:, :], AF.Square,
                                      accum_out=STAT[:, 50:51]), ss)
    em.wait("v", ss)
    em.op("v", lambda v: v.tensor_scalar(
        STAT[:, 51:52], STAT[:, 50:51], 1.0 / D, None, op0=ALU.mult), sv)
    em.wait("s", sv)
    em.op("s", lambda s: s.activation(STAT[:, 52:53], STAT[:, 51:52], AF.Ln,
                                      bias=f32(EPS)), ss)
    em.wait("s", ss)
    em.op("s", lambda s: s.activation(STAT[:, 16 + c:17 + c], STAT[:, 52:53],
                                      AF.Exp, scale=f32(-0.5)), ss)
    em.wait("v", ss)
    em.op("v", lambda v: v.tensor_scalar(
        xn32[:, :], xn32[:, :], STAT[:, 16 + c:17 + c], None, op0=ALU.mult), sv)
    em.wait("v", sv)


def build_l0(env):
    nc = env["nc"]
    A, B, X, WB1, STAT, MASK, BB, bqp, ROW, ones1, idf, idb = (
        env["A"], env["B"], env["X"], env["WB1"], env["STAT"], env["MASK"],
        env["BB"], env["bqp"], env["ROW"], env["ones1"], env["idf"], env["idb"])
    W = env["W"]
    sd, sv, ss, st, sg = env["sd"], env["sv"], env["ss"], env["st"], env["sg"]

    with ExitStack() as _es2:
        ent = _es2.enter_context
        srow = ent(nc.sbuf_tensor("srow", [1, T], F32))
        s_row = ent(nc.sbuf_tensor("s_row", [128, T], F32))
        XN32 = ent(nc.sbuf_tensor("XN32", [128, D], F32))
        SC = ent(nc.sbuf_tensor("SC", [128, D], F32))
        wrb = ent(nc.sbuf_tensor("wrb", [128, D], F32))
        em = mk_em(env)
        load_weight_dxe(env, em, lambda dc: W["wq0"][dc * 128:(dc + 1) * 128, :], D)
        em.op("y", lambda y: y.dma_start(out=bqp[:, :], in_=W["bq0t"]),
              sd, 16)
        em.op("y", lambda y: y.dma_start(out=ROW[:, :], in_=W["bq0"].rearrange("(a d) -> a d", a=1)),
              sd, 16)
        em.wait("y", sd)
        with nc.Block() as blk:
            em.flush(blk)

        with nc.psum_tensor("bc_ps", [128, 512], F32) as bc_ps:
            em = mk_em(env)
            for sl in range(2):
                em.op("t", lambda t, s=sl: t.matmul(
                    bc_ps[:, :], ones1[:, :], ROW[:, s * 512:(s + 1) * 512],
                    start=True, stop=True), st)
                em.wait("v", st)
                em.op("v", lambda v, s=sl: v.tensor_copy(
                    BB[:, s * 512:(s + 1) * 512], bc_ps[:, :]), sv)
                em.wait("t", sv)
            with nc.Block() as blk:
                em.flush(blk)

            em = mk_em(env)
            em.op("y", lambda y: y.dma_start(out=ROW[:, :], in_=W["wr0"].rearrange("(a d) -> a d", a=1)),
                  sd, 16)
            em.wait("t", sd)
            for sl in range(2):
                em.op("t", lambda t, s=sl: t.matmul(
                    bc_ps[:, :], ones1[:, :], ROW[:, s * 512:(s + 1) * 512],
                    start=True, stop=True), st)
                em.wait("v", st)
                em.op("v", lambda v, s=sl: v.tensor_copy(
                    wrb[:, s * 512:(s + 1) * 512], bc_ps[:, :]), sv)
                em.wait("t", sv)
            with nc.Block() as blk:
                em.flush(blk)

        with ExitStack() as _es3:
            ent = _es3.enter_context
            XNF = ent(nc.sbuf_tensor("XNF", [128, D], BF16))
            tp0 = ent(nc.psum_tensor("tp0", [128, 128], F32))
            tp1 = ent(nc.psum_tensor("tp1", [128, 128], F32))
            qf0 = ent(nc.psum_tensor("qf0", [128, 128], F32))
            qf1 = ent(nc.psum_tensor("qf1", [128, 128], F32))
            qt0 = ent(nc.psum_tensor("qt0", [128, 512], F32))
            qt1 = ent(nc.psum_tensor("qt1", [128, 512], F32))
            for c in range(NT):
                em = mk_em(env)
                emit_ln_stats(env, em, c, XN32, s_row)
                em.op("v", lambda v: v.tensor_tensor(
                    s_row[:, 0:D], XN32[:, :], wrb[:, :], op=ALU.mult), sv)
                em.op("v", lambda v, cc=c: v.reduce_sum(
                    STAT[:, 32 + cc:33 + cc], s_row[:, 0:D], axis=AX.X), sv)
                em.wait("t", sv)
                tps = (tp0, tp1)
                for dc in range(ND):
                    if dc >= 2:
                        em.wait("t", sv)
                    em.op("t", lambda t, d=dc, p=tps[dc % 2]: t.transpose(
                        p[:, :], XN32[:, d * 128:(d + 1) * 128], idf[:, :]), st)
                    em.wait("v", st)
                    em.op("v", lambda v, d=dc, p=tps[dc % 2]: v.tensor_copy(
                        XNF[:, d * 128:(d + 1) * 128], p[:, :]), sv)
                env["fence_v"](em)
                em.wait("t", sv)
                qfs = (qf0, qf1)
                for i in range(ND):
                    ps = qfs[i % 2]
                    if i >= 2:
                        em.wait("t", sv, sv.n)
                    for dc in range(ND):
                        em.op("t", lambda t, ii=i, d=dc, p=ps: t.matmul(
                            p[:, :],
                            WB1[:, d * D + ii * 128:d * D + (ii + 1) * 128],
                            XNF[:, d * 128:(d + 1) * 128],
                            start=(d == 0), stop=(d == ND - 1)),
                            st if dc == ND - 1 else None)
                    em.wait("v", st)
                    em.op("v", lambda v, ii=i, cc=c, p=ps: v.tensor_scalar(
                        A[:, ii * T + cc * 128:ii * T + (cc + 1) * 128],
                        p[:, :], bqp[:, ii:ii + 1], None, op0=ALU.add), sv)
                qts = (qt0, qt1)
                for h in range(2):
                    ps = qts[h]
                    for dc in range(ND):
                        em.op("t", lambda t, hh=h, d=dc, p=ps: t.matmul(
                            p[:, :],
                            XNF[:, d * 128:(d + 1) * 128],
                            WB1[:, d * D + hh * 512:d * D + (hh + 1) * 512],
                            start=(d == 0), stop=(d == ND - 1)),
                            st if dc == ND - 1 else None)
                    em.wait("v", st)
                    em.op("v", lambda v, hh=h, cc=c, p=ps: v.tensor_tensor(
                        B[:, cc * D + hh * 512:cc * D + (hh + 1) * 512],
                        p[:, :], BB[:, hh * 512:(hh + 1) * 512], op=ALU.add), sv)
                with nc.Block() as blk:
                    em.flush(blk)

        with (
            nc.psum_tensor("trp", [1, 128], F32) as trp,
            nc.psum_tensor("bcp", [128, 512], F32) as bcp,
        ):
            em = mk_em(env)
            for c in range(NT):
                em.op("t", lambda t, cc=c: t.transpose(
                    trp[:, :], STAT[:, 32 + cc:33 + cc], idf[:, :]), st)
                em.wait("v", st)
                em.op("v", lambda v, cc=c: v.tensor_copy(
                    srow[:, cc * 128:(cc + 1) * 128], trp[:, :]), sv)
                em.wait("t", sv)
            with nc.Block() as blk:
                em.flush(blk)
            em = mk_em(env)
            for sl in range(4):
                em.op("t", lambda t, s=sl: t.matmul(
                    bcp[:, :], ones1[:, :], srow[:, s * 512:(s + 1) * 512],
                    start=True, stop=True), st)
                em.wait("v", st)
                em.op("v", lambda v, s=sl: v.tensor_copy(
                    s_row[:, s * 512:(s + 1) * 512], bcp[:, :]), sv)
                em.wait("t", sv)
            with nc.Block() as blk:
                em.flush(blk)
            em = mk_em(env)
            for c in range(NT):
                for sl in range(4):
                    em.op("v", lambda v, cc=c, s=sl: v.tensor_scalar(
                        SC[:, 0:512], s_row[:, s * 512:(s + 1) * 512],
                        STAT[:, 32 + cc:33 + cc], 0.0, op0=ALU.is_gt,
                        op1=ALU.add, accum_out=STAT[:, 53 + s:54 + s]), sv)
                    em.wait("v", sv)
                em.op("v", lambda v: v.tensor_tensor(
                    STAT[:, 53:54], STAT[:, 53:54], STAT[:, 54:55], op=ALU.add), sv)
                em.wait("v", sv)
                em.op("v", lambda v: v.tensor_tensor(
                    STAT[:, 55:56], STAT[:, 55:56], STAT[:, 56:57], op=ALU.add), sv)
                em.wait("v", sv)
                em.op("v", lambda v: v.tensor_tensor(
                    STAT[:, 53:54], STAT[:, 53:54], STAT[:, 55:56], op=ALU.add), sv)
                em.wait("v", sv)
                em.op("v", lambda v, cc=c: v.tensor_scalar(
                    MASK[:, cc:cc + 1], STAT[:, 53:54], float(KSEL), None,
                    op0=ALU.is_lt), sv)
                em.wait("v", sv)
            with nc.Block() as blk:
                em.flush(blk)

        if os.environ.get("MK_DBG6"):
            qdbg6 = nc.dram_tensor("qdbg6", [128, NT * D], F32,
                                   kind="ExternalOutput").ap()
            bdbg6 = nc.dram_tensor("bdbg6", [128, NT * D], F32,
                                   kind="ExternalOutput").ap()
            wdbg6 = nc.dram_tensor("wdbg6", [128, ND * D], F32,
                                   kind="ExternalOutput").ap()
            em = mk_em(env)
            for seg in range(ND):
                em.op("v", lambda v, s2=seg: v.tensor_copy(
                    XN32[:, :], WB1[:, s2 * D:(s2 + 1) * D]), sv)
                em.wait("y", sv)
                em.op("y", lambda y, s2=seg: y.dma_start(
                    out=wdbg6[:, s2 * D:(s2 + 1) * D], in_=XN32[:, :]),
                    env["sdo"], 16)
                em.wait("v", env["sdo"])
            for seg in range(NT):
                em.op("v", lambda v, s2=seg: v.tensor_copy(
                    XN32[:, :], A[:, s2 * D:(s2 + 1) * D]), sv)
                em.wait("y", sv)
                em.op("y", lambda y, s2=seg: y.dma_start(
                    out=qdbg6[:, s2 * D:(s2 + 1) * D], in_=XN32[:, :]),
                    env["sdo"], 16)
                em.wait("v", env["sdo"])
                em.op("v", lambda v, s2=seg: v.tensor_copy(
                    XN32[:, :], B[:, s2 * D:(s2 + 1) * D]), sv)
                em.wait("y", sv)
                em.op("y", lambda y, s2=seg: y.dma_start(
                    out=bdbg6[:, s2 * D:(s2 + 1) * D], in_=XN32[:, :]),
                    env["sdo"], 16)
                em.wait("v", env["sdo"])
            with nc.Block() as blk:
                em.flush(blk)

        em = mk_em(env)
        load_weight_dxe(env, em, lambda dc: W["wb0"][dc * 128:(dc + 1) * 128, :], D)
        em.wait("y", env["sd"])
        with nc.Block() as blk:
            em.flush(blk)

        with ExitStack() as _es4:
            ent = _es4.enter_context
            Pb = ent(nc.sbuf_tensor("Pb", [128, T], BF16))
            PT = ent(nc.sbuf_tensor("PT", [128, 512], BF16))
            Pg = ent(nc.sbuf_tensor("Pg", [128, D], F32))
            XNF3 = ent(nc.sbuf_tensor("XNF3", [128, D], BF16))
            scp0 = ent(nc.psum_tensor("scp0", [128, 512], F32))
            scp1 = ent(nc.psum_tensor("scp1", [128, 512], F32))
            ptp0 = ent(nc.psum_tensor("ptp0", [128, 128], BF16))
            ptp1 = ent(nc.psum_tensor("ptp1", [128, 128], BF16))
            pvp0 = ent(nc.psum_tensor("pvp0", [128, 512], F32))
            pvp1 = ent(nc.psum_tensor("pvp1", [128, 512], F32))
            tpf0 = ent(nc.psum_tensor("tpf0", [128, 128], F32))
            tpf1 = ent(nc.psum_tensor("tpf1", [128, 128], F32))
            pgp0, pgp1 = scp0, scp1
            for c in range(NT):
                em = mk_em(env)
                scs = (scp0, scp1)
                AB2 = os.environ.get("MK_AB", "") == "2"
                for sl in (range(4) if not AB2 else []):
                    ps = scs[sl % 2]
                    if sl >= 2:
                        em.wait("t", sv)
                    for i in range(ND):
                        em.op("t", lambda t, s=sl, ii=i, p=ps: t.matmul(
                            p[:, :],
                            A[:, ii * T + c * 128:ii * T + (c + 1) * 128],
                            A[:, ii * T + s * 512:ii * T + (s + 1) * 512],
                            start=(ii == 0), stop=(ii == ND - 1)),
                            st if i == ND - 1 else None)
                    em.wait("v", st)
                    em.op("v", lambda v, s=sl, p=ps: v.tensor_copy(
                        s_row[:, s * 512:(s + 1) * 512], p[:, :]), sv)
                if AB2:
                    em.op("v", lambda v: v.memset(SC[:, :], 1.0), sv)
                em.wait("v", sv)
                if not AB2:
                    em.op("v", lambda v: v.reduce_max(
                        STAT[:, 48:49], s_row[:, :], axis=AX.X, negate=True), sv)
                    em.wait("v", sv)
                    em.op("v", lambda v: v.tensor_scalar(
                        STAT[:, 49:50], STAT[:, 48:49], 1.0 / 32.0, None,
                        op0=ALU.mult), sv)
                    em.wait("s", sv)
                    em.op("s", lambda s: s.activation(
                        Pb[:, :], s_row[:, :], AF.Exp, bias=STAT[:, 49:50],
                        scale=f32(1.0 / 32.0), accum_out=STAT[:, 50:51]), ss)
                    em.wait("v", ss)
                    em.op("v", lambda v: v.reciprocal(STAT[:, 51:52], STAT[:, 50:51]),
                          sv)
                    pts = (ptp0, ptp1)
                    env["fence_s"](em)
                    em.wait("t", ss)
                    for jb in range(4):
                        for jo in range(4):
                            j = jb * 4 + jo
                            if j >= 2:
                                em.wait("t", ss)
                            em.op("t", lambda t, jj=j, p=pts[j % 2]: t.transpose(
                                p[:, :], Pb[:, jj * 128:(jj + 1) * 128],
                                idb[:, :]), st)
                            em.wait("s", st)
                            em.op("s", lambda s, jj=j, o=jo, p=pts[j % 2]: s.copy(
                                PT[:, o * 128:(o + 1) * 128], p[:, :]), ss)
                        env["fence_s"](em)
                        em.wait("t", ss)
                        for jo in range(4):
                            j = jb * 4 + jo
                            for h in range(2):
                                em.op("t", lambda t, jj=j, o=jo, hh=h,
                                      p=(pvp0, pvp1)[h]: t.matmul(
                                    p[:, :],
                                    PT[:, o * 128:(o + 1) * 128],
                                    B[:, jj * D + hh * 512:jj * D + (hh + 1) * 512],
                                    start=(jj == 0), stop=(jj == NT - 1)),
                                    st if (j == NT - 1 and h == 1) else None)
                    em.wait("v", st)
                    em.wait("v", sv, sv.n)
                    for h in range(2):
                        em.op("v", lambda v, hh=h, p=(pvp0, pvp1)[h]: v.tensor_scalar(
                            SC[:, hh * 512:(hh + 1) * 512], p[:, :],
                            STAT[:, 51:52], None, op0=ALU.mult), sv)
                em.op("v", lambda v: v.tensor_scalar(
                    XN32[:, :], X[:, c * D:(c + 1) * D], STAT[:, c:c + 1], None,
                    op0=ALU.subtract), sv)
                em.wait("v", sv)
                em.op("v", lambda v: v.tensor_scalar(
                    XN32[:, :], XN32[:, :], STAT[:, 16 + c:17 + c], None,
                    op0=ALU.mult), sv)
                AB = os.environ.get("MK_AB", "")
                em.wait("t", sv)
                tfs = (tpf0, tpf1)
                for dc in (range(ND) if AB not in ("1", "2") else []):
                    if dc >= 2:
                        em.wait("t", ss)
                    em.op("t", lambda t, d=dc, p=tfs[dc % 2]: t.transpose(
                        p[:, 0:128], XN32[:, d * 128:(d + 1) * 128],
                        idf[:, :]), st)
                    em.wait("s", st)
                    em.op("s", lambda s, d=dc, p=tfs[dc % 2]: s.copy(
                        XNF3[:, d * 128:(d + 1) * 128], p[:, 0:128]), ss)
                env["fence_s"](em)
                em.wait("t", ss)
                pgs = (pgp0, pgp1)
                for h in (range(2) if AB not in ("1", "2") else []):
                    for dc in range(ND):
                        em.op("t", lambda t, hh=h, d=dc, p=pgs[h]: t.matmul(
                            p[:, :],
                            XNF3[:, d * 128:(d + 1) * 128],
                            WB1[:, d * D + hh * 512:d * D + (hh + 1) * 512],
                            start=(d == 0), stop=(d == ND - 1)),
                            st if dc == ND - 1 else None)
                    em.wait("s", st)
                    em.op("s", lambda s, hh=h, p=pgs[h]: s.copy(
                        Pg[:, hh * 512:(hh + 1) * 512], p[:, :]), ss)
                if AB in ("1", "2"):
                    em.op("v", lambda v: v.tensor_copy(
                        s_row[:, D:2 * D], XN32[:, :]), sv)
                em.wait("s", ss)
                if AB not in ("1", "2"):
                    em.op("s", lambda s: s.activation(
                        s_row[:, 0:D], Pg[:, :], AF.Square), ss)
                em.wait("v", ss)
                if AB not in ("1", "2"):
                    em.op("v", lambda v: v.tensor_scalar(
                        s_row[:, 0:D], s_row[:, 0:D], GC1 * GC2, GC1,
                        op0=ALU.mult, op1=ALU.add), sv)
                    em.op("v", lambda v: v.tensor_tensor(
                        s_row[:, D:2 * D], s_row[:, 0:D], Pg[:, :], op=ALU.mult), sv)
                em.wait("s", sv)
                if AB not in ("1", "2"):
                    em.op("s", lambda s: s.activation(
                        s_row[:, 0:D], s_row[:, D:2 * D], AF.Tanh), ss)
                em.wait("v", ss)
                if AB not in ("1", "2"):
                    em.op("v", lambda v: v.scalar_tensor_tensor(
                        s_row[:, D:2 * D], s_row[:, 0:D], 1.0, Pg[:, :],
                        op0=ALU.add, op1=ALU.mult), sv)
                    em.op("v", lambda v: v.tensor_scalar(
                        s_row[:, D:2 * D], s_row[:, D:2 * D], 0.5, None,
                        op0=ALU.mult), sv)
                    em.op("v", lambda v: v.tensor_tensor(
                        Pg[:, :], s_row[:, D:2 * D], XN32[:, :], op=ALU.subtract), sv)
                    em.op("v", lambda v, cc=c: v.tensor_scalar(
                        Pg[:, :], Pg[:, :], MASK[:, cc:cc + 1], None,
                        op0=ALU.mult), sv)
                    em.op("v", lambda v: v.tensor_tensor(
                        s_row[:, D:2 * D], XN32[:, :], Pg[:, :], op=ALU.add), sv)
                em.wait("s", sv)
                em.op("s", lambda s: s.activation(
                    s_row[:, 0:D], s_row[:, D:2 * D], AF.Tanh,
                    scale=f32(0.5)), ss)
                em.wait("v", ss)
                em.op("v", lambda v: v.tensor_scalar(
                    s_row[:, 0:D], s_row[:, 0:D], 0.5, 0.5,
                    op0=ALU.mult, op1=ALU.add), sv)
                em.op("v", lambda v: v.tensor_tensor(
                    s_row[:, 0:D], s_row[:, 0:D], s_row[:, D:2 * D],
                    op=ALU.mult), sv)
                em.op("v", lambda v: v.tensor_tensor(
                    SC[:, :], SC[:, :], s_row[:, 0:D], op=ALU.mult), sv)
                em.op("v", lambda v, cc=c: v.tensor_tensor(
                    X[:, cc * D:(cc + 1) * D], X[:, cc * D:(cc + 1) * D],
                    SC[:, :], op=ALU.add), sv)
                with nc.Block() as blk:
                    em.flush(blk)


def build_l1_part1(env):
    nc = env["nc"]
    A, B, X, WB1, STAT, GATES, BB, bqp, ROW, ones1, idf, idb = (
        env["A"], env["B"], env["X"], env["WB1"], env["STAT"], env["GATES"],
        env["BB"], env["bqp"], env["ROW"], env["ones1"], env["idf"], env["idb"])
    W = env["W"]
    sd, sd2, sdo, sv, ss, st, sg = (env["sd"], env["sd2"], env["sdo"],
                                    env["sv"], env["ss"], env["st"], env["sg"])

    with ExitStack() as _es5:
        ent = _es5.enter_context
        Kp = ent(nc.sbuf_tensor("Kp", [128, 2304], BF16))
        Vt = ent(nc.sbuf_tensor("Vt", [128, 2304], BF16))
        WKB = ent(nc.sbuf_tensor("WKB", [128, D], BF16))
        WVB = ent(nc.sbuf_tensor("WVB", [128, D], BF16))
        MRH = ent(nc.sbuf_tensor("MRH", [128, 64], BF16))
        MRL = ent(nc.sbuf_tensor("MRL", [128, 64], BF16))
        MRS = ent(nc.sbuf_tensor("MRS", [128, 64], F32))
        bqp1 = ent(nc.sbuf_tensor("bqp1", [128, 8], F32))
        bkvp = ent(nc.sbuf_tensor("bkvp", [128, 2], F32))
        em = mk_em(env)
        load_weight_dxe(env, em,
                        lambda dc: W["wq1"][dc * 128:(dc + 1) * 128, :], D)
        em.op("y", lambda y: y.dma_start(out=bqp1[:, :], in_=W["bq1"]), sd, 16)
        em.op("y", lambda y: y.dma_start(out=bkvp[:, 0:1], in_=W["bk1"]),
              sd, 16)
        em.op("y", lambda y: y.dma_start(out=bkvp[:, 1:2], in_=W["bv1"]),
              sd, 16)
        em.wait("g", sv)
        em.wait("g", ss)
        em.op("g", lambda g: g.dma_start(
            out=env["STG"][0][:, :], in_=W["wk1"]), sd2, 16)
        em.wait("v", sd2)
        em.op("v", lambda v: v.tensor_copy(WKB[:, :], env["STG"][0][:, :]), sv)
        em.op("g", lambda g: g.dma_start(
            out=env["STG"][1][:, :], in_=W["wv1"]), sd2, 16)
        em.wait("v", sd2)
        em.op("v", lambda v: v.tensor_copy(WVB[:, :], env["STG"][1][:, :]), sv)
        em.op("g", lambda g: g.dma_start(out=MRS[:, :], in_=W["mr1"]), sd2, 16)
        em.wait("v", sd2)
        em.op("v", lambda v: v.tensor_copy(MRH[:, :], MRS[:, :]), sv)
        em.wait("v", sv)
        em.op("v", lambda v: v.tensor_copy(env["STG"][0][:, 0:64], MRH[:, :]), sv)
        em.wait("v", sv)
        em.op("v", lambda v: v.tensor_tensor(
            MRS[:, :], MRS[:, :], env["STG"][0][:, 0:64], op=ALU.subtract), sv)
        em.wait("v", sv)
        em.op("v", lambda v: v.tensor_copy(MRL[:, :], MRS[:, :]), sv)
        em.op("g", lambda g: g.memset(Kp[:, 0:128], 0.0))
        em.op("g", lambda g: g.memset(Kp[:, 2176:2304], 0.0))
        em.op("g", lambda g: g.memset(Vt[:, 0:128], 0.0))
        em.op("g", lambda g: g.memset(Vt[:, 2176:2304], 0.0))
        em.wait("y", sd)
        em.wait("g", sd2)
        with nc.Block() as blk:
            em.flush(blk)

        with ExitStack() as _es6:
            ent = _es6.enter_context
            XN32 = ent(nc.sbuf_tensor("XN32L1", [128, D], F32))
            XNB = ent(nc.sbuf_tensor("XNB", [128, D], BF16))
            HI32 = ent(nc.sbuf_tensor("HI32", [128, D], F32))
            LOB = ent(nc.sbuf_tensor("LOB", [128, D], BF16))
            LOF = ent(nc.sbuf_tensor("LOF", [128, D], BF16))
            VFC = ent(nc.sbuf_tensor("VFC", [128, 128], BF16))
            GS = ent(nc.sbuf_tensor("GS", [128, 48], F32))
            txp0 = ent(nc.psum_tensor("txp0", [128, 128], BF16))
            txp1 = ent(nc.psum_tensor("txp1", [128, 128], BF16))
            lgp = ent(nc.psum_tensor("lgp", [128, 8], F32))
            qfp0 = ent(nc.psum_tensor("qfp0", [128, 128], F32))
            qfp1 = ent(nc.psum_tensor("qfp1", [128, 128], F32))
            kvp0 = ent(nc.psum_tensor("kvp0", [128, 128], F32))
            kvp1 = ent(nc.psum_tensor("kvp1", [128, 128], F32))
            vtp = ent(nc.psum_tensor("vtp", [128, 128], BF16))
            for c in range(NT):
                em = mk_em(env)
                emit_ln_stats(env, em, c, XN32, HI32)
                em.op("v", lambda v: v.tensor_copy(XNB[:, :], XN32[:, :]), sv)
                em.wait("v", sv)
                em.op("v", lambda v: v.tensor_copy(HI32[:, :], XNB[:, :]), sv)
                em.wait("v", sv)
                em.op("v", lambda v: v.tensor_tensor(
                    XN32[:, :], XN32[:, :], HI32[:, :], op=ALU.subtract), sv)
                em.wait("v", sv)
                em.op("v", lambda v: v.tensor_copy(LOB[:, :], XN32[:, :]), sv)
                em.op("y", lambda y, cc=c: y.dma_start(
                    out=W["x1d"].rearrange("(n p) m -> p n m", p=128)[:, cc, :],
                    in_=X[:, cc * D:(cc + 1) * D]), sdo, 16)
                em.wait("t", sv)
                txs = (txp0, txp1)
                for dc in range(ND):
                    if dc >= 2:
                        em.wait("t", ss)
                    em.op("t", lambda t, d=dc, p=txs[dc % 2]: t.transpose(
                        p[:, :], XNB[:, d * 128:(d + 1) * 128], idb[:, :]), st)
                    em.wait("s", st)
                    em.op("s", lambda s, d=dc, cc=c, p=txs[dc % 2]: s.copy(
                        A[:, d * T + cc * 128:d * T + (cc + 1) * 128], p[:, :]),
                        ss)
                for dc in range(ND):
                    em.wait("t", ss)
                    em.op("t", lambda t, d=dc, p=txs[dc % 2]: t.transpose(
                        p[:, :], LOB[:, d * 128:(d + 1) * 128], idb[:, :]), st)
                    em.wait("s", st)
                    em.op("s", lambda s, d=dc, p=txs[dc % 2]: s.copy(
                        LOF[:, d * 128:(d + 1) * 128], p[:, :]), ss)
                env["fence_s"](em)
                em.wait("t", ss)
                for pi in range(3):
                    for dc in range(ND):
                        if pi == 0:
                            lh, rh = (
                                A[:, dc * T + c * 128:dc * T + (c + 1) * 128],
                                MRH[:, dc * 8:(dc + 1) * 8])
                        elif pi == 1:
                            lh, rh = (LOF[:, dc * 128:(dc + 1) * 128],
                                      MRH[:, dc * 8:(dc + 1) * 8])
                        else:
                            lh, rh = (
                                A[:, dc * T + c * 128:dc * T + (c + 1) * 128],
                                MRL[:, dc * 8:(dc + 1) * 8])
                        em.op("t", lambda t, l=lh, r=rh, pp=pi, d=dc: t.matmul(
                            lgp[:, :], l, r,
                            start=(pp == 0 and d == 0),
                            stop=(pp == 2 and d == ND - 1)),
                            st if (pi == 2 and dc == ND - 1) else None)
                em.wait("v", st)
                em.op("v", lambda v: v.tensor_copy(GS[:, 0:8], lgp[:, :]), sv)
                em.wait("v", sv)
                em.op("v", lambda v: v.reduce_max(GS[:, 8:9], GS[:, 0:8],
                                                  axis=AX.X), sv)
                em.wait("v", sv)
                em.op("v", lambda v: v.tensor_scalar(
                    GS[:, 16:24], GS[:, 0:8], GS[:, 8:9], None, op0=ALU.is_ge),
                    sv)
                em.wait("v", sv)
                em.op("v", lambda v: v.scalar_tensor_tensor(
                    GS[:, 24:32], GS[:, 16:24], NEG, GS[:, 0:8],
                    op0=ALU.mult, op1=ALU.add), sv)
                em.wait("v", sv)
                em.op("v", lambda v: v.reduce_max(GS[:, 9:10], GS[:, 24:32],
                                                  axis=AX.X), sv)
                em.wait("v", sv)
                em.op("v", lambda v: v.tensor_scalar(
                    GS[:, 10:11], GS[:, 8:9], -1.0, None, op0=ALU.mult), sv)
                em.wait("s", sv)
                em.op("s", lambda s: s.activation(
                    GS[:, 32:40], GS[:, 0:8], AF.Exp, bias=GS[:, 10:11]), ss)
                em.op("s", lambda s: s.activation(
                    GS[:, 11:12], GS[:, 9:10], AF.Exp, bias=GS[:, 10:11]), ss)
                em.wait("v", ss)
                em.op("v", lambda v: v.tensor_scalar(
                    GS[:, 12:13], GS[:, 11:12], 1.0, None, op0=ALU.add), sv)
                em.wait("v", sv)
                em.op("v", lambda v: v.reciprocal(GS[:, 13:14], GS[:, 12:13]), sv)
                em.wait("v", sv)
                em.op("v", lambda v: v.tensor_scalar(
                    GS[:, 40:48], GS[:, 0:8], GS[:, 9:10], None, op0=ALU.is_ge),
                    sv)
                em.wait("v", sv)
                em.op("v", lambda v: v.tensor_tensor(
                    GS[:, 40:48], GS[:, 40:48], GS[:, 32:40], op=ALU.mult), sv)
                em.wait("v", sv)
                em.op("v", lambda v, cc=c: v.tensor_scalar(
                    GATES[:, cc * 8:(cc + 1) * 8], GS[:, 40:48], GS[:, 13:14],
                    None, op0=ALU.mult), sv)
                qfs = (qfp0, qfp1)
                for i in range(ND):
                    ps = qfs[i % 2]
                    if i >= 2:
                        em.wait("t", sv)
                    for dc in range(ND):
                        em.op("t", lambda t, ii=i, d=dc, cc=c, p=ps: t.matmul(
                            p[:, :],
                            WB1[:, d * D + ii * 128:d * D + (ii + 1) * 128],
                            A[:, d * T + cc * 128:d * T + (cc + 1) * 128],
                            start=(d == 0), stop=(d == ND - 1)),
                            st if dc == ND - 1 else None)
                    em.wait("v", st)
                    em.op("v", lambda v, ii=i, cc=c, p=ps: v.tensor_scalar(
                        B[:, ii * T + cc * 128:ii * T + (cc + 1) * 128],
                        p[:, :], bqp1[:, ii:ii + 1], None, op0=ALU.add), sv)
                kvs = (kvp0, kvp1)
                for dc in range(ND):
                    em.op("t", lambda t, d=dc, cc=c: t.matmul(
                        kvs[0][:, :], WKB[:, d * 128:(d + 1) * 128],
                        A[:, d * T + cc * 128:d * T + (cc + 1) * 128],
                        start=(d == 0), stop=(d == ND - 1)),
                        st if dc == ND - 1 else None)
                em.wait("v", st)
                em.op("v", lambda v, cc=c: v.tensor_scalar(
                    Kp[:, 128 + cc * 128:128 + (cc + 1) * 128], kvs[0][:, :],
                    bkvp[:, 0:1], None, op0=ALU.add), sv)
                for dc in range(ND):
                    em.op("t", lambda t, d=dc, cc=c: t.matmul(
                        kvs[1][:, :], WVB[:, d * 128:(d + 1) * 128],
                        A[:, d * T + cc * 128:d * T + (cc + 1) * 128],
                        start=(d == 0), stop=(d == ND - 1)),
                        st if dc == ND - 1 else None)
                em.wait("v", st)
                em.op("v", lambda v: v.tensor_scalar(
                    VFC[:, :], kvs[1][:, :], bkvp[:, 1:2], None, op0=ALU.add),
                    sv)
                em.op("v", lambda v: v.tensor_copy(STAT[:, 61:62],
                                                   STAT[:, 60:61]), sv)
                em.wait("t", sv)
                em.op("t", lambda t: t.transpose(vtp[:, :], VFC[:, :],
                                                 idb[:, :]), st)
                em.wait("v", st)
                em.op("v", lambda v, cc=c: v.tensor_copy(
                    Vt[:, 128 + cc * 128:128 + (cc + 1) * 128], vtp[:, :]), sv)
                with nc.Block() as blk:
                    em.flush(blk)

        if os.environ.get("MK_DBG2"):
            qdbg = nc.dram_tensor("qdbg", [128, NT * D], F32,
                                  kind="ExternalOutput").ap()
            kdbg = nc.dram_tensor("kdbg", [128, 2304], F32,
                                  kind="ExternalOutput").ap()
            vdbg = nc.dram_tensor("vdbg", [128, 2304], F32,
                                  kind="ExternalOutput").ap()
            adbg = nc.dram_tensor("adbg", [128, NT * D], F32,
                                  kind="ExternalOutput").ap()
            wdbg = nc.dram_tensor("wdbg", [128, 2 * D], F32,
                                  kind="ExternalOutput").ap()
            em = mk_em(env)
            em.op("v", lambda v: v.tensor_copy(env["STG"][0][:, :],
                                               WKB[:, :]), sv)
            em.op("v", lambda v: v.tensor_copy(env["STG"][1][:, :],
                                               WB1[:, 0:D]), sv)
            em.wait("y", sv)
            em.op("y", lambda y: y.dma_start(out=wdbg[:, 0:D],
                                             in_=env["STG"][0][:, :]),
                  env["sdo"], 16)
            em.op("y", lambda y: y.dma_start(out=wdbg[:, D:2 * D],
                                             in_=env["STG"][1][:, :]),
                  env["sdo"], 16)
            em.wait("y", env["sdo"])
            with nc.Block() as blk:
                em.flush(blk)
            em = mk_em(env)
            for seg in range(NT):
                em.op("v", lambda v, s2=seg: v.tensor_copy(
                    env["STG"][0][:, :], A[:, s2 * D:(s2 + 1) * D]), sv)
                em.wait("y", sv)
                em.op("y", lambda y, s2=seg: y.dma_start(
                    out=adbg[:, s2 * D:(s2 + 1) * D],
                    in_=env["STG"][0][:, :]), env["sdo"], 16)
                em.wait("v", env["sdo"])
            with nc.Block() as blk:
                em.flush(blk)
            em = mk_em(env)
            for seg in range(NT):
                em.op("v", lambda v, s2=seg: v.tensor_copy(
                    env["STG"][0][:, :], B[:, s2 * D:(s2 + 1) * D]), sv)
                em.wait("y", sv)
                em.op("y", lambda y, s2=seg: y.dma_start(
                    out=qdbg[:, s2 * D:(s2 + 1) * D],
                    in_=env["STG"][0][:, :]), env["sdo"], 16)
                em.wait("v", env["sdo"])
            for seg in range(3):
                w = 768
                em.op("v", lambda v, s2=seg, w2=w: v.tensor_copy(
                    env["STG"][0][:, 0:w2], Kp[:, s2 * w2:(s2 + 1) * w2]), sv)
                em.wait("y", sv)
                em.op("y", lambda y, s2=seg, w2=w: y.dma_start(
                    out=kdbg[:, s2 * w2:(s2 + 1) * w2],
                    in_=env["STG"][0][:, 0:w2]), env["sdo"], 16)
                em.wait("v", env["sdo"])
                em.op("v", lambda v, s2=seg, w2=w: v.tensor_copy(
                    env["STG"][0][:, 0:w2], Vt[:, s2 * w2:(s2 + 1) * w2]), sv)
                em.wait("y", sv)
                em.op("y", lambda y, s2=seg, w2=w: y.dma_start(
                    out=vdbg[:, s2 * w2:(s2 + 1) * w2],
                    in_=env["STG"][0][:, 0:w2]), env["sdo"], 16)
                em.wait("v", env["sdo"])
            with nc.Block() as blk:
                em.flush(blk)

        em = mk_em(env)
        load_weight_dxe(env, em,
                        lambda dc: W["wo1"][dc * 128:(dc + 1) * 128, :], D)
        em.op("y", lambda y: y.dma_start(out=ROW[:, :], in_=W["bo1"].rearrange("(a d) -> a d", a=1)),
              sd, 16)
        em.wait("y", sd)
        with nc.Block() as blk:
            em.flush(blk)
        with nc.psum_tensor("bc_psL1", [128, 512], F32) as bc_ps:
            em = mk_em(env)
            for sl in range(2):
                em.op("t", lambda t, s=sl: t.matmul(
                    bc_ps[:, :], ones1[:, :], ROW[:, s * 512:(s + 1) * 512],
                    start=True, stop=True), st)
                em.wait("v", st)
                em.op("v", lambda v, s=sl: v.tensor_copy(
                    BB[:, s * 512:(s + 1) * 512], bc_ps[:, :]), sv)
                em.wait("t", sv)
            with nc.Block() as blk:
                em.flush(blk)

        with ExitStack() as _est:
            ent = _est.enter_context
            BAND = ent(nc.sbuf_tensor("BAND", [128, 1024], F32))
            S5 = ent(nc.sbuf_tensor("S5", [128, 1024], F32))
            E5 = ent(nc.sbuf_tensor("E5", [128, 1024], F32))
            Pb5 = ent(nc.sbuf_tensor("Pb5", [128, 1024], BF16))
            PT5 = ent(nc.sbuf_tensor("PT5", [128, 1024], BF16))
            OC = ent(nc.sbuf_tensor("OC", [128, 2 * D], BF16))
            S1C = ent(nc.sbuf_tensor("S1C", [128, 2 * D], F32))
            scp0 = ent(nc.psum_tensor("scp0L1", [128, 512], F32))
            scp1 = ent(nc.psum_tensor("scp1L1", [128, 512], F32))
            ptp0 = ent(nc.psum_tensor("ptp0L1", [128, 512], BF16))
            ptp1 = ent(nc.psum_tensor("ptp1L1", [128, 512], BF16))
            pvp0 = ent(nc.psum_tensor("pvp0L1", [64, 128], F32))
            pvp1 = ent(nc.psum_tensor("pvp1L1", [64, 128], F32))
            s1p0 = ent(nc.psum_tensor("s1p0L1", [128, 512], F32))
            s1p1 = ent(nc.psum_tensor("s1p1L1", [128, 512], F32))

            em = mk_em(env)
            for r in range(2):
                em.op("g", lambda g, rr=r: g.memset(
                    BAND[:, rr * 512:(rr + 1) * 512], 0.0))
                em.op("g", lambda g, rr=r: g.affine_select(
                    out=BAND[:, rr * 512:(rr + 1) * 512],
                    in_=BAND[:, rr * 512:(rr + 1) * 512],
                    compare_op=ALU.is_ge, fill=NEG,
                    base=-128 * rr, pattern=[[1, 512]], channel_multiplier=-1))
                em.op("g", lambda g, rr=r: g.affine_select(
                    out=BAND[:, rr * 512:(rr + 1) * 512],
                    in_=BAND[:, rr * 512:(rr + 1) * 512],
                    compare_op=ALU.is_ge, fill=NEG,
                    base=128 * rr + 255, pattern=[[-1, 512]],
                    channel_multiplier=1))
            with nc.Block() as blk:
                em.flush(blk)

            em = mk_em(env)
            scs = (scp0, scp1)
            pts = (ptp0, ptp1)
            pvs = (pvp0, pvp1)
            maskmax_done = {0: 0, 1: 0}
            exp_done = {0: 0, 1: 0}
            pb_done = {0: 0, 1: 0}
            ptc_done = {0: 0, 1: 0}
            pv_done = {0: 0, 1: 0}
            ocevac_done = {0: 0, 1: 0}
            s1evac_done = {0: 0, 1: 0}
            spill_done = {0: 0, 1: 0}
            NH = NT * 16

            def emit_scores(k):
                c2 = k // 16
                h2 = k % 16
                par2 = k % 2
                g2 = h2 // 8
                i2 = h2 % 8
                n2 = c2 // 2
                if maskmax_done[par2]:
                    em.wait("t", sv, maskmax_done[par2])
                em.op("t", lambda t, gg=g2, ii=i2, p=scs[par2], cc=c2, nn=n2:
                      t.matmul(
                    p[:, :],
                    B[64 * gg:64 * (gg + 1),
                      ii * T + cc * 128:ii * T + (cc + 1) * 128],
                    Kp[64 * gg:64 * (gg + 1), 256 * nn:256 * nn + 512],
                    start=True, stop=True, tile_position=(64 * gg, 0)), st)
                scores_done[par2] = st.n

            scores_done = {0: 0, 1: 0}
            pbg_done = {0: 0, 1: 0}
            emit_scores(0)
            for k in range(NH):
                c = k // 16
                h = k % 16
                par = k % 2
                g = h // 8
                i = h % 8
                n = c // 2
                r = c % 2
                cpar = c % 2
                my_scores = scores_done[par]
                if k + 1 < NH:
                    emit_scores(k + 1)
                em.wait("v", st, my_scores)
                if exp_done[par]:
                    em.wait("v", ss, exp_done[par])
                em.op("v", lambda v, p=scs[par], rr=r, q=par:
                      v.tensor_tensor(
                    S5[:, q * 512:(q + 1) * 512], p[:, :],
                    BAND[:, rr * 512:(rr + 1) * 512], op=ALU.add), sv)
                em.wait("v", sv)
                em.op("v", lambda v, q=par: v.reduce_max(
                    STAT[:, 40 + q:41 + q], S5[:, q * 512:(q + 1) * 512],
                    axis=AX.X, negate=True), sv)
                em._fence("v")
                maskmax_done[par] = sv.n
                em.wait("s", sv, maskmax_done[par])
                em.op("s", lambda s, q=par: s.activation(
                    E5[:, q * 512:(q + 1) * 512],
                    S5[:, q * 512:(q + 1) * 512], AF.Exp,
                    bias=STAT[:, 40 + q:41 + q],
                    accum_out=STAT[:, 42 + q:43 + q]), ss)
                em._fence("s")
                exp_done[par] = ss.n
                em.wait("v", ss, exp_done[par])
                em.op("v", lambda v, q=par: v.reciprocal(
                    STAT[:, 44 + q:45 + q], STAT[:, 42 + q:43 + q]), sv)
                em._fence("v")
                em.wait("g", sv)
                if pbg_done[par]:
                    em.wait("g", st, pbg_done[par])
                em.op("g", lambda g2, q=par: g2.tensor_scalar(
                    Pb5[:, q * 512:(q + 1) * 512],
                    E5[:, q * 512:(q + 1) * 512],
                    STAT[:, 44 + q:45 + q], None, op0=ALU.mult), sg)
                em._fence("g")
                pb_done[par] = sg.n
                em.wait("t", sg, pb_done[par])
                if ptc_done[par]:
                    em.wait("t", ss, ptc_done[par])
                for j in range(4):
                    em.op("t", lambda t, jj=j, q=par, p=pts[par]:
                          t.transpose(
                        p[:, jj * 128:(jj + 1) * 128],
                        Pb5[:, q * 512 + jj * 128:q * 512 + (jj + 1) * 128],
                        idb[:, :]), st if j == 3 else None)
                pbg_done[par] = st.n
                em.wait("s", st)
                em.op("s", lambda s, q=par, p=pts[par]: s.copy(
                    PT5[:, q * 512:(q + 1) * 512], p[:, :]), ss)
                em._fence("s")
                ptc_done[par] = ss.n
                em.wait("t", ss, ptc_done[par])
                if ocevac_done[par]:
                    em.wait("t", ss, ocevac_done[par])
                for j in range(4):
                    em.op("t", lambda t, jj=j, gg=g, q=par, nn=n,
                          p=pvs[par]: t.matmul(
                        p[:, :],
                        Vt[:, (2 * nn + jj) * 128 + 64 * gg:
                           (2 * nn + jj) * 128 + 64 * (gg + 1)],
                        PT5[:, q * 512 + jj * 128:q * 512 + (jj + 1) * 128],
                        start=(jj == 0), stop=(jj == 3)),
                        st if j == 3 else None)
                pv_done[par] = st.n
                em.wait("s", st, pv_done[par])
                em.op("s", lambda s, gg=g, ii=i, q=cpar, p=pvs[par]:
                      s.copy(
                    OC[64 * gg:64 * (gg + 1),
                       q * D + ii * 128:q * D + (ii + 1) * 128],
                    p[:, :]), ss)
                em._fence("s")
                ocevac_done[par] = ss.n

                if h == 15:
                    oc_ready = max(ocevac_done[0], ocevac_done[1])
                    s1s = (s1p0, s1p1)
                    for hh in range(2):
                        em.wait("t", ss, oc_ready)
                        if s1evac_done[hh]:
                            em.wait("t", sv, s1evac_done[hh])
                        for i2 in range(ND):
                            em.op("t", lambda t, h2=hh, ii=i2, q=cpar:
                                  t.matmul(
                                s1s[h2][:, :],
                                OC[:, q * D + ii * 128:q * D + (ii + 1) * 128],
                                WB1[:, ii * D + h2 * 512:ii * D + (h2 + 1) * 512],
                                start=(ii == 0), stop=(ii == ND - 1)),
                                st if i2 == ND - 1 else None)
                        s1mm = st.n
                        em.wait("v", st, s1mm)
                        if spill_done[cpar]:
                            em.wait("v", sdo, spill_done[cpar])
                        em.op("v", lambda v, h2=hh, q=cpar: v.tensor_tensor(
                            S1C[:, q * D + h2 * 512:q * D + (h2 + 1) * 512],
                            s1s[h2][:, :],
                            BB[:, h2 * 512:(h2 + 1) * 512], op=ALU.add), sv)
                        em._fence("v")
                        s1evac_done[hh] = sv.n
                    em.wait("y", sv, max(s1evac_done[0], s1evac_done[1]))
                    em.op("y", lambda y, cc=c, q=cpar: y.dma_start(
                        out=W["s1d"].rearrange("(n p) m -> p n m", p=128)
                        [:, cc, :],
                        in_=S1C[:, q * D:(q + 1) * D]), sdo, 16)
                    spill_done[cpar] = sdo.n
            with nc.Block() as blk:
                em.flush(blk)

def build_l1_moe(env):
    nc = env["nc"]
    A, WB1, STAT, GATES = env["A"], env["WB1"], env["STAT"], env["GATES"]
    W = env["W"]
    sd, sd2, sd3, sd4, sdo, sv, ss, st, sg = (
        env["sd"], env["sd2"], env["sd3"], env["sd4"], env["sdo"],
        env["sv"], env["ss"], env["st"], env["sg"])
    out = env["out"]
    stg = env["STG"]
    NE = 8
    NM = 16
    TT = 2

    gdbg = None
    if os.environ.get("MK_DBG3"):
        gdbg = nc.dram_tensor("gdbg", [T, D], F32, kind="ExternalOutput").ap()

    with ExitStack() as _esm:
        ent = _esm.enter_context
        H = ent(nc.sbuf_tensor("H", [128, NM * 1024], BF16))
        OUTA = ent(nc.sbuf_tensor("OUTA", [128, 8 * D], F32))
        W13G0 = ent(nc.sbuf_tensor("W13G0", [128, D], BF16))
        W13G1 = ent(nc.sbuf_tensor("W13G1", [128, D], BF16))
        W13U0 = ent(nc.sbuf_tensor("W13U0", [128, D], BF16))
        W13U1 = ent(nc.sbuf_tensor("W13U1", [128, D], BF16))
        W2B0 = ent(nc.sbuf_tensor("W2B0", [128, 4 * D], BF16))
        W2B1 = ent(nc.sbuf_tensor("W2B1", [128, 4 * D], BF16))
        GSIL = ent(nc.sbuf_tensor("GSIL", [128, D], F32))
        XC = ent(nc.sbuf_tensor("XC", [128, D], F32))
        S1R = ent(nc.sbuf_tensor("S1R", [128, D], F32))
        STG2 = ent(nc.sbuf_tensor("STG2", [128, D], F32))
        STG3 = ent(nc.sbuf_tensor("STG3", [128, D], F32))
        YST0 = ent(nc.sbuf_tensor("YST0", [128, D], F32))
        YST1 = ent(nc.sbuf_tensor("YST1", [128, D], F32))
        gp0 = ent(nc.psum_tensor("gp0", [128, 512], F32))
        gp1 = ent(nc.psum_tensor("gp1", [128, 512], F32))
        up0 = ent(nc.psum_tensor("up0", [128, 512], F32))
        up1 = ent(nc.psum_tensor("up1", [128, 512], F32))
        yp0 = ent(nc.psum_tensor("yp0", [128, 512], F32))
        yp1 = ent(nc.psum_tensor("yp1", [128, 512], F32))

        w13g = (W13G0, W13G1)
        w13u = (W13U0, W13U1)
        w2b = (W2B0, W2B1)
        ysts = (YST0, YST1)

        for tt in range(TT):
            em = mk_em(env)
            em.op("g", lambda g: g.memset(OUTA[:, :], 0.0))
            with nc.Block() as blk:
                em.flush(blk)

            em = mk_em(env)
            nexp = int(os.environ.get("MK_NEXP", NE))

            def h_dma(e, m):
                par = m % 2
                sgg = (stg[0], STG2)[par]
                sgu = (stg[1], STG3)[par]
                em.wait("y", sv)
                em.wait("g", sv)
                em.op("y", lambda y, ee=e, mm=m, b=sgg: y.dma_start(
                    out=b[:, :], in_=W["w13"][ee, mm]), sd, 16)
                em.op("g", lambda g, ee=e, mm=m, b=sgu: g.dma_start(
                    out=b[:, :], in_=W["w13"][ee, 16 + mm]), sd2, 16)

            def y_dma(e, j):
                b = ysts[j % 2]
                em.wait("y" if j % 2 == 0 else "g", ss)
                qe = "y" if j % 2 == 0 else "g"
                qs = sd if j % 2 == 0 else sd2
                em.op(qe, lambda q, ee=e, mm=j, bb=b: q.dma_start(
                    out=bb[:, :], in_=W["w2"][ee, mm * 128:(mm + 1) * 128, :]),
                    qs, 16)

            def y_cast(e, j):
                b = ysts[j % 2]
                dst = w2b[(j // 4) % 2]
                em.wait("s", st)
                em.wait("s", sd if j % 2 == 0 else sd2)
                em.op("s", lambda s, bb=b, dd=dst, k=j % 4: s.copy(
                    dd[:, k * D:(k + 1) * D], bb[:, :]), ss)

            hready = {}
            ycast_ready = {}
            evacdone = {0: 0, 1: 0}
            ttready = 0
            ssready = 0

            def h_cast2(e, m):
                par = m % 2
                sgg = (stg[0], STG2)[par]
                sgu = (stg[1], STG3)[par]
                em.wait("v", st)
                em.wait("v", sd)
                em.op("v", lambda v, p=par, b=sgg: v.tensor_copy(
                    w13g[p][:, :], b[:, :]), sv)
                em.wait("v", sd2)
                em.op("v", lambda v, p=par, b=sgu: v.tensor_copy(
                    w13u[p][:, :], b[:, :]), sv)
                em._fence("v")
                hready[(e, m)] = sv.n

            for e in range(nexp):
                if e == 0:
                    h_dma(e, 0)
                    h_cast2(e, 0)
                for m in range(NM):
                    if m + 1 < NM:
                        h_dma(e, m + 1)
                        h_cast2(e, m + 1)
                    elif e + 1 < nexp:
                        h_dma(e + 1, 0)
                        h_cast2(e + 1, 0)
                    if m >= 8:
                        j = m - 8
                        y_dma(e, j)
                        y_cast(e, j)
                        if j % 4 == 3:
                            em._fence("s")
                            ycast_ready[(e, j // 4)] = ss.n
                    em.wait("t", sv, max(hready[(e, m)], ttready))
                    if ssready:
                        em.wait("t", ss, ssready)
                    for th in range(2):
                        gps = (gp0, gp1)[th]
                        ups = (up0, up1)[th]
                        for dc in range(ND):
                            em.op("t", lambda t, d=dc, p=m % 2, h2=th, q=gps:
                                  t.matmul(
                                q[:, :], w13g[p][:, d * 128:(d + 1) * 128],
                                A[:, d * T + tt * 1024 + h2 * 512:
                                  d * T + tt * 1024 + (h2 + 1) * 512],
                                start=(d == 0), stop=(d == ND - 1)),
                                st if dc == ND - 1 else None)
                        for dc in range(ND):
                            em.op("t", lambda t, d=dc, p=m % 2, h2=th, q=ups:
                                  t.matmul(
                                q[:, :], w13u[p][:, d * 128:(d + 1) * 128],
                                A[:, d * T + tt * 1024 + h2 * 512:
                                  d * T + tt * 1024 + (h2 + 1) * 512],
                                start=(d == 0), stop=(d == ND - 1)),
                                st if dc == ND - 1 else None)
                    em.wait("s", st)
                    for th in range(2):
                        em.op("s", lambda s, h2=th, q=(gp0, gp1)[th]:
                              s.activation(
                            GSIL[:, h2 * 512:(h2 + 1) * 512], q[:, :],
                            AF.Silu), ss)
                    em._fence("s")
                    ssready = ss.n
                    em.wait("v", ss, ssready)
                    for th in range(2):
                        em.op("v", lambda v, mm=m, h2=th, q=(up0, up1)[th]:
                              v.tensor_tensor(
                            H[:, mm * 1024 + h2 * 512:mm * 1024 + (h2 + 1) * 512],
                            GSIL[:, h2 * 512:(h2 + 1) * 512], q[:, :],
                            op=ALU.mult), sv)
                    em._fence("v")
                    ttready = sv.n

                for s4 in range(4):
                    for tc in range(8):
                        cidx = tt * 8 + tc
                        for eb in range(2):
                            par = (tc * 2 + eb) % 2
                            yps = (yp0, yp1)[par]
                            w = evacdone[par]
                            if s4 == 0 and tc == 0 and eb == 0:
                                w = max(w, ttready)
                            if w:
                                em.wait("t", sv, w)
                            if tc == 0 and eb == 0:
                                em.wait("t", ss, ycast_ready[(e, s4)])
                            for k in range(4):
                                m = s4 * 4 + k
                                em.op("t", lambda t, mm=m, k2=k, t2=tc, b=eb,
                                      p=yps, dd=w2b[s4 % 2]: t.matmul(
                                    p[:, :],
                                    H[:, mm * 1024 + t2 * 128:
                                      mm * 1024 + (t2 + 1) * 128],
                                    dd[:, k2 * D + b * 512:
                                       k2 * D + (b + 1) * 512],
                                    start=(k2 == 0), stop=(k2 == 3)),
                                    st if k == 3 else None)
                            em.wait("v", st)
                            em.op("v", lambda v, t2=tc, b=eb, ci=cidx, ee=e,
                                  p=yps: v.scalar_tensor_tensor(
                                OUTA[:, t2 * D + b * 512:t2 * D + (b + 1) * 512],
                                p[:, :], GATES[:, ci * 8 + ee:ci * 8 + ee + 1],
                                OUTA[:, t2 * D + b * 512:t2 * D + (b + 1) * 512],
                                op0=ALU.mult, op1=ALU.add), sv)
                            em._fence("v")
                            evacdone[par] = sv.n
                        if 1 <= s4 <= 2 and tc < 4:
                            j = (s4 + 1) * 4 + tc
                            y_dma(e, j)
                            y_cast(e, j)
                            if j % 4 == 3:
                                em._fence("s")
                                ycast_ready[(e, s4 + 1)] = ss.n
            with nc.Block() as blk:
                em.flush(blk)

            for tc in range(8):
                cidx = tt * 8 + tc
                em = mk_em(env)
                em.wait("s", sdo)
                em.wait("y", sv)
                em.wait("g", sv)
                em.op("y", lambda y, ci=cidx: y.dma_start(
                    out=XC[:, :],
                    in_=W["x1d"].rearrange("(n p) m -> p n m", p=128)[:, ci, :]),
                    sd, 16)
                em.op("g", lambda g, ci=cidx: g.dma_start(
                    out=S1R[:, :],
                    in_=W["s1d"].rearrange("(n p) m -> p n m", p=128)[:, ci, :]),
                    sd2, 16)
                em.op("s", lambda s, t2=tc: s.activation(
                    GSIL[:, :], OUTA[:, t2 * D:(t2 + 1) * D], AF.Silu), ss)
                env["fence_s"](em)
                if gdbg is not None:
                    em.wait("g", ss)
                    em.op("g", lambda g, ci=cidx: g.dma_start(
                        out=gdbg.rearrange("(n p) m -> p n m", p=128)[:, ci, :],
                        in_=GSIL[:, :]), sdo, 16)
                    em.wait("g", sdo)
                    em.wait("v", sdo)
                em.wait("v", ss)
                em.wait("v", sd)
                em.wait("v", sd2)
                em.op("v", lambda v: v.tensor_tensor(
                    GSIL[:, :], GSIL[:, :], S1R[:, :], op=ALU.mult), sv)
                em.wait("v", sv)
                em.op("v", lambda v: v.tensor_tensor(
                    GSIL[:, :], GSIL[:, :], XC[:, :], op=ALU.add), sv)
                env["fence_v"](em)
                em.wait("y", sv)
                em.op("y", lambda y, ci=cidx: y.dma_start(
                    out=out.rearrange("(n p) m -> p n m", p=128)[:, ci, :],
                    in_=GSIL[:, :]), sdo, 16)
                with nc.Block() as blk:
                    em.flush(blk)


def kernel(**inputs):
    n = 8
    x = np.ascontiguousarray(np.asarray(inputs["L0_attn_inputs"],
                                        dtype=np.float32))
    wmap = {
        "wq0": "L0_attn_Wq", "bq0": "L0_attn_bq", "wr0": "L0_ffn_w_router",
        "wb0": "L0_ffn_w_block", "wq1": "L1_attn_Wq", "bq1": "L1_attn_bq",
        "wk1": "L1_attn_Wk", "bk1": "L1_attn_bk", "wv1": "L1_attn_Wv",
        "bv1": "L1_attn_bv", "wo1": "L1_attn_Wo", "bo1": "L1_attn_bo",
        "mr1": "L1_ffn_moe_router", "w13": "L1_ffn_moe_w13",
        "w2": "L1_ffn_moe_w2",
    }
    ws = {k: np.ascontiguousarray(np.asarray(inputs[v], dtype=np.float32))
          for k, v in wmap.items()}
    idx = (np.arange(2)[None, :, None] * 512 + np.arange(8)[:, None, None] * 64
           + np.arange(64)[None, None, :]).reshape(-1)
    ws["bq0t"] = np.ascontiguousarray(ws["bq0"].reshape(8, 128).T)
    ws["wq1"] = np.ascontiguousarray(ws["wq1"][:, idx])
    ws["bq1"] = np.ascontiguousarray(
        ws["bq1"][idx].reshape(8, 2, 64).transpose(1, 2, 0).reshape(128, 8))
    ws["wo1"] = np.ascontiguousarray(ws["wo1"][idx, :])
    ws["wk1"] = np.ascontiguousarray(
        ws["wk1"].reshape(8, 128, 128).transpose(1, 0, 2).reshape(128, 1024))
    ws["wv1"] = np.ascontiguousarray(
        ws["wv1"].reshape(8, 128, 128).transpose(1, 0, 2).reshape(128, 1024))
    ws["mr1"] = np.ascontiguousarray(
        ws["mr1"].reshape(8, 128, 8).transpose(1, 0, 2).reshape(128, 64))
    ws["bk1"] = np.ascontiguousarray(ws["bk1"].reshape(128, 1))
    ws["bv1"] = np.ascontiguousarray(ws["bv1"].reshape(128, 1))
    ws["w13"] = np.ascontiguousarray(
        ws["w13"].reshape(8, 8, 128, 32, 128).transpose(0, 3, 2, 1, 4)
        .reshape(8, 32, 128, 1024))
    nc = build(do_l0=True, do_l1=True)
    in_maps = []
    for c in range(n):
        m = {"x": np.ascontiguousarray(x[c])}
        m.update(ws)
        in_maps.append(m)
    res = run_bass_kernel_spmd(nc, in_maps, core_ids=list(range(n)))
    outs = [np.asarray(res.results[c]["out"]) for c in range(n)]
    return np.stack(outs, axis=0)


# version 08
# speedup vs baseline: 1.4105x; 1.0212x over previous
import os
import numpy as np
from contextlib import ExitStack
import concourse.bass as bass
import concourse.mybir as mybir
from concourse.bass_utils import run_bass_kernel_spmd

F32 = mybir.dt.float32
BF16 = mybir.dt.bfloat16
AF = mybir.ActivationFunctionType
ALU = mybir.AluOpType
AX = mybir.AxisListType

T = 2048
D = 1024
NT = 16
ND = 8
KSEL = 1536
GC1 = 0.7978845608028654
GC2 = 0.044715
EPS = 1e-5
NEG = -1e30


def f32(v):
    return np.float32(v)


class Sem:
    def __init__(self, h, is_dma=False):
        self.h = h
        self.n = 0
        self.is_dma = is_dma


class Em:
    ENGS = ("t", "v", "s", "g", "y")

    def __init__(self, nc):
        self.nc = nc
        self.L = {e: [] for e in self.ENGS}
        self._stat = None
        self._engsem = {}

    def set_fences(self, stat, sems_by_eng):
        self._stat = stat
        self._engsem = sems_by_eng

    def _fence(self, peng):
        stat = self._stat
        sem = self._engsem[peng]
        if peng == "v":
            self.op("v", lambda v: v.tensor_copy(stat[:, 61:62],
                                                 stat[:, 60:61]), sem)
        elif peng == "s":
            self.op("s", lambda s: s.copy(stat[:, 62:63],
                                          stat[:, 60:61]), sem)
        elif peng == "g":
            self.op("g", lambda g: g.tensor_copy(stat[:, 63:64],
                                                 stat[:, 60:61]), sem)

    def op(self, eng, fn, sem=None, k=1):
        if sem is not None:
            sem.n += k
            kk = k
            hh = sem.h
            self.L[eng].append(lambda e: fn(e).then_inc(hh, kk))
        else:
            self.L[eng].append(fn)

    DMA_SETTLE_CYC = 800

    def wait(self, eng, sem, val=None):
        if val is None and self._stat is not None:
            for peng, psem in self._engsem.items():
                if sem is psem and eng != peng:
                    self._fence(peng)
                    break
        v = sem.n if val is None else val
        hh = sem.h
        self.L[eng].append(lambda e: e.wait_ge(hh, v))
        if sem.is_dma:
            cyc = self.DMA_SETTLE_CYC
            self.L[eng].append(lambda e: e.nop(cycle_cnt=cyc))

    def flush(self, blk):
        if self._stat is not None:
            for peng in ("v", "s", "g"):
                if self.L[peng]:
                    self._fence(peng)
        names = {"t": "tensor", "v": "vector", "s": "scalar",
                 "g": "gpsimd", "y": "sync"}
        for e in self.ENGS:
            lst = self.L[e]
            if not lst:
                continue

            def body(eng, lst=lst):
                for f in lst:
                    f(eng)
            getattr(blk, names[e])(body)
        self.L = {e: [] for e in self.ENGS}


def mk_em(env):
    em = Em(env["nc"])
    em.set_fences(env["STAT"], {"v": env["sv"], "s": env["ss"],
                                "g": env["sg"]})
    return em


def build(do_l0=True, do_l1=True):
    nc = bass.Bass("TRN2", target_bir_lowering=False)

    xin = nc.dram_tensor("x", [T, D], F32, kind="ExternalInput").ap()
    out = nc.dram_tensor("out", [T, D], F32, kind="ExternalOutput").ap()
    W = {}
    if do_l0:
        W["wq0"] = nc.dram_tensor("wq0", [D, D], F32, kind="ExternalInput").ap()
        W["bq0"] = nc.dram_tensor("bq0", [D], F32, kind="ExternalInput").ap()
        if not do_l1:
            W["bq0t"] = nc.dram_tensor("bq0t", [128, 8], F32,
                                       kind="ExternalInput").ap()
        W["wr0"] = nc.dram_tensor("wr0", [D], F32, kind="ExternalInput").ap()
        W["wb0"] = nc.dram_tensor("wb0", [D, D], F32, kind="ExternalInput").ap()
    if do_l1:
        W["wq1"] = nc.dram_tensor("wq1", [D, D], F32, kind="ExternalInput").ap()
        W["bq1"] = nc.dram_tensor("bq1", [128, 8], F32, kind="ExternalInput").ap()
        W["wk1"] = nc.dram_tensor("wk1", [128, D], F32, kind="ExternalInput").ap()
        W["bk1"] = nc.dram_tensor("bk1", [128, 1], F32, kind="ExternalInput").ap()
        W["wv1"] = nc.dram_tensor("wv1", [128, D], F32, kind="ExternalInput").ap()
        W["bv1"] = nc.dram_tensor("bv1", [128, 1], F32, kind="ExternalInput").ap()
        W["wo1"] = nc.dram_tensor("wo1", [D, D], F32, kind="ExternalInput").ap()
        W["bo1"] = nc.dram_tensor("bo1", [D], F32, kind="ExternalInput").ap()
        W["mr1"] = nc.dram_tensor("mr1", [128, 64], F32, kind="ExternalInput").ap()
        W["bq0t"] = nc.dram_tensor("bq0t", [128, 8], F32, kind="ExternalInput").ap()
        W["w13"] = nc.dram_tensor("w13", [8, 32, 128, 1024], F32,
                                  kind="ExternalInput").ap()
        W["w2"] = nc.dram_tensor("w2", [8, T, D], F32, kind="ExternalInput").ap()
        dbg = "ExternalOutput" if os.environ.get("MK_DBG") else "Internal"
        W["x1d"] = nc.dram_tensor("x1d", [T, D], F32, kind=dbg).ap()
        W["s1d"] = nc.dram_tensor("s1d", [T, D], F32, kind=dbg).ap()

    with ExitStack() as _es1:
        ent = _es1.enter_context
        A = ent(nc.sbuf_tensor("A", [128, NT * D], BF16))
        WB1 = ent(nc.sbuf_tensor("WB1", [128, ND * D], BF16))
        STG0 = ent(nc.sbuf_tensor("STG0", [128, D], F32))
        STG1 = ent(nc.sbuf_tensor("STG1", [128, D], F32))
        idb = ent(nc.sbuf_tensor("idb", [128, 128], BF16))
        idf = ent(nc.sbuf_tensor("idf", [128, 128], F32))
        STAT = ent(nc.sbuf_tensor("STAT", [128, 64], F32))
        GATES = ent(nc.sbuf_tensor("GATES", [128, 128], F32))
        MASK = ent(nc.sbuf_tensor("MASK", [128, 16], F32))
        BB = ent(nc.sbuf_tensor("BB", [128, D], F32))
        bqp = ent(nc.sbuf_tensor("bqp", [128, 8], F32))
        ROW = ent(nc.sbuf_tensor("ROW", [1, D], F32))
        ones1 = ent(nc.sbuf_tensor("ones1", [1, 128], F32))
        _sd = ent(nc.semaphore("sd"))
        _sd2 = ent(nc.semaphore("sd2"))
        _sdo = ent(nc.semaphore("sdo"))
        _sv = ent(nc.semaphore("sv"))
        _ss = ent(nc.semaphore("ss"))
        _st = ent(nc.semaphore("st"))
        _sg = ent(nc.semaphore("sg"))
        _sd3 = ent(nc.semaphore("sd3"))
        _sd4 = ent(nc.semaphore("sd4"))
        env = dict(nc=nc, A=A, WB1=WB1, STG=(STG0, STG1), idb=idb, idf=idf,
                   STAT=STAT, GATES=GATES, MASK=MASK, BB=BB, bqp=bqp,
                   ROW=ROW, ones1=ones1, W=W, xin=xin, out=out,
                   sd=Sem(_sd, True), sd2=Sem(_sd2, True), sdo=Sem(_sdo, True),
                   sv=Sem(_sv), ss=Sem(_ss), st=Sem(_st), sg=Sem(_sg),
                   sd3=Sem(_sd3, True), sd4=Sem(_sd4, True))

        def fence_v(em):
            em.op("v", lambda v: v.tensor_copy(STAT[:, 61:62],
                                               STAT[:, 60:61]), env["sv"])

        def fence_s(em):
            em.op("s", lambda s: s.copy(STAT[:, 62:63], STAT[:, 60:61]),
                  env["ss"])
        env["fence_v"] = fence_v
        env["fence_s"] = fence_s

        em = mk_em(env)
        em.op("g", lambda g: g.memset(idb[:, :], 0.0))
        em.op("g", lambda g: g.affine_select(
            out=idb[:, :], in_=idb[:, :], compare_op=ALU.not_equal, fill=1.0,
            base=0, pattern=[[-1, 128]], channel_multiplier=1))
        em.op("g", lambda g: g.memset(idf[:, :], 0.0))
        em.op("g", lambda g: g.affine_select(
            out=idf[:, :], in_=idf[:, :], compare_op=ALU.not_equal, fill=1.0,
            base=0, pattern=[[-1, 128]], channel_multiplier=1))
        em.op("g", lambda g: g.memset(ones1[:, :], 1.0))
        with nc.Block() as blk:
            em.flush(blk)

        with (
            nc.sbuf_tensor("X", [128, NT * D], F32) as X,
            nc.sbuf_tensor("B", [128, NT * D], BF16) as B,
        ):
            env["X"] = X
            env["B"] = B
            em = mk_em(env)
            em.op("y", lambda y: y.dma_start(
                out=X[:, :].rearrange("p (n m) -> p n m", n=NT),
                in_=xin.rearrange("(n p) m -> p n m", p=128)), env["sd"], 16)
            em.wait("y", env["sd"])
            with nc.Block() as blk:
                em.flush(blk)

            if do_l0:
                build_l0(env)
            if do_l1:
                build_l1_part1(env)
            else:
                em = mk_em(env)
                em.op("y", lambda y: y.dma_start(
                    out=out.rearrange("(n p) m -> p n m", p=128),
                    in_=X[:, :].rearrange("p (n m) -> p n m", n=NT)),
                    env["sdo"], 16)
                with nc.Block() as blk:
                    em.flush(blk)

        if do_l1:
            build_l1_moe(env)

        em = mk_em(env)
        em.wait("y", env["sdo"])
        with nc.Block() as blk:
            em.flush(blk)

    return nc


def load_weight_dxe(env, em, src_ap, ncols, oview=None):
    nc = env["nc"]
    WB1, (STG0, STG1) = env["WB1"], env["STG"]
    sd, sv, ss = env["sd"], env["sv"], env["ss"]
    for dc in range(ND):
        stg = (STG0, STG1)[dc % 2]
        if dc >= 2:
            em.wait("y", sv if dc % 2 == 0 else ss)
        ov = (lambda a: a) if oview is None else oview
        em.op("y", lambda y, s=src_ap(dc), g=stg, nco=ncols, o=ov:
              y.dma_start(out=o(g[:, 0:nco]), in_=s), sd, 16)
        em.wait("v" if dc % 2 == 0 else "s", sd)
        eng = "v" if dc % 2 == 0 else "s"
        if eng == "v":
            em.op("v", lambda v, g=stg, dcc=dc, nco=ncols:
                  v.tensor_copy(WB1[:, dcc * nco:(dcc + 1) * nco], g[:, 0:nco]), sv)
        else:
            em.op("s", lambda s, g=stg, dcc=dc, nco=ncols:
                  s.copy(WB1[:, dcc * nco:(dcc + 1) * nco], g[:, 0:nco]), ss)


def emit_ln_stats(env, em, c, xn32, sq):
    X, STAT = env["X"], env["STAT"]
    sv, ss = env["sv"], env["ss"]
    xc = X[:, c * D:(c + 1) * D]
    em.op("v", lambda v: v.reduce_sum(STAT[:, 48:49], xc, axis=AX.X), sv)
    em.wait("v", sv)
    em.op("v", lambda v: v.tensor_scalar(
        STAT[:, c:c + 1], STAT[:, 48:49], 1.0 / D, None, op0=ALU.mult), sv)
    em.wait("v", sv)
    em.op("v", lambda v: v.tensor_scalar(
        xn32[:, :], xc, STAT[:, c:c + 1], None, op0=ALU.subtract), sv)
    em.wait("s", sv)
    em.op("s", lambda s: s.activation(sq[:, 0:D], xn32[:, :], AF.Square,
                                      accum_out=STAT[:, 50:51]), ss)
    em.wait("v", ss)
    em.op("v", lambda v: v.tensor_scalar(
        STAT[:, 51:52], STAT[:, 50:51], 1.0 / D, None, op0=ALU.mult), sv)
    em.wait("s", sv)
    em.op("s", lambda s: s.activation(STAT[:, 52:53], STAT[:, 51:52], AF.Ln,
                                      bias=f32(EPS)), ss)
    em.wait("s", ss)
    em.op("s", lambda s: s.activation(STAT[:, 16 + c:17 + c], STAT[:, 52:53],
                                      AF.Exp, scale=f32(-0.5)), ss)
    em.wait("v", ss)
    em.op("v", lambda v: v.tensor_scalar(
        xn32[:, :], xn32[:, :], STAT[:, 16 + c:17 + c], None, op0=ALU.mult), sv)
    em.wait("v", sv)


def build_l0(env):
    nc = env["nc"]
    A, B, X, WB1, STAT, MASK, BB, bqp, ROW, ones1, idf, idb = (
        env["A"], env["B"], env["X"], env["WB1"], env["STAT"], env["MASK"],
        env["BB"], env["bqp"], env["ROW"], env["ones1"], env["idf"], env["idb"])
    W = env["W"]
    sd, sv, ss, st, sg = env["sd"], env["sv"], env["ss"], env["st"], env["sg"]

    with ExitStack() as _es2:
        ent = _es2.enter_context
        srow = ent(nc.sbuf_tensor("srow", [1, T], F32))
        s_row = ent(nc.sbuf_tensor("s_row", [128, T], F32))
        XN32 = ent(nc.sbuf_tensor("XN32", [128, D], F32))
        SC = ent(nc.sbuf_tensor("SC", [128, D], F32))
        wrb = ent(nc.sbuf_tensor("wrb", [128, D], F32))
        em = mk_em(env)
        load_weight_dxe(env, em, lambda dc: W["wq0"][dc * 128:(dc + 1) * 128, :], D)
        em.op("y", lambda y: y.dma_start(out=bqp[:, :], in_=W["bq0t"]),
              sd, 16)
        em.op("y", lambda y: y.dma_start(out=ROW[:, :], in_=W["bq0"].rearrange("(a d) -> a d", a=1)),
              sd, 16)
        em.wait("y", sd)
        with nc.Block() as blk:
            em.flush(blk)

        with nc.psum_tensor("bc_ps", [128, 512], F32) as bc_ps:
            em = mk_em(env)
            for sl in range(2):
                em.op("t", lambda t, s=sl: t.matmul(
                    bc_ps[:, :], ones1[:, :], ROW[:, s * 512:(s + 1) * 512],
                    start=True, stop=True), st)
                em.wait("v", st)
                em.op("v", lambda v, s=sl: v.tensor_copy(
                    BB[:, s * 512:(s + 1) * 512], bc_ps[:, :]), sv)
                em.wait("t", sv)
            with nc.Block() as blk:
                em.flush(blk)

            em = mk_em(env)
            em.op("y", lambda y: y.dma_start(out=ROW[:, :], in_=W["wr0"].rearrange("(a d) -> a d", a=1)),
                  sd, 16)
            em.wait("t", sd)
            for sl in range(2):
                em.op("t", lambda t, s=sl: t.matmul(
                    bc_ps[:, :], ones1[:, :], ROW[:, s * 512:(s + 1) * 512],
                    start=True, stop=True), st)
                em.wait("v", st)
                em.op("v", lambda v, s=sl: v.tensor_copy(
                    wrb[:, s * 512:(s + 1) * 512], bc_ps[:, :]), sv)
                em.wait("t", sv)
            with nc.Block() as blk:
                em.flush(blk)

        with ExitStack() as _es3:
            ent = _es3.enter_context
            XNF = ent(nc.sbuf_tensor("XNF", [128, D], BF16))
            tp0 = ent(nc.psum_tensor("tp0", [128, 128], F32))
            tp1 = ent(nc.psum_tensor("tp1", [128, 128], F32))
            qf0 = ent(nc.psum_tensor("qf0", [128, 128], F32))
            qf1 = ent(nc.psum_tensor("qf1", [128, 128], F32))
            qt0 = ent(nc.psum_tensor("qt0", [128, 512], F32))
            qt1 = ent(nc.psum_tensor("qt1", [128, 512], F32))
            for c in range(NT):
                em = mk_em(env)
                emit_ln_stats(env, em, c, XN32, s_row)
                em.op("v", lambda v: v.tensor_tensor(
                    s_row[:, 0:D], XN32[:, :], wrb[:, :], op=ALU.mult), sv)
                em.op("v", lambda v, cc=c: v.reduce_sum(
                    STAT[:, 32 + cc:33 + cc], s_row[:, 0:D], axis=AX.X), sv)
                em.wait("t", sv)
                tps = (tp0, tp1)
                for dc in range(ND):
                    if dc >= 2:
                        em.wait("t", sv)
                    em.op("t", lambda t, d=dc, p=tps[dc % 2]: t.transpose(
                        p[:, :], XN32[:, d * 128:(d + 1) * 128], idf[:, :]), st)
                    em.wait("v", st)
                    em.op("v", lambda v, d=dc, p=tps[dc % 2]: v.tensor_copy(
                        XNF[:, d * 128:(d + 1) * 128], p[:, :]), sv)
                env["fence_v"](em)
                em.wait("t", sv)
                qfs = (qf0, qf1)
                for i in range(ND):
                    ps = qfs[i % 2]
                    if i >= 2:
                        em.wait("t", sv, sv.n)
                    for dc in range(ND):
                        em.op("t", lambda t, ii=i, d=dc, p=ps: t.matmul(
                            p[:, :],
                            WB1[:, d * D + ii * 128:d * D + (ii + 1) * 128],
                            XNF[:, d * 128:(d + 1) * 128],
                            start=(d == 0), stop=(d == ND - 1)),
                            st if dc == ND - 1 else None)
                    em.wait("v", st)
                    em.op("v", lambda v, ii=i, cc=c, p=ps: v.tensor_scalar(
                        A[:, ii * T + cc * 128:ii * T + (cc + 1) * 128],
                        p[:, :], bqp[:, ii:ii + 1], None, op0=ALU.add), sv)
                qts = (qt0, qt1)
                for h in range(2):
                    ps = qts[h]
                    for dc in range(ND):
                        em.op("t", lambda t, hh=h, d=dc, p=ps: t.matmul(
                            p[:, :],
                            XNF[:, d * 128:(d + 1) * 128],
                            WB1[:, d * D + hh * 512:d * D + (hh + 1) * 512],
                            start=(d == 0), stop=(d == ND - 1)),
                            st if dc == ND - 1 else None)
                    em.wait("v", st)
                    em.op("v", lambda v, hh=h, cc=c, p=ps: v.tensor_tensor(
                        B[:, cc * D + hh * 512:cc * D + (hh + 1) * 512],
                        p[:, :], BB[:, hh * 512:(hh + 1) * 512], op=ALU.add), sv)
                with nc.Block() as blk:
                    em.flush(blk)

        with (
            nc.psum_tensor("trp", [1, 128], F32) as trp,
            nc.psum_tensor("bcp", [128, 512], F32) as bcp,
        ):
            em = mk_em(env)
            for c in range(NT):
                em.op("t", lambda t, cc=c: t.transpose(
                    trp[:, :], STAT[:, 32 + cc:33 + cc], idf[:, :]), st)
                em.wait("v", st)
                em.op("v", lambda v, cc=c: v.tensor_copy(
                    srow[:, cc * 128:(cc + 1) * 128], trp[:, :]), sv)
                em.wait("t", sv)
            with nc.Block() as blk:
                em.flush(blk)
            em = mk_em(env)
            for sl in range(4):
                em.op("t", lambda t, s=sl: t.matmul(
                    bcp[:, :], ones1[:, :], srow[:, s * 512:(s + 1) * 512],
                    start=True, stop=True), st)
                em.wait("v", st)
                em.op("v", lambda v, s=sl: v.tensor_copy(
                    s_row[:, s * 512:(s + 1) * 512], bcp[:, :]), sv)
                em.wait("t", sv)
            with nc.Block() as blk:
                em.flush(blk)
            em = mk_em(env)
            for c in range(NT):
                for sl in range(4):
                    em.op("v", lambda v, cc=c, s=sl: v.tensor_scalar(
                        SC[:, 0:512], s_row[:, s * 512:(s + 1) * 512],
                        STAT[:, 32 + cc:33 + cc], 0.0, op0=ALU.is_gt,
                        op1=ALU.add, accum_out=STAT[:, 53 + s:54 + s]), sv)
                    em.wait("v", sv)
                em.op("v", lambda v: v.tensor_tensor(
                    STAT[:, 53:54], STAT[:, 53:54], STAT[:, 54:55], op=ALU.add), sv)
                em.wait("v", sv)
                em.op("v", lambda v: v.tensor_tensor(
                    STAT[:, 55:56], STAT[:, 55:56], STAT[:, 56:57], op=ALU.add), sv)
                em.wait("v", sv)
                em.op("v", lambda v: v.tensor_tensor(
                    STAT[:, 53:54], STAT[:, 53:54], STAT[:, 55:56], op=ALU.add), sv)
                em.wait("v", sv)
                em.op("v", lambda v, cc=c: v.tensor_scalar(
                    MASK[:, cc:cc + 1], STAT[:, 53:54], float(KSEL), None,
                    op0=ALU.is_lt), sv)
                em.wait("v", sv)
            with nc.Block() as blk:
                em.flush(blk)

        if os.environ.get("MK_DBG6"):
            qdbg6 = nc.dram_tensor("qdbg6", [128, NT * D], F32,
                                   kind="ExternalOutput").ap()
            bdbg6 = nc.dram_tensor("bdbg6", [128, NT * D], F32,
                                   kind="ExternalOutput").ap()
            wdbg6 = nc.dram_tensor("wdbg6", [128, ND * D], F32,
                                   kind="ExternalOutput").ap()
            em = mk_em(env)
            for seg in range(ND):
                em.op("v", lambda v, s2=seg: v.tensor_copy(
                    XN32[:, :], WB1[:, s2 * D:(s2 + 1) * D]), sv)
                em.wait("y", sv)
                em.op("y", lambda y, s2=seg: y.dma_start(
                    out=wdbg6[:, s2 * D:(s2 + 1) * D], in_=XN32[:, :]),
                    env["sdo"], 16)
                em.wait("v", env["sdo"])
            for seg in range(NT):
                em.op("v", lambda v, s2=seg: v.tensor_copy(
                    XN32[:, :], A[:, s2 * D:(s2 + 1) * D]), sv)
                em.wait("y", sv)
                em.op("y", lambda y, s2=seg: y.dma_start(
                    out=qdbg6[:, s2 * D:(s2 + 1) * D], in_=XN32[:, :]),
                    env["sdo"], 16)
                em.wait("v", env["sdo"])
                em.op("v", lambda v, s2=seg: v.tensor_copy(
                    XN32[:, :], B[:, s2 * D:(s2 + 1) * D]), sv)
                em.wait("y", sv)
                em.op("y", lambda y, s2=seg: y.dma_start(
                    out=bdbg6[:, s2 * D:(s2 + 1) * D], in_=XN32[:, :]),
                    env["sdo"], 16)
                em.wait("v", env["sdo"])
            with nc.Block() as blk:
                em.flush(blk)

        em = mk_em(env)
        load_weight_dxe(env, em, lambda dc: W["wb0"][dc * 128:(dc + 1) * 128, :], D)
        em.wait("y", env["sd"])
        with nc.Block() as blk:
            em.flush(blk)

        with ExitStack() as _es4:
            ent = _es4.enter_context
            Pb = ent(nc.sbuf_tensor("Pb", [128, T], BF16))
            PT = ent(nc.sbuf_tensor("PT", [128, 512], BF16))
            Pg = ent(nc.sbuf_tensor("Pg", [128, D], F32))
            XNF3 = ent(nc.sbuf_tensor("XNF3", [128, D], BF16))
            scp0 = ent(nc.psum_tensor("scp0", [128, 512], F32))
            scp1 = ent(nc.psum_tensor("scp1", [128, 512], F32))
            ptp0 = ent(nc.psum_tensor("ptp0", [128, 128], BF16))
            ptp1 = ent(nc.psum_tensor("ptp1", [128, 128], BF16))
            pvp0 = ent(nc.psum_tensor("pvp0", [128, 512], F32))
            pvp1 = ent(nc.psum_tensor("pvp1", [128, 512], F32))
            tpf0 = ent(nc.psum_tensor("tpf0", [128, 128], F32))
            tpf1 = ent(nc.psum_tensor("tpf1", [128, 128], F32))
            pgp0, pgp1 = scp0, scp1
            for c in range(NT):
                em = mk_em(env)
                scs = (scp0, scp1)
                AB2 = os.environ.get("MK_AB", "") == "2"
                for sl in (range(4) if not AB2 else []):
                    ps = scs[sl % 2]
                    if sl >= 2:
                        em.wait("t", sv)
                    for i in range(ND):
                        em.op("t", lambda t, s=sl, ii=i, p=ps: t.matmul(
                            p[:, :],
                            A[:, ii * T + c * 128:ii * T + (c + 1) * 128],
                            A[:, ii * T + s * 512:ii * T + (s + 1) * 512],
                            start=(ii == 0), stop=(ii == ND - 1)),
                            st if i == ND - 1 else None)
                    em.wait("v", st)
                    em.op("v", lambda v, s=sl, p=ps: v.tensor_copy(
                        s_row[:, s * 512:(s + 1) * 512], p[:, :]), sv)
                if AB2:
                    em.op("v", lambda v: v.memset(SC[:, :], 1.0), sv)
                em.wait("v", sv)
                if not AB2:
                    em.op("v", lambda v: v.reduce_max(
                        STAT[:, 48:49], s_row[:, :], axis=AX.X, negate=True), sv)
                    em.wait("v", sv)
                    em.op("v", lambda v: v.tensor_scalar(
                        STAT[:, 49:50], STAT[:, 48:49], 1.0 / 32.0, None,
                        op0=ALU.mult), sv)
                    em.wait("s", sv)
                    em.op("s", lambda s: s.activation(
                        Pb[:, :], s_row[:, :], AF.Exp, bias=STAT[:, 49:50],
                        scale=f32(1.0 / 32.0), accum_out=STAT[:, 50:51]), ss)
                    em.wait("v", ss)
                    em.op("v", lambda v: v.reciprocal(STAT[:, 51:52], STAT[:, 50:51]),
                          sv)
                    pts = (ptp0, ptp1)
                    env["fence_s"](em)
                    em.wait("t", ss)
                    for jb in range(4):
                        for jo in range(4):
                            j = jb * 4 + jo
                            if j >= 2:
                                em.wait("t", ss)
                            em.op("t", lambda t, jj=j, p=pts[j % 2]: t.transpose(
                                p[:, :], Pb[:, jj * 128:(jj + 1) * 128],
                                idb[:, :]), st)
                            em.wait("s", st)
                            em.op("s", lambda s, jj=j, o=jo, p=pts[j % 2]: s.copy(
                                PT[:, o * 128:(o + 1) * 128], p[:, :]), ss)
                        env["fence_s"](em)
                        em.wait("t", ss)
                        for jo in range(4):
                            j = jb * 4 + jo
                            for h in range(2):
                                em.op("t", lambda t, jj=j, o=jo, hh=h,
                                      p=(pvp0, pvp1)[h]: t.matmul(
                                    p[:, :],
                                    PT[:, o * 128:(o + 1) * 128],
                                    B[:, jj * D + hh * 512:jj * D + (hh + 1) * 512],
                                    start=(jj == 0), stop=(jj == NT - 1)),
                                    st if (j == NT - 1 and h == 1) else None)
                    em.wait("v", st)
                    em.wait("v", sv, sv.n)
                    for h in range(2):
                        em.op("v", lambda v, hh=h, p=(pvp0, pvp1)[h]: v.tensor_scalar(
                            SC[:, hh * 512:(hh + 1) * 512], p[:, :],
                            STAT[:, 51:52], None, op0=ALU.mult), sv)
                em.op("v", lambda v: v.tensor_scalar(
                    XN32[:, :], X[:, c * D:(c + 1) * D], STAT[:, c:c + 1], None,
                    op0=ALU.subtract), sv)
                em.wait("v", sv)
                em.op("v", lambda v: v.tensor_scalar(
                    XN32[:, :], XN32[:, :], STAT[:, 16 + c:17 + c], None,
                    op0=ALU.mult), sv)
                AB = os.environ.get("MK_AB", "")
                em.wait("t", sv)
                tfs = (tpf0, tpf1)
                for dc in (range(ND) if AB not in ("1", "2") else []):
                    if dc >= 2:
                        em.wait("t", ss)
                    em.op("t", lambda t, d=dc, p=tfs[dc % 2]: t.transpose(
                        p[:, 0:128], XN32[:, d * 128:(d + 1) * 128],
                        idf[:, :]), st)
                    em.wait("s", st)
                    em.op("s", lambda s, d=dc, p=tfs[dc % 2]: s.copy(
                        XNF3[:, d * 128:(d + 1) * 128], p[:, 0:128]), ss)
                env["fence_s"](em)
                em.wait("t", ss)
                pgs = (pgp0, pgp1)
                for h in (range(2) if AB not in ("1", "2") else []):
                    for dc in range(ND):
                        em.op("t", lambda t, hh=h, d=dc, p=pgs[h]: t.matmul(
                            p[:, :],
                            XNF3[:, d * 128:(d + 1) * 128],
                            WB1[:, d * D + hh * 512:d * D + (hh + 1) * 512],
                            start=(d == 0), stop=(d == ND - 1)),
                            st if dc == ND - 1 else None)
                    em.wait("s", st)
                    em.op("s", lambda s, hh=h, p=pgs[h]: s.copy(
                        Pg[:, hh * 512:(hh + 1) * 512], p[:, :]), ss)
                if AB in ("1", "2"):
                    em.op("v", lambda v: v.tensor_copy(
                        s_row[:, D:2 * D], XN32[:, :]), sv)
                em.wait("s", ss)
                if AB not in ("1", "2"):
                    em.op("s", lambda s: s.activation(
                        s_row[:, 0:D], Pg[:, :], AF.Square), ss)
                em.wait("v", ss)
                if AB not in ("1", "2"):
                    em.op("v", lambda v: v.tensor_scalar(
                        s_row[:, 0:D], s_row[:, 0:D], GC1 * GC2, GC1,
                        op0=ALU.mult, op1=ALU.add), sv)
                    em.op("v", lambda v: v.tensor_tensor(
                        s_row[:, D:2 * D], s_row[:, 0:D], Pg[:, :], op=ALU.mult), sv)
                em.wait("s", sv)
                if AB not in ("1", "2"):
                    em.op("s", lambda s: s.activation(
                        s_row[:, 0:D], s_row[:, D:2 * D], AF.Tanh), ss)
                em.wait("v", ss)
                if AB not in ("1", "2"):
                    em.op("v", lambda v: v.scalar_tensor_tensor(
                        s_row[:, D:2 * D], s_row[:, 0:D], 1.0, Pg[:, :],
                        op0=ALU.add, op1=ALU.mult), sv)
                    em.op("v", lambda v: v.tensor_scalar(
                        s_row[:, D:2 * D], s_row[:, D:2 * D], 0.5, None,
                        op0=ALU.mult), sv)
                    em.op("v", lambda v: v.tensor_tensor(
                        Pg[:, :], s_row[:, D:2 * D], XN32[:, :], op=ALU.subtract), sv)
                    em.op("v", lambda v, cc=c: v.tensor_scalar(
                        Pg[:, :], Pg[:, :], MASK[:, cc:cc + 1], None,
                        op0=ALU.mult), sv)
                    em.op("v", lambda v: v.tensor_tensor(
                        s_row[:, D:2 * D], XN32[:, :], Pg[:, :], op=ALU.add), sv)
                em.wait("s", sv)
                em.op("s", lambda s: s.activation(
                    s_row[:, 0:D], s_row[:, D:2 * D], AF.Tanh,
                    scale=f32(0.5)), ss)
                em.wait("v", ss)
                em.op("v", lambda v: v.tensor_scalar(
                    s_row[:, 0:D], s_row[:, 0:D], 0.5, 0.5,
                    op0=ALU.mult, op1=ALU.add), sv)
                em.op("v", lambda v: v.tensor_tensor(
                    s_row[:, 0:D], s_row[:, 0:D], s_row[:, D:2 * D],
                    op=ALU.mult), sv)
                em.op("v", lambda v: v.tensor_tensor(
                    SC[:, :], SC[:, :], s_row[:, 0:D], op=ALU.mult), sv)
                em.op("v", lambda v, cc=c: v.tensor_tensor(
                    X[:, cc * D:(cc + 1) * D], X[:, cc * D:(cc + 1) * D],
                    SC[:, :], op=ALU.add), sv)
                with nc.Block() as blk:
                    em.flush(blk)


def build_l1_part1(env):
    nc = env["nc"]
    A, B, X, WB1, STAT, GATES, BB, bqp, ROW, ones1, idf, idb = (
        env["A"], env["B"], env["X"], env["WB1"], env["STAT"], env["GATES"],
        env["BB"], env["bqp"], env["ROW"], env["ones1"], env["idf"], env["idb"])
    W = env["W"]
    sd, sd2, sdo, sv, ss, st, sg = (env["sd"], env["sd2"], env["sdo"],
                                    env["sv"], env["ss"], env["st"], env["sg"])

    with ExitStack() as _es5:
        ent = _es5.enter_context
        Kp = ent(nc.sbuf_tensor("Kp", [128, 2304], BF16))
        Vt = ent(nc.sbuf_tensor("Vt", [128, 2304], BF16))
        WKB = ent(nc.sbuf_tensor("WKB", [128, D], BF16))
        WVB = ent(nc.sbuf_tensor("WVB", [128, D], BF16))
        MRH = ent(nc.sbuf_tensor("MRH", [128, 64], BF16))
        MRL = ent(nc.sbuf_tensor("MRL", [128, 64], BF16))
        MRS = ent(nc.sbuf_tensor("MRS", [128, 64], F32))
        bqp1 = ent(nc.sbuf_tensor("bqp1", [128, 8], F32))
        bkvp = ent(nc.sbuf_tensor("bkvp", [128, 2], F32))
        em = mk_em(env)
        load_weight_dxe(env, em,
                        lambda dc: W["wq1"][dc * 128:(dc + 1) * 128, :], D)
        em.op("y", lambda y: y.dma_start(out=bqp1[:, :], in_=W["bq1"]), sd, 16)
        em.op("y", lambda y: y.dma_start(out=bkvp[:, 0:1], in_=W["bk1"]),
              sd, 16)
        em.op("y", lambda y: y.dma_start(out=bkvp[:, 1:2], in_=W["bv1"]),
              sd, 16)
        em.wait("g", sv)
        em.wait("g", ss)
        em.op("g", lambda g: g.dma_start(
            out=env["STG"][0][:, :], in_=W["wk1"]), sd2, 16)
        em.wait("v", sd2)
        em.op("v", lambda v: v.tensor_copy(WKB[:, :], env["STG"][0][:, :]), sv)
        em.op("g", lambda g: g.dma_start(
            out=env["STG"][1][:, :], in_=W["wv1"]), sd2, 16)
        em.wait("v", sd2)
        em.op("v", lambda v: v.tensor_copy(WVB[:, :], env["STG"][1][:, :]), sv)
        em.op("g", lambda g: g.dma_start(out=MRS[:, :], in_=W["mr1"]), sd2, 16)
        em.wait("v", sd2)
        em.op("v", lambda v: v.tensor_copy(MRH[:, :], MRS[:, :]), sv)
        em.wait("v", sv)
        em.op("v", lambda v: v.tensor_copy(env["STG"][0][:, 0:64], MRH[:, :]), sv)
        em.wait("v", sv)
        em.op("v", lambda v: v.tensor_tensor(
            MRS[:, :], MRS[:, :], env["STG"][0][:, 0:64], op=ALU.subtract), sv)
        em.wait("v", sv)
        em.op("v", lambda v: v.tensor_copy(MRL[:, :], MRS[:, :]), sv)
        em.op("g", lambda g: g.memset(Kp[:, 0:128], 0.0))
        em.op("g", lambda g: g.memset(Kp[:, 2176:2304], 0.0))
        em.op("g", lambda g: g.memset(Vt[:, 0:128], 0.0))
        em.op("g", lambda g: g.memset(Vt[:, 2176:2304], 0.0))
        em.wait("y", sd)
        em.wait("g", sd2)
        with nc.Block() as blk:
            em.flush(blk)

        with ExitStack() as _es6:
            ent = _es6.enter_context
            XN32 = ent(nc.sbuf_tensor("XN32L1", [128, D], F32))
            XNB = ent(nc.sbuf_tensor("XNB", [128, D], BF16))
            HI32 = ent(nc.sbuf_tensor("HI32", [128, D], F32))
            LOB = ent(nc.sbuf_tensor("LOB", [128, D], BF16))
            LOF = ent(nc.sbuf_tensor("LOF", [128, D], BF16))
            VFC = ent(nc.sbuf_tensor("VFC", [128, 128], BF16))
            GS = ent(nc.sbuf_tensor("GS", [128, 48], F32))
            txp0 = ent(nc.psum_tensor("txp0", [128, 128], BF16))
            txp1 = ent(nc.psum_tensor("txp1", [128, 128], BF16))
            lgp = ent(nc.psum_tensor("lgp", [128, 8], F32))
            qfp0 = ent(nc.psum_tensor("qfp0", [128, 128], F32))
            qfp1 = ent(nc.psum_tensor("qfp1", [128, 128], F32))
            kvp0 = ent(nc.psum_tensor("kvp0", [128, 128], F32))
            kvp1 = ent(nc.psum_tensor("kvp1", [128, 128], F32))
            vtp = ent(nc.psum_tensor("vtp", [128, 128], BF16))
            for c in range(NT):
                em = mk_em(env)
                emit_ln_stats(env, em, c, XN32, HI32)
                em.op("v", lambda v: v.tensor_copy(XNB[:, :], XN32[:, :]), sv)
                em.wait("v", sv)
                em.op("v", lambda v: v.tensor_copy(HI32[:, :], XNB[:, :]), sv)
                em.wait("v", sv)
                em.op("v", lambda v: v.tensor_tensor(
                    XN32[:, :], XN32[:, :], HI32[:, :], op=ALU.subtract), sv)
                em.wait("v", sv)
                em.op("v", lambda v: v.tensor_copy(LOB[:, :], XN32[:, :]), sv)
                em.op("y", lambda y, cc=c: y.dma_start(
                    out=W["x1d"].rearrange("(n p) m -> p n m", p=128)[:, cc, :],
                    in_=X[:, cc * D:(cc + 1) * D]), sdo, 16)
                em.wait("t", sv)
                txs = (txp0, txp1)
                for dc in range(ND):
                    if dc >= 2:
                        em.wait("t", ss)
                    em.op("t", lambda t, d=dc, p=txs[dc % 2]: t.transpose(
                        p[:, :], XNB[:, d * 128:(d + 1) * 128], idb[:, :]), st)
                    em.wait("s", st)
                    em.op("s", lambda s, d=dc, cc=c, p=txs[dc % 2]: s.copy(
                        A[:, d * T + cc * 128:d * T + (cc + 1) * 128], p[:, :]),
                        ss)
                for dc in range(ND):
                    em.wait("t", ss)
                    em.op("t", lambda t, d=dc, p=txs[dc % 2]: t.transpose(
                        p[:, :], LOB[:, d * 128:(d + 1) * 128], idb[:, :]), st)
                    em.wait("s", st)
                    em.op("s", lambda s, d=dc, p=txs[dc % 2]: s.copy(
                        LOF[:, d * 128:(d + 1) * 128], p[:, :]), ss)
                env["fence_s"](em)
                em.wait("t", ss)
                for pi in range(3):
                    for dc in range(ND):
                        if pi == 0:
                            lh, rh = (
                                A[:, dc * T + c * 128:dc * T + (c + 1) * 128],
                                MRH[:, dc * 8:(dc + 1) * 8])
                        elif pi == 1:
                            lh, rh = (LOF[:, dc * 128:(dc + 1) * 128],
                                      MRH[:, dc * 8:(dc + 1) * 8])
                        else:
                            lh, rh = (
                                A[:, dc * T + c * 128:dc * T + (c + 1) * 128],
                                MRL[:, dc * 8:(dc + 1) * 8])
                        em.op("t", lambda t, l=lh, r=rh, pp=pi, d=dc: t.matmul(
                            lgp[:, :], l, r,
                            start=(pp == 0 and d == 0),
                            stop=(pp == 2 and d == ND - 1)),
                            st if (pi == 2 and dc == ND - 1) else None)
                em.wait("v", st)
                em.op("v", lambda v: v.tensor_copy(GS[:, 0:8], lgp[:, :]), sv)
                em.wait("v", sv)
                em.op("v", lambda v: v.reduce_max(GS[:, 8:9], GS[:, 0:8],
                                                  axis=AX.X), sv)
                em.wait("v", sv)
                em.op("v", lambda v: v.tensor_scalar(
                    GS[:, 16:24], GS[:, 0:8], GS[:, 8:9], None, op0=ALU.is_ge),
                    sv)
                em.wait("v", sv)
                em.op("v", lambda v: v.scalar_tensor_tensor(
                    GS[:, 24:32], GS[:, 16:24], NEG, GS[:, 0:8],
                    op0=ALU.mult, op1=ALU.add), sv)
                em.wait("v", sv)
                em.op("v", lambda v: v.reduce_max(GS[:, 9:10], GS[:, 24:32],
                                                  axis=AX.X), sv)
                em.wait("v", sv)
                em.op("v", lambda v: v.tensor_scalar(
                    GS[:, 10:11], GS[:, 8:9], -1.0, None, op0=ALU.mult), sv)
                em.wait("s", sv)
                em.op("s", lambda s: s.activation(
                    GS[:, 32:40], GS[:, 0:8], AF.Exp, bias=GS[:, 10:11]), ss)
                em.op("s", lambda s: s.activation(
                    GS[:, 11:12], GS[:, 9:10], AF.Exp, bias=GS[:, 10:11]), ss)
                em.wait("v", ss)
                em.op("v", lambda v: v.tensor_scalar(
                    GS[:, 12:13], GS[:, 11:12], 1.0, None, op0=ALU.add), sv)
                em.wait("v", sv)
                em.op("v", lambda v: v.reciprocal(GS[:, 13:14], GS[:, 12:13]), sv)
                em.wait("v", sv)
                em.op("v", lambda v: v.tensor_scalar(
                    GS[:, 40:48], GS[:, 0:8], GS[:, 9:10], None, op0=ALU.is_ge),
                    sv)
                em.wait("v", sv)
                em.op("v", lambda v: v.tensor_tensor(
                    GS[:, 40:48], GS[:, 40:48], GS[:, 32:40], op=ALU.mult), sv)
                em.wait("v", sv)
                em.op("v", lambda v, cc=c: v.tensor_scalar(
                    GATES[:, cc * 8:(cc + 1) * 8], GS[:, 40:48], GS[:, 13:14],
                    None, op0=ALU.mult), sv)
                qfs = (qfp0, qfp1)
                for i in range(ND):
                    ps = qfs[i % 2]
                    if i >= 2:
                        em.wait("t", sv)
                    for dc in range(ND):
                        em.op("t", lambda t, ii=i, d=dc, cc=c, p=ps: t.matmul(
                            p[:, :],
                            WB1[:, d * D + ii * 128:d * D + (ii + 1) * 128],
                            A[:, d * T + cc * 128:d * T + (cc + 1) * 128],
                            start=(d == 0), stop=(d == ND - 1)),
                            st if dc == ND - 1 else None)
                    em.wait("v", st)
                    em.op("v", lambda v, ii=i, cc=c, p=ps: v.tensor_scalar(
                        B[:, ii * T + cc * 128:ii * T + (cc + 1) * 128],
                        p[:, :], bqp1[:, ii:ii + 1], None, op0=ALU.add), sv)
                kvs = (kvp0, kvp1)
                for dc in range(ND):
                    em.op("t", lambda t, d=dc, cc=c: t.matmul(
                        kvs[0][:, :], WKB[:, d * 128:(d + 1) * 128],
                        A[:, d * T + cc * 128:d * T + (cc + 1) * 128],
                        start=(d == 0), stop=(d == ND - 1)),
                        st if dc == ND - 1 else None)
                em.wait("v", st)
                em.op("v", lambda v, cc=c: v.tensor_scalar(
                    Kp[:, 128 + cc * 128:128 + (cc + 1) * 128], kvs[0][:, :],
                    bkvp[:, 0:1], None, op0=ALU.add), sv)
                for dc in range(ND):
                    em.op("t", lambda t, d=dc, cc=c: t.matmul(
                        kvs[1][:, :], WVB[:, d * 128:(d + 1) * 128],
                        A[:, d * T + cc * 128:d * T + (cc + 1) * 128],
                        start=(d == 0), stop=(d == ND - 1)),
                        st if dc == ND - 1 else None)
                em.wait("v", st)
                em.op("v", lambda v: v.tensor_scalar(
                    VFC[:, :], kvs[1][:, :], bkvp[:, 1:2], None, op0=ALU.add),
                    sv)
                em.op("v", lambda v: v.tensor_copy(STAT[:, 61:62],
                                                   STAT[:, 60:61]), sv)
                em.wait("t", sv)
                em.op("t", lambda t: t.transpose(vtp[:, :], VFC[:, :],
                                                 idb[:, :]), st)
                em.wait("v", st)
                em.op("v", lambda v, cc=c: v.tensor_copy(
                    Vt[:, 128 + cc * 128:128 + (cc + 1) * 128], vtp[:, :]), sv)
                with nc.Block() as blk:
                    em.flush(blk)

        if os.environ.get("MK_DBG2"):
            qdbg = nc.dram_tensor("qdbg", [128, NT * D], F32,
                                  kind="ExternalOutput").ap()
            kdbg = nc.dram_tensor("kdbg", [128, 2304], F32,
                                  kind="ExternalOutput").ap()
            vdbg = nc.dram_tensor("vdbg", [128, 2304], F32,
                                  kind="ExternalOutput").ap()
            adbg = nc.dram_tensor("adbg", [128, NT * D], F32,
                                  kind="ExternalOutput").ap()
            wdbg = nc.dram_tensor("wdbg", [128, 2 * D], F32,
                                  kind="ExternalOutput").ap()
            em = mk_em(env)
            em.op("v", lambda v: v.tensor_copy(env["STG"][0][:, :],
                                               WKB[:, :]), sv)
            em.op("v", lambda v: v.tensor_copy(env["STG"][1][:, :],
                                               WB1[:, 0:D]), sv)
            em.wait("y", sv)
            em.op("y", lambda y: y.dma_start(out=wdbg[:, 0:D],
                                             in_=env["STG"][0][:, :]),
                  env["sdo"], 16)
            em.op("y", lambda y: y.dma_start(out=wdbg[:, D:2 * D],
                                             in_=env["STG"][1][:, :]),
                  env["sdo"], 16)
            em.wait("y", env["sdo"])
            with nc.Block() as blk:
                em.flush(blk)
            em = mk_em(env)
            for seg in range(NT):
                em.op("v", lambda v, s2=seg: v.tensor_copy(
                    env["STG"][0][:, :], A[:, s2 * D:(s2 + 1) * D]), sv)
                em.wait("y", sv)
                em.op("y", lambda y, s2=seg: y.dma_start(
                    out=adbg[:, s2 * D:(s2 + 1) * D],
                    in_=env["STG"][0][:, :]), env["sdo"], 16)
                em.wait("v", env["sdo"])
            with nc.Block() as blk:
                em.flush(blk)
            em = mk_em(env)
            for seg in range(NT):
                em.op("v", lambda v, s2=seg: v.tensor_copy(
                    env["STG"][0][:, :], B[:, s2 * D:(s2 + 1) * D]), sv)
                em.wait("y", sv)
                em.op("y", lambda y, s2=seg: y.dma_start(
                    out=qdbg[:, s2 * D:(s2 + 1) * D],
                    in_=env["STG"][0][:, :]), env["sdo"], 16)
                em.wait("v", env["sdo"])
            for seg in range(3):
                w = 768
                em.op("v", lambda v, s2=seg, w2=w: v.tensor_copy(
                    env["STG"][0][:, 0:w2], Kp[:, s2 * w2:(s2 + 1) * w2]), sv)
                em.wait("y", sv)
                em.op("y", lambda y, s2=seg, w2=w: y.dma_start(
                    out=kdbg[:, s2 * w2:(s2 + 1) * w2],
                    in_=env["STG"][0][:, 0:w2]), env["sdo"], 16)
                em.wait("v", env["sdo"])
                em.op("v", lambda v, s2=seg, w2=w: v.tensor_copy(
                    env["STG"][0][:, 0:w2], Vt[:, s2 * w2:(s2 + 1) * w2]), sv)
                em.wait("y", sv)
                em.op("y", lambda y, s2=seg, w2=w: y.dma_start(
                    out=vdbg[:, s2 * w2:(s2 + 1) * w2],
                    in_=env["STG"][0][:, 0:w2]), env["sdo"], 16)
                em.wait("v", env["sdo"])
            with nc.Block() as blk:
                em.flush(blk)

        em = mk_em(env)
        load_weight_dxe(env, em,
                        lambda dc: W["wo1"][dc * 128:(dc + 1) * 128, :], D)
        em.op("y", lambda y: y.dma_start(out=ROW[:, :], in_=W["bo1"].rearrange("(a d) -> a d", a=1)),
              sd, 16)
        em.wait("y", sd)
        with nc.Block() as blk:
            em.flush(blk)
        with nc.psum_tensor("bc_psL1", [128, 512], F32) as bc_ps:
            em = mk_em(env)
            for sl in range(2):
                em.op("t", lambda t, s=sl: t.matmul(
                    bc_ps[:, :], ones1[:, :], ROW[:, s * 512:(s + 1) * 512],
                    start=True, stop=True), st)
                em.wait("v", st)
                em.op("v", lambda v, s=sl: v.tensor_copy(
                    BB[:, s * 512:(s + 1) * 512], bc_ps[:, :]), sv)
                em.wait("t", sv)
            with nc.Block() as blk:
                em.flush(blk)

        with ExitStack() as _est:
            ent = _est.enter_context
            BAND = ent(nc.sbuf_tensor("BAND", [128, 1024], F32))
            S5 = ent(nc.sbuf_tensor("S5", [128, 1024], F32))
            E5 = ent(nc.sbuf_tensor("E5", [128, 1024], F32))
            Pb5 = ent(nc.sbuf_tensor("Pb5", [128, 1024], BF16))
            PT5 = ent(nc.sbuf_tensor("PT5", [128, 1024], BF16))
            OC = ent(nc.sbuf_tensor("OC", [128, 2 * D], BF16))
            S1C = ent(nc.sbuf_tensor("S1C", [128, 2 * D], F32))
            scp0 = ent(nc.psum_tensor("scp0L1", [128, 512], F32))
            scp1 = ent(nc.psum_tensor("scp1L1", [128, 512], F32))
            ptp0 = ent(nc.psum_tensor("ptp0L1", [128, 512], BF16))
            ptp1 = ent(nc.psum_tensor("ptp1L1", [128, 512], BF16))
            pvp0 = ent(nc.psum_tensor("pvp0L1", [64, 128], F32))
            pvp1 = ent(nc.psum_tensor("pvp1L1", [64, 128], F32))
            s1p0 = ent(nc.psum_tensor("s1p0L1", [128, 512], F32))
            s1p1 = ent(nc.psum_tensor("s1p1L1", [128, 512], F32))

            em = mk_em(env)
            for r in range(2):
                em.op("g", lambda g, rr=r: g.memset(
                    BAND[:, rr * 512:(rr + 1) * 512], 0.0))
                em.op("g", lambda g, rr=r: g.affine_select(
                    out=BAND[:, rr * 512:(rr + 1) * 512],
                    in_=BAND[:, rr * 512:(rr + 1) * 512],
                    compare_op=ALU.is_ge, fill=NEG,
                    base=-128 * rr, pattern=[[1, 512]], channel_multiplier=-1))
                em.op("g", lambda g, rr=r: g.affine_select(
                    out=BAND[:, rr * 512:(rr + 1) * 512],
                    in_=BAND[:, rr * 512:(rr + 1) * 512],
                    compare_op=ALU.is_ge, fill=NEG,
                    base=128 * rr + 255, pattern=[[-1, 512]],
                    channel_multiplier=1))
            with nc.Block() as blk:
                em.flush(blk)

            em = mk_em(env)
            scs = (scp0, scp1)
            pts = (ptp0, ptp1)
            pvs = (pvp0, pvp1)
            maskmax_done = {0: 0, 1: 0}
            exp_done = {0: 0, 1: 0}
            pb_done = {0: 0, 1: 0}
            ptc_done = {0: 0, 1: 0}
            pv_done = {0: 0, 1: 0}
            ocevac_done = {0: 0, 1: 0}
            s1evac_done = {0: 0, 1: 0}
            spill_done = {0: 0, 1: 0}
            NH = NT * 16

            def emit_scores(k):
                c2 = k // 16
                h2 = k % 16
                par2 = k % 2
                g2 = h2 // 8
                i2 = h2 % 8
                n2 = c2 // 2
                if maskmax_done[par2]:
                    em.wait("t", sv, maskmax_done[par2])
                em.op("t", lambda t, gg=g2, ii=i2, p=scs[par2], cc=c2, nn=n2:
                      t.matmul(
                    p[:, :],
                    B[64 * gg:64 * (gg + 1),
                      ii * T + cc * 128:ii * T + (cc + 1) * 128],
                    Kp[64 * gg:64 * (gg + 1), 256 * nn:256 * nn + 512],
                    start=True, stop=True, tile_position=(64 * gg, 0)), st)
                scores_done[par2] = st.n

            scores_done = {0: 0, 1: 0}
            pbg_done = {0: 0, 1: 0}
            emit_scores(0)
            for k in range(NH):
                c = k // 16
                h = k % 16
                par = k % 2
                g = h // 8
                i = h % 8
                n = c // 2
                r = c % 2
                cpar = c % 2
                my_scores = scores_done[par]
                if k + 1 < NH:
                    emit_scores(k + 1)
                em.wait("v", st, my_scores)
                if exp_done[par]:
                    em.wait("v", ss, exp_done[par])
                em.op("v", lambda v, p=scs[par], rr=r, q=par:
                      v.tensor_tensor(
                    S5[:, q * 512:(q + 1) * 512], p[:, :],
                    BAND[:, rr * 512:(rr + 1) * 512], op=ALU.add), sv)
                em.wait("v", sv)
                em.op("v", lambda v, q=par: v.reduce_max(
                    STAT[:, 40 + q:41 + q], S5[:, q * 512:(q + 1) * 512],
                    axis=AX.X, negate=True), sv)
                em._fence("v")
                maskmax_done[par] = sv.n
                em.wait("s", sv, maskmax_done[par])
                em.op("s", lambda s, q=par: s.activation(
                    E5[:, q * 512:(q + 1) * 512],
                    S5[:, q * 512:(q + 1) * 512], AF.Exp,
                    bias=STAT[:, 40 + q:41 + q],
                    accum_out=STAT[:, 42 + q:43 + q]), ss)
                em._fence("s")
                exp_done[par] = ss.n
                em.wait("v", ss, exp_done[par])
                em.op("v", lambda v, q=par: v.reciprocal(
                    STAT[:, 44 + q:45 + q], STAT[:, 42 + q:43 + q]), sv)
                em.wait("v", sv)
                if pbg_done[par]:
                    em.wait("v", st, pbg_done[par])
                em.op("v", lambda v, q=par: v.tensor_scalar(
                    Pb5[:, q * 512:(q + 1) * 512],
                    E5[:, q * 512:(q + 1) * 512],
                    STAT[:, 44 + q:45 + q], None, op0=ALU.mult), sv)
                em._fence("v")
                pb_done[par] = sv.n
                em.wait("t", sv, pb_done[par])
                if ptc_done[par]:
                    em.wait("t", ss, ptc_done[par])
                for j in range(4):
                    em.op("t", lambda t, jj=j, q=par, p=pts[par]:
                          t.transpose(
                        p[:, jj * 128:(jj + 1) * 128],
                        Pb5[:, q * 512 + jj * 128:q * 512 + (jj + 1) * 128],
                        idb[:, :]), st if j == 3 else None)
                pbg_done[par] = st.n
                em.wait("s", st)
                em.op("s", lambda s, q=par, p=pts[par]: s.copy(
                    PT5[:, q * 512:(q + 1) * 512], p[:, :]), ss)
                em._fence("s")
                ptc_done[par] = ss.n
                em.wait("t", ss, ptc_done[par])
                if ocevac_done[par]:
                    em.wait("t", ss, ocevac_done[par])
                for j in range(4):
                    em.op("t", lambda t, jj=j, gg=g, q=par, nn=n,
                          p=pvs[par]: t.matmul(
                        p[:, :],
                        Vt[:, (2 * nn + jj) * 128 + 64 * gg:
                           (2 * nn + jj) * 128 + 64 * (gg + 1)],
                        PT5[:, q * 512 + jj * 128:q * 512 + (jj + 1) * 128],
                        start=(jj == 0), stop=(jj == 3)),
                        st if j == 3 else None)
                pv_done[par] = st.n
                em.wait("s", st, pv_done[par])
                em.op("s", lambda s, gg=g, ii=i, q=cpar, p=pvs[par]:
                      s.copy(
                    OC[64 * gg:64 * (gg + 1),
                       q * D + ii * 128:q * D + (ii + 1) * 128],
                    p[:, :]), ss)
                em._fence("s")
                ocevac_done[par] = ss.n

                if h == 15:
                    oc_ready = max(ocevac_done[0], ocevac_done[1])
                    s1s = (s1p0, s1p1)
                    for hh in range(2):
                        em.wait("t", ss, oc_ready)
                        if s1evac_done[hh]:
                            em.wait("t", sv, s1evac_done[hh])
                        for i2 in range(ND):
                            em.op("t", lambda t, h2=hh, ii=i2, q=cpar:
                                  t.matmul(
                                s1s[h2][:, :],
                                OC[:, q * D + ii * 128:q * D + (ii + 1) * 128],
                                WB1[:, ii * D + h2 * 512:ii * D + (h2 + 1) * 512],
                                start=(ii == 0), stop=(ii == ND - 1)),
                                st if i2 == ND - 1 else None)
                        s1mm = st.n
                        em.wait("v", st, s1mm)
                        if spill_done[cpar]:
                            em.wait("v", sdo, spill_done[cpar])
                        em.op("v", lambda v, h2=hh, q=cpar: v.tensor_tensor(
                            S1C[:, q * D + h2 * 512:q * D + (h2 + 1) * 512],
                            s1s[h2][:, :],
                            BB[:, h2 * 512:(h2 + 1) * 512], op=ALU.add), sv)
                        em._fence("v")
                        s1evac_done[hh] = sv.n
                    em.wait("y", sv, max(s1evac_done[0], s1evac_done[1]))
                    em.op("y", lambda y, cc=c, q=cpar: y.dma_start(
                        out=W["s1d"].rearrange("(n p) m -> p n m", p=128)
                        [:, cc, :],
                        in_=S1C[:, q * D:(q + 1) * D]), sdo, 16)
                    spill_done[cpar] = sdo.n
            with nc.Block() as blk:
                em.flush(blk)

def build_l1_moe(env):
    nc = env["nc"]
    A, WB1, STAT, GATES = env["A"], env["WB1"], env["STAT"], env["GATES"]
    W = env["W"]
    sd, sd2, sd3, sd4, sdo, sv, ss, st, sg = (
        env["sd"], env["sd2"], env["sd3"], env["sd4"], env["sdo"],
        env["sv"], env["ss"], env["st"], env["sg"])
    out = env["out"]
    stg = env["STG"]
    NE = 8
    NM = 16
    TT = 2

    gdbg = None
    if os.environ.get("MK_DBG3"):
        gdbg = nc.dram_tensor("gdbg", [T, D], F32, kind="ExternalOutput").ap()

    with ExitStack() as _esm:
        ent = _esm.enter_context
        H = ent(nc.sbuf_tensor("H", [128, NM * 1024], BF16))
        OUTA = ent(nc.sbuf_tensor("OUTA", [128, 8 * D], F32))
        W13G0 = ent(nc.sbuf_tensor("W13G0", [128, D], BF16))
        W13G1 = ent(nc.sbuf_tensor("W13G1", [128, D], BF16))
        W13U0 = ent(nc.sbuf_tensor("W13U0", [128, D], BF16))
        W13U1 = ent(nc.sbuf_tensor("W13U1", [128, D], BF16))
        W2B0 = ent(nc.sbuf_tensor("W2B0", [128, 4 * D], BF16))
        W2B1 = ent(nc.sbuf_tensor("W2B1", [128, 4 * D], BF16))
        GSIL = ent(nc.sbuf_tensor("GSIL", [128, D], F32))
        XC = ent(nc.sbuf_tensor("XC", [128, D], F32))
        S1R = ent(nc.sbuf_tensor("S1R", [128, D], F32))
        STG2 = ent(nc.sbuf_tensor("STG2", [128, D], F32))
        STG3 = ent(nc.sbuf_tensor("STG3", [128, D], F32))
        YST0 = ent(nc.sbuf_tensor("YST0", [128, D], F32))
        YST1 = ent(nc.sbuf_tensor("YST1", [128, D], F32))
        gp0 = ent(nc.psum_tensor("gp0", [128, 512], F32))
        gp1 = ent(nc.psum_tensor("gp1", [128, 512], F32))
        up0 = ent(nc.psum_tensor("up0", [128, 512], F32))
        up1 = ent(nc.psum_tensor("up1", [128, 512], F32))
        yp0 = ent(nc.psum_tensor("yp0", [128, 512], F32))
        yp1 = ent(nc.psum_tensor("yp1", [128, 512], F32))

        w13g = (W13G0, W13G1)
        w13u = (W13U0, W13U1)
        w2b = (W2B0, W2B1)
        ysts = (YST0, YST1)

        for tt in range(TT):
            em = mk_em(env)
            em.op("g", lambda g: g.memset(OUTA[:, :], 0.0))
            with nc.Block() as blk:
                em.flush(blk)

            em = mk_em(env)
            nexp = int(os.environ.get("MK_NEXP", NE))

            def h_dma(e, m):
                par = m % 2
                sgg = (stg[0], STG2)[par]
                sgu = (stg[1], STG3)[par]
                em.wait("y", sv)
                em.wait("g", sv)
                em.op("y", lambda y, ee=e, mm=m, b=sgg: y.dma_start(
                    out=b[:, :], in_=W["w13"][ee, mm]), sd, 16)
                em.op("g", lambda g, ee=e, mm=m, b=sgu: g.dma_start(
                    out=b[:, :], in_=W["w13"][ee, 16 + mm]), sd2, 16)

            def y_dma(e, j):
                b = ysts[j % 2]
                em.wait("y" if j % 2 == 0 else "g", ss)
                qe = "y" if j % 2 == 0 else "g"
                qs = sd if j % 2 == 0 else sd2
                em.op(qe, lambda q, ee=e, mm=j, bb=b: q.dma_start(
                    out=bb[:, :], in_=W["w2"][ee, mm * 128:(mm + 1) * 128, :]),
                    qs, 16)

            def y_cast(e, j):
                b = ysts[j % 2]
                dst = w2b[(j // 4) % 2]
                em.wait("s", st)
                em.wait("s", sd if j % 2 == 0 else sd2)
                em.op("s", lambda s, bb=b, dd=dst, k=j % 4: s.copy(
                    dd[:, k * D:(k + 1) * D], bb[:, :]), ss)

            hready = {}
            ycast_ready = {}
            evacdone = {0: 0, 1: 0}
            ttready = 0
            ssready = 0

            def h_cast2(e, m):
                par = m % 2
                sgg = (stg[0], STG2)[par]
                sgu = (stg[1], STG3)[par]
                em.wait("v", st)
                em.wait("v", sd)
                em.op("v", lambda v, p=par, b=sgg: v.tensor_copy(
                    w13g[p][:, :], b[:, :]), sv)
                em.wait("v", sd2)
                em.op("v", lambda v, p=par, b=sgu: v.tensor_copy(
                    w13u[p][:, :], b[:, :]), sv)
                em._fence("v")
                hready[(e, m)] = sv.n

            for e in range(nexp):
                if e == 0:
                    h_dma(e, 0)
                    h_cast2(e, 0)
                for m in range(NM):
                    if m + 1 < NM:
                        h_dma(e, m + 1)
                        h_cast2(e, m + 1)
                    elif e + 1 < nexp:
                        h_dma(e + 1, 0)
                        h_cast2(e + 1, 0)
                    if m >= 8:
                        j = m - 8
                        y_dma(e, j)
                        y_cast(e, j)
                        if j % 4 == 3:
                            em._fence("s")
                            ycast_ready[(e, j // 4)] = ss.n
                    em.wait("t", sv, max(hready[(e, m)], ttready))
                    if ssready:
                        em.wait("t", ss, ssready)
                    for th in range(2):
                        gps = (gp0, gp1)[th]
                        ups = (up0, up1)[th]
                        for dc in range(ND):
                            em.op("t", lambda t, d=dc, p=m % 2, h2=th, q=gps:
                                  t.matmul(
                                q[:, :], w13g[p][:, d * 128:(d + 1) * 128],
                                A[:, d * T + tt * 1024 + h2 * 512:
                                  d * T + tt * 1024 + (h2 + 1) * 512],
                                start=(d == 0), stop=(d == ND - 1)),
                                st if dc == ND - 1 else None)
                        for dc in range(ND):
                            em.op("t", lambda t, d=dc, p=m % 2, h2=th, q=ups:
                                  t.matmul(
                                q[:, :], w13u[p][:, d * 128:(d + 1) * 128],
                                A[:, d * T + tt * 1024 + h2 * 512:
                                  d * T + tt * 1024 + (h2 + 1) * 512],
                                start=(d == 0), stop=(d == ND - 1)),
                                st if dc == ND - 1 else None)
                    em.wait("s", st)
                    for th in range(2):
                        em.op("s", lambda s, h2=th, q=(gp0, gp1)[th]:
                              s.activation(
                            GSIL[:, h2 * 512:(h2 + 1) * 512], q[:, :],
                            AF.Silu), ss)
                    em._fence("s")
                    ssready = ss.n
                    em.wait("v", ss, ssready)
                    for th in range(2):
                        em.op("v", lambda v, mm=m, h2=th, q=(up0, up1)[th]:
                              v.tensor_tensor(
                            H[:, mm * 1024 + h2 * 512:mm * 1024 + (h2 + 1) * 512],
                            GSIL[:, h2 * 512:(h2 + 1) * 512], q[:, :],
                            op=ALU.mult), sv)
                    em._fence("v")
                    ttready = sv.n

                for s4 in range(4):
                    for tc in range(8):
                        cidx = tt * 8 + tc
                        for eb in range(2):
                            par = (tc * 2 + eb) % 2
                            yps = (yp0, yp1)[par]
                            w = evacdone[par]
                            if s4 == 0 and tc == 0 and eb == 0:
                                w = max(w, ttready)
                            if w:
                                em.wait("t", sv, w)
                            if tc == 0 and eb == 0:
                                em.wait("t", ss, ycast_ready[(e, s4)])
                            for k in range(4):
                                m = s4 * 4 + k
                                em.op("t", lambda t, mm=m, k2=k, t2=tc, b=eb,
                                      p=yps, dd=w2b[s4 % 2]: t.matmul(
                                    p[:, :],
                                    H[:, mm * 1024 + t2 * 128:
                                      mm * 1024 + (t2 + 1) * 128],
                                    dd[:, k2 * D + b * 512:
                                       k2 * D + (b + 1) * 512],
                                    start=(k2 == 0), stop=(k2 == 3)),
                                    st if k == 3 else None)
                            em.wait("v", st)
                            em.op("v", lambda v, t2=tc, b=eb, ci=cidx, ee=e,
                                  p=yps: v.scalar_tensor_tensor(
                                OUTA[:, t2 * D + b * 512:t2 * D + (b + 1) * 512],
                                p[:, :], GATES[:, ci * 8 + ee:ci * 8 + ee + 1],
                                OUTA[:, t2 * D + b * 512:t2 * D + (b + 1) * 512],
                                op0=ALU.mult, op1=ALU.add), sv)
                            em._fence("v")
                            evacdone[par] = sv.n
                        if 1 <= s4 <= 2 and tc < 4:
                            j = (s4 + 1) * 4 + tc
                            y_dma(e, j)
                            y_cast(e, j)
                            if j % 4 == 3:
                                em._fence("s")
                                ycast_ready[(e, s4 + 1)] = ss.n
            with nc.Block() as blk:
                em.flush(blk)

            for tc in range(8):
                cidx = tt * 8 + tc
                em = mk_em(env)
                em.wait("s", sdo)
                em.wait("y", sv)
                em.wait("g", sv)
                em.op("y", lambda y, ci=cidx: y.dma_start(
                    out=XC[:, :],
                    in_=W["x1d"].rearrange("(n p) m -> p n m", p=128)[:, ci, :]),
                    sd, 16)
                em.op("g", lambda g, ci=cidx: g.dma_start(
                    out=S1R[:, :],
                    in_=W["s1d"].rearrange("(n p) m -> p n m", p=128)[:, ci, :]),
                    sd2, 16)
                em.op("s", lambda s, t2=tc: s.activation(
                    GSIL[:, :], OUTA[:, t2 * D:(t2 + 1) * D], AF.Silu), ss)
                env["fence_s"](em)
                if gdbg is not None:
                    em.wait("g", ss)
                    em.op("g", lambda g, ci=cidx: g.dma_start(
                        out=gdbg.rearrange("(n p) m -> p n m", p=128)[:, ci, :],
                        in_=GSIL[:, :]), sdo, 16)
                    em.wait("g", sdo)
                    em.wait("v", sdo)
                em.wait("v", ss)
                em.wait("v", sd)
                em.wait("v", sd2)
                em.op("v", lambda v: v.tensor_tensor(
                    GSIL[:, :], GSIL[:, :], S1R[:, :], op=ALU.mult), sv)
                em.wait("v", sv)
                em.op("v", lambda v: v.tensor_tensor(
                    GSIL[:, :], GSIL[:, :], XC[:, :], op=ALU.add), sv)
                env["fence_v"](em)
                em.wait("y", sv)
                em.op("y", lambda y, ci=cidx: y.dma_start(
                    out=out.rearrange("(n p) m -> p n m", p=128)[:, ci, :],
                    in_=GSIL[:, :]), sdo, 16)
                with nc.Block() as blk:
                    em.flush(blk)


def kernel(**inputs):
    n = 8
    x = np.ascontiguousarray(np.asarray(inputs["L0_attn_inputs"],
                                        dtype=np.float32))
    wmap = {
        "wq0": "L0_attn_Wq", "bq0": "L0_attn_bq", "wr0": "L0_ffn_w_router",
        "wb0": "L0_ffn_w_block", "wq1": "L1_attn_Wq", "bq1": "L1_attn_bq",
        "wk1": "L1_attn_Wk", "bk1": "L1_attn_bk", "wv1": "L1_attn_Wv",
        "bv1": "L1_attn_bv", "wo1": "L1_attn_Wo", "bo1": "L1_attn_bo",
        "mr1": "L1_ffn_moe_router", "w13": "L1_ffn_moe_w13",
        "w2": "L1_ffn_moe_w2",
    }
    ws = {k: np.ascontiguousarray(np.asarray(inputs[v], dtype=np.float32))
          for k, v in wmap.items()}
    idx = (np.arange(2)[None, :, None] * 512 + np.arange(8)[:, None, None] * 64
           + np.arange(64)[None, None, :]).reshape(-1)
    ws["bq0t"] = np.ascontiguousarray(ws["bq0"].reshape(8, 128).T)
    ws["wq1"] = np.ascontiguousarray(ws["wq1"][:, idx])
    ws["bq1"] = np.ascontiguousarray(
        ws["bq1"][idx].reshape(8, 2, 64).transpose(1, 2, 0).reshape(128, 8))
    ws["wo1"] = np.ascontiguousarray(ws["wo1"][idx, :])
    ws["wk1"] = np.ascontiguousarray(
        ws["wk1"].reshape(8, 128, 128).transpose(1, 0, 2).reshape(128, 1024))
    ws["wv1"] = np.ascontiguousarray(
        ws["wv1"].reshape(8, 128, 128).transpose(1, 0, 2).reshape(128, 1024))
    ws["mr1"] = np.ascontiguousarray(
        ws["mr1"].reshape(8, 128, 8).transpose(1, 0, 2).reshape(128, 64))
    ws["bk1"] = np.ascontiguousarray(ws["bk1"].reshape(128, 1))
    ws["bv1"] = np.ascontiguousarray(ws["bv1"].reshape(128, 1))
    ws["w13"] = np.ascontiguousarray(
        ws["w13"].reshape(8, 8, 128, 32, 128).transpose(0, 3, 2, 1, 4)
        .reshape(8, 32, 128, 1024))
    nc = build(do_l0=True, do_l1=True)
    in_maps = []
    for c in range(n):
        m = {"x": np.ascontiguousarray(x[c])}
        m.update(ws)
        in_maps.append(m)
    res = run_bass_kernel_spmd(nc, in_maps, core_ids=list(range(n)))
    outs = [np.asarray(res.results[c]["out"]) for c in range(n)]
    return np.stack(outs, axis=0)


# version 09
# speedup vs baseline: 1.6471x; 1.1677x over previous
import os
import numpy as np
from contextlib import ExitStack
import concourse.bass as bass
import concourse.mybir as mybir
from concourse.bass_utils import run_bass_kernel_spmd

F32 = mybir.dt.float32
BF16 = mybir.dt.bfloat16
AF = mybir.ActivationFunctionType
ALU = mybir.AluOpType
AX = mybir.AxisListType

T = 2048
D = 1024
NT = 16
ND = 8
KSEL = 1536
GC1 = 0.7978845608028654
GC2 = 0.044715
EPS = 1e-5
NEG = -1e30


def f32(v):
    return np.float32(v)


class Sem:
    def __init__(self, h, is_dma=False):
        self.h = h
        self.n = 0
        self.is_dma = is_dma


class Em:
    ENGS = ("t", "v", "s", "g", "y")

    def __init__(self, nc):
        self.nc = nc
        self.L = {e: [] for e in self.ENGS}
        self._stat = None
        self._engsem = {}

    def set_fences(self, stat, sems_by_eng):
        self._stat = stat
        self._engsem = sems_by_eng

    def _fence(self, peng):
        stat = self._stat
        sem = self._engsem[peng]
        if peng == "v":
            self.op("v", lambda v: v.tensor_copy(stat[:, 61:62],
                                                 stat[:, 60:61]), sem)
        elif peng == "s":
            self.op("s", lambda s: s.copy(stat[:, 62:63],
                                          stat[:, 60:61]), sem)
        elif peng == "g":
            self.op("g", lambda g: g.tensor_copy(stat[:, 63:64],
                                                 stat[:, 60:61]), sem)

    def op(self, eng, fn, sem=None, k=1):
        if sem is not None:
            sem.n += k
            kk = k
            hh = sem.h
            self.L[eng].append(lambda e: fn(e).then_inc(hh, kk))
        else:
            self.L[eng].append(fn)

    DMA_SETTLE_CYC = 800

    def wait(self, eng, sem, val=None):
        if val is None and self._stat is not None:
            for peng, psem in self._engsem.items():
                if sem is psem and eng != peng:
                    self._fence(peng)
                    break
        v = sem.n if val is None else val
        hh = sem.h
        self.L[eng].append(lambda e: e.wait_ge(hh, v))
        if sem.is_dma:
            cyc = self.DMA_SETTLE_CYC
            self.L[eng].append(lambda e: e.nop(cycle_cnt=cyc))

    def flush(self, blk):
        if self._stat is not None:
            for peng in ("v", "s", "g"):
                if self.L[peng]:
                    self._fence(peng)
        names = {"t": "tensor", "v": "vector", "s": "scalar",
                 "g": "gpsimd", "y": "sync"}
        for e in self.ENGS:
            lst = self.L[e]
            if not lst:
                continue

            def body(eng, lst=lst):
                for f in lst:
                    f(eng)
            getattr(blk, names[e])(body)
        self.L = {e: [] for e in self.ENGS}


def mk_em(env):
    em = Em(env["nc"])
    em.set_fences(env["STAT"], {"v": env["sv"], "s": env["ss"],
                                "g": env["sg"]})
    return em


def build(do_l0=True, do_l1=True):
    nc = bass.Bass("TRN2", target_bir_lowering=False)

    xin = nc.dram_tensor("x", [T, D], F32, kind="ExternalInput").ap()
    out = nc.dram_tensor("out", [T, D], F32, kind="ExternalOutput").ap()
    W = {}
    if do_l0:
        W["wq0"] = nc.dram_tensor("wq0", [D, D], F32, kind="ExternalInput").ap()
        W["bq0"] = nc.dram_tensor("bq0", [D], F32, kind="ExternalInput").ap()
        if not do_l1:
            W["bq0t"] = nc.dram_tensor("bq0t", [128, 8], F32,
                                       kind="ExternalInput").ap()
        W["wr0"] = nc.dram_tensor("wr0", [D], F32, kind="ExternalInput").ap()
        W["wb0"] = nc.dram_tensor("wb0", [D, D], F32, kind="ExternalInput").ap()
    if do_l1:
        W["wq1"] = nc.dram_tensor("wq1", [D, D], F32, kind="ExternalInput").ap()
        W["bq1"] = nc.dram_tensor("bq1", [128, 8], F32, kind="ExternalInput").ap()
        W["wk1"] = nc.dram_tensor("wk1", [128, D], F32, kind="ExternalInput").ap()
        W["bk1"] = nc.dram_tensor("bk1", [128, 1], F32, kind="ExternalInput").ap()
        W["wv1"] = nc.dram_tensor("wv1", [128, D], F32, kind="ExternalInput").ap()
        W["bv1"] = nc.dram_tensor("bv1", [128, 1], F32, kind="ExternalInput").ap()
        W["wo1"] = nc.dram_tensor("wo1", [D, D], F32, kind="ExternalInput").ap()
        W["bo1"] = nc.dram_tensor("bo1", [D], F32, kind="ExternalInput").ap()
        W["mr1"] = nc.dram_tensor("mr1", [128, 64], F32, kind="ExternalInput").ap()
        W["bq0t"] = nc.dram_tensor("bq0t", [128, 8], F32, kind="ExternalInput").ap()
        W["w13"] = nc.dram_tensor("w13", [8, 32, 128, 1024], F32,
                                  kind="ExternalInput").ap()
        W["w2"] = nc.dram_tensor("w2", [8, T, D], F32, kind="ExternalInput").ap()
        dbg = "ExternalOutput" if os.environ.get("MK_DBG") else "Internal"
        W["x1d"] = nc.dram_tensor("x1d", [T, D], F32, kind=dbg).ap()
        W["s1d"] = nc.dram_tensor("s1d", [T, D], F32, kind=dbg).ap()

    with ExitStack() as _es1:
        ent = _es1.enter_context
        A = ent(nc.sbuf_tensor("A", [128, NT * D], BF16))
        WB1 = ent(nc.sbuf_tensor("WB1", [128, ND * D], BF16))
        STG0 = ent(nc.sbuf_tensor("STG0", [128, D], F32))
        STG1 = ent(nc.sbuf_tensor("STG1", [128, D], F32))
        idb = ent(nc.sbuf_tensor("idb", [128, 128], BF16))
        idf = ent(nc.sbuf_tensor("idf", [128, 128], F32))
        STAT = ent(nc.sbuf_tensor("STAT", [128, 64], F32))
        GATES = ent(nc.sbuf_tensor("GATES", [128, 128], F32))
        MASK = ent(nc.sbuf_tensor("MASK", [128, 16], F32))
        BB = ent(nc.sbuf_tensor("BB", [128, D], F32))
        bqp = ent(nc.sbuf_tensor("bqp", [128, 8], F32))
        ROW = ent(nc.sbuf_tensor("ROW", [1, D], F32))
        ones1 = ent(nc.sbuf_tensor("ones1", [1, 128], F32))
        _sd = ent(nc.semaphore("sd"))
        _sd2 = ent(nc.semaphore("sd2"))
        _sdo = ent(nc.semaphore("sdo"))
        _sv = ent(nc.semaphore("sv"))
        _ss = ent(nc.semaphore("ss"))
        _st = ent(nc.semaphore("st"))
        _sg = ent(nc.semaphore("sg"))
        _sd3 = ent(nc.semaphore("sd3"))
        _sd4 = ent(nc.semaphore("sd4"))
        env = dict(nc=nc, A=A, WB1=WB1, STG=(STG0, STG1), idb=idb, idf=idf,
                   STAT=STAT, GATES=GATES, MASK=MASK, BB=BB, bqp=bqp,
                   ROW=ROW, ones1=ones1, W=W, xin=xin, out=out,
                   sd=Sem(_sd, True), sd2=Sem(_sd2, True), sdo=Sem(_sdo, True),
                   sv=Sem(_sv), ss=Sem(_ss), st=Sem(_st), sg=Sem(_sg),
                   sd3=Sem(_sd3, True), sd4=Sem(_sd4, True))

        def fence_v(em):
            em.op("v", lambda v: v.tensor_copy(STAT[:, 61:62],
                                               STAT[:, 60:61]), env["sv"])

        def fence_s(em):
            em.op("s", lambda s: s.copy(STAT[:, 62:63], STAT[:, 60:61]),
                  env["ss"])
        env["fence_v"] = fence_v
        env["fence_s"] = fence_s

        em = mk_em(env)
        em.op("g", lambda g: g.memset(idb[:, :], 0.0))
        em.op("g", lambda g: g.affine_select(
            out=idb[:, :], in_=idb[:, :], compare_op=ALU.not_equal, fill=1.0,
            base=0, pattern=[[-1, 128]], channel_multiplier=1))
        em.op("g", lambda g: g.memset(idf[:, :], 0.0))
        em.op("g", lambda g: g.affine_select(
            out=idf[:, :], in_=idf[:, :], compare_op=ALU.not_equal, fill=1.0,
            base=0, pattern=[[-1, 128]], channel_multiplier=1))
        em.op("g", lambda g: g.memset(ones1[:, :], 1.0))
        with nc.Block(no_gpsimd_drain=True) as blk:
            em.flush(blk)

        with (
            nc.sbuf_tensor("X", [128, NT * D], F32) as X,
            nc.sbuf_tensor("B", [128, NT * D], BF16) as B,
        ):
            env["X"] = X
            env["B"] = B
            em = mk_em(env)
            em.op("y", lambda y: y.dma_start(
                out=X[:, :].rearrange("p (n m) -> p n m", n=NT),
                in_=xin.rearrange("(n p) m -> p n m", p=128)), env["sd"], 16)
            em.wait("y", env["sd"])
            with nc.Block(no_gpsimd_drain=True) as blk:
                em.flush(blk)

            if do_l0:
                build_l0(env)
            if do_l1:
                build_l1_part1(env)
            else:
                em = mk_em(env)
                em.op("y", lambda y: y.dma_start(
                    out=out.rearrange("(n p) m -> p n m", p=128),
                    in_=X[:, :].rearrange("p (n m) -> p n m", n=NT)),
                    env["sdo"], 16)
                with nc.Block(no_gpsimd_drain=True) as blk:
                    em.flush(blk)

        if do_l1:
            build_l1_moe(env)

        em = mk_em(env)
        em.wait("y", env["sdo"])
        with nc.Block(no_gpsimd_drain=True) as blk:
            em.flush(blk)

    return nc


def load_weight_dxe(env, em, src_ap, ncols, oview=None):
    nc = env["nc"]
    WB1, (STG0, STG1) = env["WB1"], env["STG"]
    sd, sv, ss = env["sd"], env["sv"], env["ss"]
    for dc in range(ND):
        stg = (STG0, STG1)[dc % 2]
        if dc >= 2:
            em.wait("y", sv if dc % 2 == 0 else ss)
        ov = (lambda a: a) if oview is None else oview
        em.op("y", lambda y, s=src_ap(dc), g=stg, nco=ncols, o=ov:
              y.dma_start(out=o(g[:, 0:nco]), in_=s), sd, 16)
        em.wait("v" if dc % 2 == 0 else "s", sd)
        eng = "v" if dc % 2 == 0 else "s"
        if eng == "v":
            em.op("v", lambda v, g=stg, dcc=dc, nco=ncols:
                  v.tensor_copy(WB1[:, dcc * nco:(dcc + 1) * nco], g[:, 0:nco]), sv)
        else:
            em.op("s", lambda s, g=stg, dcc=dc, nco=ncols:
                  s.copy(WB1[:, dcc * nco:(dcc + 1) * nco], g[:, 0:nco]), ss)


def emit_ln_stats(env, em, c, xn32, sq):
    X, STAT = env["X"], env["STAT"]
    sv, ss = env["sv"], env["ss"]
    xc = X[:, c * D:(c + 1) * D]
    em.op("v", lambda v: v.reduce_sum(STAT[:, 48:49], xc, axis=AX.X), sv)
    em.wait("v", sv)
    em.op("v", lambda v: v.tensor_scalar(
        STAT[:, c:c + 1], STAT[:, 48:49], 1.0 / D, None, op0=ALU.mult), sv)
    em.wait("v", sv)
    em.op("v", lambda v: v.tensor_scalar(
        xn32[:, :], xc, STAT[:, c:c + 1], None, op0=ALU.subtract), sv)
    em.wait("s", sv)
    em.op("s", lambda s: s.activation(sq[:, 0:D], xn32[:, :], AF.Square,
                                      accum_out=STAT[:, 50:51]), ss)
    em.wait("v", ss)
    em.op("v", lambda v: v.tensor_scalar(
        STAT[:, 51:52], STAT[:, 50:51], 1.0 / D, None, op0=ALU.mult), sv)
    em.wait("s", sv)
    em.op("s", lambda s: s.activation(STAT[:, 52:53], STAT[:, 51:52], AF.Ln,
                                      bias=f32(EPS)), ss)
    em.wait("s", ss)
    em.op("s", lambda s: s.activation(STAT[:, 16 + c:17 + c], STAT[:, 52:53],
                                      AF.Exp, scale=f32(-0.5)), ss)
    em.wait("v", ss)
    em.op("v", lambda v: v.tensor_scalar(
        xn32[:, :], xn32[:, :], STAT[:, 16 + c:17 + c], None, op0=ALU.mult), sv)
    em.wait("v", sv)


def build_l0(env):
    nc = env["nc"]
    A, B, X, WB1, STAT, MASK, BB, bqp, ROW, ones1, idf, idb = (
        env["A"], env["B"], env["X"], env["WB1"], env["STAT"], env["MASK"],
        env["BB"], env["bqp"], env["ROW"], env["ones1"], env["idf"], env["idb"])
    W = env["W"]
    sd, sv, ss, st, sg = env["sd"], env["sv"], env["ss"], env["st"], env["sg"]

    with ExitStack() as _es2:
        ent = _es2.enter_context
        srow = ent(nc.sbuf_tensor("srow", [1, T], F32))
        s_row = ent(nc.sbuf_tensor("s_row", [128, T], F32))
        XN32 = ent(nc.sbuf_tensor("XN32", [128, D], F32))
        SC = ent(nc.sbuf_tensor("SC", [128, D], F32))
        wrb = ent(nc.sbuf_tensor("wrb", [128, D], F32))
        em = mk_em(env)
        load_weight_dxe(env, em, lambda dc: W["wq0"][dc * 128:(dc + 1) * 128, :], D)
        em.op("y", lambda y: y.dma_start(out=bqp[:, :], in_=W["bq0t"]),
              sd, 16)
        em.op("y", lambda y: y.dma_start(out=ROW[:, :], in_=W["bq0"].rearrange("(a d) -> a d", a=1)),
              sd, 16)
        em.wait("y", sd)
        with nc.Block(no_gpsimd_drain=True) as blk:
            em.flush(blk)

        with nc.psum_tensor("bc_ps", [128, 512], F32) as bc_ps:
            em = mk_em(env)
            for sl in range(2):
                em.op("t", lambda t, s=sl: t.matmul(
                    bc_ps[:, :], ones1[:, :], ROW[:, s * 512:(s + 1) * 512],
                    start=True, stop=True), st)
                em.wait("v", st)
                em.op("v", lambda v, s=sl: v.tensor_copy(
                    BB[:, s * 512:(s + 1) * 512], bc_ps[:, :]), sv)
                em.wait("t", sv)
            with nc.Block(no_gpsimd_drain=True) as blk:
                em.flush(blk)

            em = mk_em(env)
            em.op("y", lambda y: y.dma_start(out=ROW[:, :], in_=W["wr0"].rearrange("(a d) -> a d", a=1)),
                  sd, 16)
            em.wait("t", sd)
            for sl in range(2):
                em.op("t", lambda t, s=sl: t.matmul(
                    bc_ps[:, :], ones1[:, :], ROW[:, s * 512:(s + 1) * 512],
                    start=True, stop=True), st)
                em.wait("v", st)
                em.op("v", lambda v, s=sl: v.tensor_copy(
                    wrb[:, s * 512:(s + 1) * 512], bc_ps[:, :]), sv)
                em.wait("t", sv)
            with nc.Block(no_gpsimd_drain=True) as blk:
                em.flush(blk)

        with ExitStack() as _es3:
            ent = _es3.enter_context
            XNF = ent(nc.sbuf_tensor("XNF", [128, D], BF16))
            tp0 = ent(nc.psum_tensor("tp0", [128, 128], F32))
            tp1 = ent(nc.psum_tensor("tp1", [128, 128], F32))
            qf0 = ent(nc.psum_tensor("qf0", [128, 128], F32))
            qf1 = ent(nc.psum_tensor("qf1", [128, 128], F32))
            qt0 = ent(nc.psum_tensor("qt0", [128, 512], F32))
            qt1 = ent(nc.psum_tensor("qt1", [128, 512], F32))
            for c in range(NT):
                em = mk_em(env)
                emit_ln_stats(env, em, c, XN32, s_row)
                em.op("v", lambda v: v.tensor_tensor(
                    s_row[:, 0:D], XN32[:, :], wrb[:, :], op=ALU.mult), sv)
                em.op("v", lambda v, cc=c: v.reduce_sum(
                    STAT[:, 32 + cc:33 + cc], s_row[:, 0:D], axis=AX.X), sv)
                em.wait("t", sv)
                tps = (tp0, tp1)
                for dc in range(ND):
                    if dc >= 2:
                        em.wait("t", sv)
                    em.op("t", lambda t, d=dc, p=tps[dc % 2]: t.transpose(
                        p[:, :], XN32[:, d * 128:(d + 1) * 128], idf[:, :]), st)
                    em.wait("v", st)
                    em.op("v", lambda v, d=dc, p=tps[dc % 2]: v.tensor_copy(
                        XNF[:, d * 128:(d + 1) * 128], p[:, :]), sv)
                env["fence_v"](em)
                em.wait("t", sv)
                qfs = (qf0, qf1)
                for i in range(ND):
                    ps = qfs[i % 2]
                    if i >= 2:
                        em.wait("t", sv, sv.n)
                    for dc in range(ND):
                        em.op("t", lambda t, ii=i, d=dc, p=ps: t.matmul(
                            p[:, :],
                            WB1[:, d * D + ii * 128:d * D + (ii + 1) * 128],
                            XNF[:, d * 128:(d + 1) * 128],
                            start=(d == 0), stop=(d == ND - 1)),
                            st if dc == ND - 1 else None)
                    em.wait("v", st)
                    em.op("v", lambda v, ii=i, cc=c, p=ps: v.tensor_scalar(
                        A[:, ii * T + cc * 128:ii * T + (cc + 1) * 128],
                        p[:, :], bqp[:, ii:ii + 1], None, op0=ALU.add), sv)
                qts = (qt0, qt1)
                for h in range(2):
                    ps = qts[h]
                    for dc in range(ND):
                        em.op("t", lambda t, hh=h, d=dc, p=ps: t.matmul(
                            p[:, :],
                            XNF[:, d * 128:(d + 1) * 128],
                            WB1[:, d * D + hh * 512:d * D + (hh + 1) * 512],
                            start=(d == 0), stop=(d == ND - 1)),
                            st if dc == ND - 1 else None)
                    em.wait("v", st)
                    em.op("v", lambda v, hh=h, cc=c, p=ps: v.tensor_tensor(
                        B[:, cc * D + hh * 512:cc * D + (hh + 1) * 512],
                        p[:, :], BB[:, hh * 512:(hh + 1) * 512], op=ALU.add), sv)
                with nc.Block(no_gpsimd_drain=True) as blk:
                    em.flush(blk)

        with (
            nc.psum_tensor("trp", [1, 128], F32) as trp,
            nc.psum_tensor("bcp", [128, 512], F32) as bcp,
        ):
            em = mk_em(env)
            for c in range(NT):
                em.op("t", lambda t, cc=c: t.transpose(
                    trp[:, :], STAT[:, 32 + cc:33 + cc], idf[:, :]), st)
                em.wait("v", st)
                em.op("v", lambda v, cc=c: v.tensor_copy(
                    srow[:, cc * 128:(cc + 1) * 128], trp[:, :]), sv)
                em.wait("t", sv)
            with nc.Block(no_gpsimd_drain=True) as blk:
                em.flush(blk)
            em = mk_em(env)
            for sl in range(4):
                em.op("t", lambda t, s=sl: t.matmul(
                    bcp[:, :], ones1[:, :], srow[:, s * 512:(s + 1) * 512],
                    start=True, stop=True), st)
                em.wait("v", st)
                em.op("v", lambda v, s=sl: v.tensor_copy(
                    s_row[:, s * 512:(s + 1) * 512], bcp[:, :]), sv)
                em.wait("t", sv)
            with nc.Block(no_gpsimd_drain=True) as blk:
                em.flush(blk)
            em = mk_em(env)
            for c in range(NT):
                for sl in range(4):
                    em.op("v", lambda v, cc=c, s=sl: v.tensor_scalar(
                        SC[:, 0:512], s_row[:, s * 512:(s + 1) * 512],
                        STAT[:, 32 + cc:33 + cc], 0.0, op0=ALU.is_gt,
                        op1=ALU.add, accum_out=STAT[:, 53 + s:54 + s]), sv)
                    em.wait("v", sv)
                em.op("v", lambda v: v.tensor_tensor(
                    STAT[:, 53:54], STAT[:, 53:54], STAT[:, 54:55], op=ALU.add), sv)
                em.wait("v", sv)
                em.op("v", lambda v: v.tensor_tensor(
                    STAT[:, 55:56], STAT[:, 55:56], STAT[:, 56:57], op=ALU.add), sv)
                em.wait("v", sv)
                em.op("v", lambda v: v.tensor_tensor(
                    STAT[:, 53:54], STAT[:, 53:54], STAT[:, 55:56], op=ALU.add), sv)
                em.wait("v", sv)
                em.op("v", lambda v, cc=c: v.tensor_scalar(
                    MASK[:, cc:cc + 1], STAT[:, 53:54], float(KSEL), None,
                    op0=ALU.is_lt), sv)
                em.wait("v", sv)
            with nc.Block(no_gpsimd_drain=True) as blk:
                em.flush(blk)

        if os.environ.get("MK_DBG6"):
            qdbg6 = nc.dram_tensor("qdbg6", [128, NT * D], F32,
                                   kind="ExternalOutput").ap()
            bdbg6 = nc.dram_tensor("bdbg6", [128, NT * D], F32,
                                   kind="ExternalOutput").ap()
            wdbg6 = nc.dram_tensor("wdbg6", [128, ND * D], F32,
                                   kind="ExternalOutput").ap()
            em = mk_em(env)
            for seg in range(ND):
                em.op("v", lambda v, s2=seg: v.tensor_copy(
                    XN32[:, :], WB1[:, s2 * D:(s2 + 1) * D]), sv)
                em.wait("y", sv)
                em.op("y", lambda y, s2=seg: y.dma_start(
                    out=wdbg6[:, s2 * D:(s2 + 1) * D], in_=XN32[:, :]),
                    env["sdo"], 16)
                em.wait("v", env["sdo"])
            for seg in range(NT):
                em.op("v", lambda v, s2=seg: v.tensor_copy(
                    XN32[:, :], A[:, s2 * D:(s2 + 1) * D]), sv)
                em.wait("y", sv)
                em.op("y", lambda y, s2=seg: y.dma_start(
                    out=qdbg6[:, s2 * D:(s2 + 1) * D], in_=XN32[:, :]),
                    env["sdo"], 16)
                em.wait("v", env["sdo"])
                em.op("v", lambda v, s2=seg: v.tensor_copy(
                    XN32[:, :], B[:, s2 * D:(s2 + 1) * D]), sv)
                em.wait("y", sv)
                em.op("y", lambda y, s2=seg: y.dma_start(
                    out=bdbg6[:, s2 * D:(s2 + 1) * D], in_=XN32[:, :]),
                    env["sdo"], 16)
                em.wait("v", env["sdo"])
            with nc.Block(no_gpsimd_drain=True) as blk:
                em.flush(blk)

        em = mk_em(env)
        load_weight_dxe(env, em, lambda dc: W["wb0"][dc * 128:(dc + 1) * 128, :], D)
        em.wait("y", env["sd"])
        with nc.Block(no_gpsimd_drain=True) as blk:
            em.flush(blk)

        with ExitStack() as _es4:
            ent = _es4.enter_context
            Pb = ent(nc.sbuf_tensor("Pb", [128, T], BF16))
            PT = ent(nc.sbuf_tensor("PT", [128, 512], BF16))
            Pg = ent(nc.sbuf_tensor("Pg", [128, D], F32))
            XNF3 = ent(nc.sbuf_tensor("XNF3", [128, D], BF16))
            scp0 = ent(nc.psum_tensor("scp0", [128, 512], F32))
            scp1 = ent(nc.psum_tensor("scp1", [128, 512], F32))
            ptp0 = ent(nc.psum_tensor("ptp0", [128, 128], BF16))
            ptp1 = ent(nc.psum_tensor("ptp1", [128, 128], BF16))
            pvp0 = ent(nc.psum_tensor("pvp0", [128, 512], F32))
            pvp1 = ent(nc.psum_tensor("pvp1", [128, 512], F32))
            tpf0 = ent(nc.psum_tensor("tpf0", [128, 128], F32))
            tpf1 = ent(nc.psum_tensor("tpf1", [128, 128], F32))
            pgp0, pgp1 = scp0, scp1
            for c in range(NT):
                em = mk_em(env)
                scs = (scp0, scp1)
                AB2 = os.environ.get("MK_AB", "") == "2"
                for sl in (range(4) if not AB2 else []):
                    ps = scs[sl % 2]
                    if sl >= 2:
                        em.wait("t", sv)
                    for i in range(ND):
                        em.op("t", lambda t, s=sl, ii=i, p=ps: t.matmul(
                            p[:, :],
                            A[:, ii * T + c * 128:ii * T + (c + 1) * 128],
                            A[:, ii * T + s * 512:ii * T + (s + 1) * 512],
                            start=(ii == 0), stop=(ii == ND - 1)),
                            st if i == ND - 1 else None)
                    em.wait("v", st)
                    em.op("v", lambda v, s=sl, p=ps: v.tensor_copy(
                        s_row[:, s * 512:(s + 1) * 512], p[:, :]), sv)
                if AB2:
                    em.op("v", lambda v: v.memset(SC[:, :], 1.0), sv)
                em.wait("v", sv)
                if not AB2:
                    em.op("v", lambda v: v.reduce_max(
                        STAT[:, 48:49], s_row[:, :], axis=AX.X, negate=True), sv)
                    em.wait("v", sv)
                    em.op("v", lambda v: v.tensor_scalar(
                        STAT[:, 49:50], STAT[:, 48:49], 1.0 / 32.0, None,
                        op0=ALU.mult), sv)
                    em.wait("s", sv)
                    em.op("s", lambda s: s.activation(
                        Pb[:, :], s_row[:, :], AF.Exp, bias=STAT[:, 49:50],
                        scale=f32(1.0 / 32.0), accum_out=STAT[:, 50:51]), ss)
                    em.wait("v", ss)
                    em.op("v", lambda v: v.reciprocal(STAT[:, 51:52], STAT[:, 50:51]),
                          sv)
                    pts = (ptp0, ptp1)
                    env["fence_s"](em)
                    em.wait("t", ss)
                    for jb in range(4):
                        for jo in range(4):
                            j = jb * 4 + jo
                            if j >= 2:
                                em.wait("t", ss)
                            em.op("t", lambda t, jj=j, p=pts[j % 2]: t.transpose(
                                p[:, :], Pb[:, jj * 128:(jj + 1) * 128],
                                idb[:, :]), st)
                            em.wait("s", st)
                            em.op("s", lambda s, jj=j, o=jo, p=pts[j % 2]: s.copy(
                                PT[:, o * 128:(o + 1) * 128], p[:, :]), ss)
                        env["fence_s"](em)
                        em.wait("t", ss)
                        for jo in range(4):
                            j = jb * 4 + jo
                            for h in range(2):
                                em.op("t", lambda t, jj=j, o=jo, hh=h,
                                      p=(pvp0, pvp1)[h]: t.matmul(
                                    p[:, :],
                                    PT[:, o * 128:(o + 1) * 128],
                                    B[:, jj * D + hh * 512:jj * D + (hh + 1) * 512],
                                    start=(jj == 0), stop=(jj == NT - 1)),
                                    st if (j == NT - 1 and h == 1) else None)
                    em.wait("v", st)
                    em.wait("v", sv, sv.n)
                    for h in range(2):
                        em.op("v", lambda v, hh=h, p=(pvp0, pvp1)[h]: v.tensor_scalar(
                            SC[:, hh * 512:(hh + 1) * 512], p[:, :],
                            STAT[:, 51:52], None, op0=ALU.mult), sv)
                em.op("v", lambda v: v.tensor_scalar(
                    XN32[:, :], X[:, c * D:(c + 1) * D], STAT[:, c:c + 1], None,
                    op0=ALU.subtract), sv)
                em.wait("v", sv)
                em.op("v", lambda v: v.tensor_scalar(
                    XN32[:, :], XN32[:, :], STAT[:, 16 + c:17 + c], None,
                    op0=ALU.mult), sv)
                AB = os.environ.get("MK_AB", "")
                em.wait("t", sv)
                tfs = (tpf0, tpf1)
                for dc in (range(ND) if AB not in ("1", "2") else []):
                    if dc >= 2:
                        em.wait("t", ss)
                    em.op("t", lambda t, d=dc, p=tfs[dc % 2]: t.transpose(
                        p[:, 0:128], XN32[:, d * 128:(d + 1) * 128],
                        idf[:, :]), st)
                    em.wait("s", st)
                    em.op("s", lambda s, d=dc, p=tfs[dc % 2]: s.copy(
                        XNF3[:, d * 128:(d + 1) * 128], p[:, 0:128]), ss)
                env["fence_s"](em)
                em.wait("t", ss)
                pgs = (pgp0, pgp1)
                for h in (range(2) if AB not in ("1", "2") else []):
                    for dc in range(ND):
                        em.op("t", lambda t, hh=h, d=dc, p=pgs[h]: t.matmul(
                            p[:, :],
                            XNF3[:, d * 128:(d + 1) * 128],
                            WB1[:, d * D + hh * 512:d * D + (hh + 1) * 512],
                            start=(d == 0), stop=(d == ND - 1)),
                            st if dc == ND - 1 else None)
                    em.wait("s", st)
                    em.op("s", lambda s, hh=h, p=pgs[h]: s.copy(
                        Pg[:, hh * 512:(hh + 1) * 512], p[:, :]), ss)
                if AB in ("1", "2"):
                    em.op("v", lambda v: v.tensor_copy(
                        s_row[:, D:2 * D], XN32[:, :]), sv)
                em.wait("s", ss)
                if AB not in ("1", "2"):
                    em.op("s", lambda s: s.activation(
                        s_row[:, 0:D], Pg[:, :], AF.Square), ss)
                em.wait("v", ss)
                if AB not in ("1", "2"):
                    em.op("v", lambda v: v.tensor_scalar(
                        s_row[:, 0:D], s_row[:, 0:D], GC1 * GC2, GC1,
                        op0=ALU.mult, op1=ALU.add), sv)
                    em.op("v", lambda v: v.tensor_tensor(
                        s_row[:, D:2 * D], s_row[:, 0:D], Pg[:, :], op=ALU.mult), sv)
                em.wait("s", sv)
                if AB not in ("1", "2"):
                    em.op("s", lambda s: s.activation(
                        s_row[:, 0:D], s_row[:, D:2 * D], AF.Tanh), ss)
                em.wait("v", ss)
                if AB not in ("1", "2"):
                    em.op("v", lambda v: v.scalar_tensor_tensor(
                        s_row[:, D:2 * D], s_row[:, 0:D], 1.0, Pg[:, :],
                        op0=ALU.add, op1=ALU.mult), sv)
                    em.op("v", lambda v: v.tensor_scalar(
                        s_row[:, D:2 * D], s_row[:, D:2 * D], 0.5, None,
                        op0=ALU.mult), sv)
                    em.op("v", lambda v: v.tensor_tensor(
                        Pg[:, :], s_row[:, D:2 * D], XN32[:, :], op=ALU.subtract), sv)
                    em.op("v", lambda v, cc=c: v.tensor_scalar(
                        Pg[:, :], Pg[:, :], MASK[:, cc:cc + 1], None,
                        op0=ALU.mult), sv)
                    em.op("v", lambda v: v.tensor_tensor(
                        s_row[:, D:2 * D], XN32[:, :], Pg[:, :], op=ALU.add), sv)
                em.wait("s", sv)
                em.op("s", lambda s: s.activation(
                    s_row[:, 0:D], s_row[:, D:2 * D], AF.Tanh,
                    scale=f32(0.5)), ss)
                em.wait("v", ss)
                em.op("v", lambda v: v.tensor_scalar(
                    s_row[:, 0:D], s_row[:, 0:D], 0.5, 0.5,
                    op0=ALU.mult, op1=ALU.add), sv)
                em.op("v", lambda v: v.tensor_tensor(
                    s_row[:, 0:D], s_row[:, 0:D], s_row[:, D:2 * D],
                    op=ALU.mult), sv)
                em.op("v", lambda v: v.tensor_tensor(
                    SC[:, :], SC[:, :], s_row[:, 0:D], op=ALU.mult), sv)
                em.op("v", lambda v, cc=c: v.tensor_tensor(
                    X[:, cc * D:(cc + 1) * D], X[:, cc * D:(cc + 1) * D],
                    SC[:, :], op=ALU.add), sv)
                with nc.Block(no_gpsimd_drain=True) as blk:
                    em.flush(blk)


def build_l1_part1(env):
    nc = env["nc"]
    A, B, X, WB1, STAT, GATES, BB, bqp, ROW, ones1, idf, idb = (
        env["A"], env["B"], env["X"], env["WB1"], env["STAT"], env["GATES"],
        env["BB"], env["bqp"], env["ROW"], env["ones1"], env["idf"], env["idb"])
    W = env["W"]
    sd, sd2, sdo, sv, ss, st, sg = (env["sd"], env["sd2"], env["sdo"],
                                    env["sv"], env["ss"], env["st"], env["sg"])

    with ExitStack() as _es5:
        ent = _es5.enter_context
        Kp = ent(nc.sbuf_tensor("Kp", [128, 2304], BF16))
        Vt = ent(nc.sbuf_tensor("Vt", [128, 2304], BF16))
        WKB = ent(nc.sbuf_tensor("WKB", [128, D], BF16))
        WVB = ent(nc.sbuf_tensor("WVB", [128, D], BF16))
        MRH = ent(nc.sbuf_tensor("MRH", [128, 64], BF16))
        MRL = ent(nc.sbuf_tensor("MRL", [128, 64], BF16))
        MRS = ent(nc.sbuf_tensor("MRS", [128, 64], F32))
        bqp1 = ent(nc.sbuf_tensor("bqp1", [128, 8], F32))
        bkvp = ent(nc.sbuf_tensor("bkvp", [128, 2], F32))
        em = mk_em(env)
        load_weight_dxe(env, em,
                        lambda dc: W["wq1"][dc * 128:(dc + 1) * 128, :], D)
        em.op("y", lambda y: y.dma_start(out=bqp1[:, :], in_=W["bq1"]), sd, 16)
        em.op("y", lambda y: y.dma_start(out=bkvp[:, 0:1], in_=W["bk1"]),
              sd, 16)
        em.op("y", lambda y: y.dma_start(out=bkvp[:, 1:2], in_=W["bv1"]),
              sd, 16)
        em.wait("g", sv)
        em.wait("g", ss)
        em.op("g", lambda g: g.dma_start(
            out=env["STG"][0][:, :], in_=W["wk1"]), sd2, 16)
        em.wait("v", sd2)
        em.op("v", lambda v: v.tensor_copy(WKB[:, :], env["STG"][0][:, :]), sv)
        em.op("g", lambda g: g.dma_start(
            out=env["STG"][1][:, :], in_=W["wv1"]), sd2, 16)
        em.wait("v", sd2)
        em.op("v", lambda v: v.tensor_copy(WVB[:, :], env["STG"][1][:, :]), sv)
        em.op("g", lambda g: g.dma_start(out=MRS[:, :], in_=W["mr1"]), sd2, 16)
        em.wait("v", sd2)
        em.op("v", lambda v: v.tensor_copy(MRH[:, :], MRS[:, :]), sv)
        em.wait("v", sv)
        em.op("v", lambda v: v.tensor_copy(env["STG"][0][:, 0:64], MRH[:, :]), sv)
        em.wait("v", sv)
        em.op("v", lambda v: v.tensor_tensor(
            MRS[:, :], MRS[:, :], env["STG"][0][:, 0:64], op=ALU.subtract), sv)
        em.wait("v", sv)
        em.op("v", lambda v: v.tensor_copy(MRL[:, :], MRS[:, :]), sv)
        em.op("g", lambda g: g.memset(Kp[:, 0:128], 0.0))
        em.op("g", lambda g: g.memset(Kp[:, 2176:2304], 0.0))
        em.op("g", lambda g: g.memset(Vt[:, 0:128], 0.0))
        em.op("g", lambda g: g.memset(Vt[:, 2176:2304], 0.0))
        em.wait("y", sd)
        em.wait("g", sd2)
        with nc.Block(no_gpsimd_drain=True) as blk:
            em.flush(blk)

        with ExitStack() as _es6:
            ent = _es6.enter_context
            XN32 = ent(nc.sbuf_tensor("XN32L1", [128, D], F32))
            XNB = ent(nc.sbuf_tensor("XNB", [128, D], BF16))
            HI32 = ent(nc.sbuf_tensor("HI32", [128, D], F32))
            LOB = ent(nc.sbuf_tensor("LOB", [128, D], BF16))
            LOF = ent(nc.sbuf_tensor("LOF", [128, D], BF16))
            VFC = ent(nc.sbuf_tensor("VFC", [128, 128], BF16))
            GS = ent(nc.sbuf_tensor("GS", [128, 48], F32))
            txp0 = ent(nc.psum_tensor("txp0", [128, 128], BF16))
            txp1 = ent(nc.psum_tensor("txp1", [128, 128], BF16))
            lgp = ent(nc.psum_tensor("lgp", [128, 8], F32))
            qfp0 = ent(nc.psum_tensor("qfp0", [128, 128], F32))
            qfp1 = ent(nc.psum_tensor("qfp1", [128, 128], F32))
            kvp0 = ent(nc.psum_tensor("kvp0", [128, 128], F32))
            kvp1 = ent(nc.psum_tensor("kvp1", [128, 128], F32))
            vtp = ent(nc.psum_tensor("vtp", [128, 128], BF16))
            for c in range(NT):
                em = mk_em(env)
                emit_ln_stats(env, em, c, XN32, HI32)
                em.op("v", lambda v: v.tensor_copy(XNB[:, :], XN32[:, :]), sv)
                em.wait("v", sv)
                em.op("v", lambda v: v.tensor_copy(HI32[:, :], XNB[:, :]), sv)
                em.wait("v", sv)
                em.op("v", lambda v: v.tensor_tensor(
                    XN32[:, :], XN32[:, :], HI32[:, :], op=ALU.subtract), sv)
                em.wait("v", sv)
                em.op("v", lambda v: v.tensor_copy(LOB[:, :], XN32[:, :]), sv)
                em.op("y", lambda y, cc=c: y.dma_start(
                    out=W["x1d"].rearrange("(n p) m -> p n m", p=128)[:, cc, :],
                    in_=X[:, cc * D:(cc + 1) * D]), sdo, 16)
                em.wait("t", sv)
                txs = (txp0, txp1)
                for dc in range(ND):
                    if dc >= 2:
                        em.wait("t", ss)
                    em.op("t", lambda t, d=dc, p=txs[dc % 2]: t.transpose(
                        p[:, :], XNB[:, d * 128:(d + 1) * 128], idb[:, :]), st)
                    em.wait("s", st)
                    em.op("s", lambda s, d=dc, cc=c, p=txs[dc % 2]: s.copy(
                        A[:, d * T + cc * 128:d * T + (cc + 1) * 128], p[:, :]),
                        ss)
                for dc in range(ND):
                    em.wait("t", ss)
                    em.op("t", lambda t, d=dc, p=txs[dc % 2]: t.transpose(
                        p[:, :], LOB[:, d * 128:(d + 1) * 128], idb[:, :]), st)
                    em.wait("s", st)
                    em.op("s", lambda s, d=dc, p=txs[dc % 2]: s.copy(
                        LOF[:, d * 128:(d + 1) * 128], p[:, :]), ss)
                env["fence_s"](em)
                em.wait("t", ss)
                for pi in range(3):
                    for dc in range(ND):
                        if pi == 0:
                            lh, rh = (
                                A[:, dc * T + c * 128:dc * T + (c + 1) * 128],
                                MRH[:, dc * 8:(dc + 1) * 8])
                        elif pi == 1:
                            lh, rh = (LOF[:, dc * 128:(dc + 1) * 128],
                                      MRH[:, dc * 8:(dc + 1) * 8])
                        else:
                            lh, rh = (
                                A[:, dc * T + c * 128:dc * T + (c + 1) * 128],
                                MRL[:, dc * 8:(dc + 1) * 8])
                        em.op("t", lambda t, l=lh, r=rh, pp=pi, d=dc: t.matmul(
                            lgp[:, :], l, r,
                            start=(pp == 0 and d == 0),
                            stop=(pp == 2 and d == ND - 1)),
                            st if (pi == 2 and dc == ND - 1) else None)
                em.wait("v", st)
                em.op("v", lambda v: v.tensor_copy(GS[:, 0:8], lgp[:, :]), sv)
                em.wait("v", sv)
                em.op("v", lambda v: v.reduce_max(GS[:, 8:9], GS[:, 0:8],
                                                  axis=AX.X), sv)
                em.wait("v", sv)
                em.op("v", lambda v: v.tensor_scalar(
                    GS[:, 16:24], GS[:, 0:8], GS[:, 8:9], None, op0=ALU.is_ge),
                    sv)
                em.wait("v", sv)
                em.op("v", lambda v: v.scalar_tensor_tensor(
                    GS[:, 24:32], GS[:, 16:24], NEG, GS[:, 0:8],
                    op0=ALU.mult, op1=ALU.add), sv)
                em.wait("v", sv)
                em.op("v", lambda v: v.reduce_max(GS[:, 9:10], GS[:, 24:32],
                                                  axis=AX.X), sv)
                em.wait("v", sv)
                em.op("v", lambda v: v.tensor_scalar(
                    GS[:, 10:11], GS[:, 8:9], -1.0, None, op0=ALU.mult), sv)
                em.wait("s", sv)
                em.op("s", lambda s: s.activation(
                    GS[:, 32:40], GS[:, 0:8], AF.Exp, bias=GS[:, 10:11]), ss)
                em.op("s", lambda s: s.activation(
                    GS[:, 11:12], GS[:, 9:10], AF.Exp, bias=GS[:, 10:11]), ss)
                em.wait("v", ss)
                em.op("v", lambda v: v.tensor_scalar(
                    GS[:, 12:13], GS[:, 11:12], 1.0, None, op0=ALU.add), sv)
                em.wait("v", sv)
                em.op("v", lambda v: v.reciprocal(GS[:, 13:14], GS[:, 12:13]), sv)
                em.wait("v", sv)
                em.op("v", lambda v: v.tensor_scalar(
                    GS[:, 40:48], GS[:, 0:8], GS[:, 9:10], None, op0=ALU.is_ge),
                    sv)
                em.wait("v", sv)
                em.op("v", lambda v: v.tensor_tensor(
                    GS[:, 40:48], GS[:, 40:48], GS[:, 32:40], op=ALU.mult), sv)
                em.wait("v", sv)
                em.op("v", lambda v, cc=c: v.tensor_scalar(
                    GATES[:, cc * 8:(cc + 1) * 8], GS[:, 40:48], GS[:, 13:14],
                    None, op0=ALU.mult), sv)
                qfs = (qfp0, qfp1)
                for i in range(ND):
                    ps = qfs[i % 2]
                    if i >= 2:
                        em.wait("t", sv)
                    for dc in range(ND):
                        em.op("t", lambda t, ii=i, d=dc, cc=c, p=ps: t.matmul(
                            p[:, :],
                            WB1[:, d * D + ii * 128:d * D + (ii + 1) * 128],
                            A[:, d * T + cc * 128:d * T + (cc + 1) * 128],
                            start=(d == 0), stop=(d == ND - 1)),
                            st if dc == ND - 1 else None)
                    em.wait("v", st)
                    em.op("v", lambda v, ii=i, cc=c, p=ps: v.tensor_scalar(
                        B[:, ii * T + cc * 128:ii * T + (cc + 1) * 128],
                        p[:, :], bqp1[:, ii:ii + 1], None, op0=ALU.add), sv)
                kvs = (kvp0, kvp1)
                for dc in range(ND):
                    em.op("t", lambda t, d=dc, cc=c: t.matmul(
                        kvs[0][:, :], WKB[:, d * 128:(d + 1) * 128],
                        A[:, d * T + cc * 128:d * T + (cc + 1) * 128],
                        start=(d == 0), stop=(d == ND - 1)),
                        st if dc == ND - 1 else None)
                em.wait("v", st)
                em.op("v", lambda v, cc=c: v.tensor_scalar(
                    Kp[:, 128 + cc * 128:128 + (cc + 1) * 128], kvs[0][:, :],
                    bkvp[:, 0:1], None, op0=ALU.add), sv)
                for dc in range(ND):
                    em.op("t", lambda t, d=dc, cc=c: t.matmul(
                        kvs[1][:, :], WVB[:, d * 128:(d + 1) * 128],
                        A[:, d * T + cc * 128:d * T + (cc + 1) * 128],
                        start=(d == 0), stop=(d == ND - 1)),
                        st if dc == ND - 1 else None)
                em.wait("v", st)
                em.op("v", lambda v: v.tensor_scalar(
                    VFC[:, :], kvs[1][:, :], bkvp[:, 1:2], None, op0=ALU.add),
                    sv)
                em.op("v", lambda v: v.tensor_copy(STAT[:, 61:62],
                                                   STAT[:, 60:61]), sv)
                em.wait("t", sv)
                em.op("t", lambda t: t.transpose(vtp[:, :], VFC[:, :],
                                                 idb[:, :]), st)
                em.wait("v", st)
                em.op("v", lambda v, cc=c: v.tensor_copy(
                    Vt[:, 128 + cc * 128:128 + (cc + 1) * 128], vtp[:, :]), sv)
                with nc.Block(no_gpsimd_drain=True) as blk:
                    em.flush(blk)

        if os.environ.get("MK_DBG2"):
            qdbg = nc.dram_tensor("qdbg", [128, NT * D], F32,
                                  kind="ExternalOutput").ap()
            kdbg = nc.dram_tensor("kdbg", [128, 2304], F32,
                                  kind="ExternalOutput").ap()
            vdbg = nc.dram_tensor("vdbg", [128, 2304], F32,
                                  kind="ExternalOutput").ap()
            adbg = nc.dram_tensor("adbg", [128, NT * D], F32,
                                  kind="ExternalOutput").ap()
            wdbg = nc.dram_tensor("wdbg", [128, 2 * D], F32,
                                  kind="ExternalOutput").ap()
            em = mk_em(env)
            em.op("v", lambda v: v.tensor_copy(env["STG"][0][:, :],
                                               WKB[:, :]), sv)
            em.op("v", lambda v: v.tensor_copy(env["STG"][1][:, :],
                                               WB1[:, 0:D]), sv)
            em.wait("y", sv)
            em.op("y", lambda y: y.dma_start(out=wdbg[:, 0:D],
                                             in_=env["STG"][0][:, :]),
                  env["sdo"], 16)
            em.op("y", lambda y: y.dma_start(out=wdbg[:, D:2 * D],
                                             in_=env["STG"][1][:, :]),
                  env["sdo"], 16)
            em.wait("y", env["sdo"])
            with nc.Block(no_gpsimd_drain=True) as blk:
                em.flush(blk)
            em = mk_em(env)
            for seg in range(NT):
                em.op("v", lambda v, s2=seg: v.tensor_copy(
                    env["STG"][0][:, :], A[:, s2 * D:(s2 + 1) * D]), sv)
                em.wait("y", sv)
                em.op("y", lambda y, s2=seg: y.dma_start(
                    out=adbg[:, s2 * D:(s2 + 1) * D],
                    in_=env["STG"][0][:, :]), env["sdo"], 16)
                em.wait("v", env["sdo"])
            with nc.Block(no_gpsimd_drain=True) as blk:
                em.flush(blk)
            em = mk_em(env)
            for seg in range(NT):
                em.op("v", lambda v, s2=seg: v.tensor_copy(
                    env["STG"][0][:, :], B[:, s2 * D:(s2 + 1) * D]), sv)
                em.wait("y", sv)
                em.op("y", lambda y, s2=seg: y.dma_start(
                    out=qdbg[:, s2 * D:(s2 + 1) * D],
                    in_=env["STG"][0][:, :]), env["sdo"], 16)
                em.wait("v", env["sdo"])
            for seg in range(3):
                w = 768
                em.op("v", lambda v, s2=seg, w2=w: v.tensor_copy(
                    env["STG"][0][:, 0:w2], Kp[:, s2 * w2:(s2 + 1) * w2]), sv)
                em.wait("y", sv)
                em.op("y", lambda y, s2=seg, w2=w: y.dma_start(
                    out=kdbg[:, s2 * w2:(s2 + 1) * w2],
                    in_=env["STG"][0][:, 0:w2]), env["sdo"], 16)
                em.wait("v", env["sdo"])
                em.op("v", lambda v, s2=seg, w2=w: v.tensor_copy(
                    env["STG"][0][:, 0:w2], Vt[:, s2 * w2:(s2 + 1) * w2]), sv)
                em.wait("y", sv)
                em.op("y", lambda y, s2=seg, w2=w: y.dma_start(
                    out=vdbg[:, s2 * w2:(s2 + 1) * w2],
                    in_=env["STG"][0][:, 0:w2]), env["sdo"], 16)
                em.wait("v", env["sdo"])
            with nc.Block(no_gpsimd_drain=True) as blk:
                em.flush(blk)

        em = mk_em(env)
        load_weight_dxe(env, em,
                        lambda dc: W["wo1"][dc * 128:(dc + 1) * 128, :], D)
        em.op("y", lambda y: y.dma_start(out=ROW[:, :], in_=W["bo1"].rearrange("(a d) -> a d", a=1)),
              sd, 16)
        em.wait("y", sd)
        with nc.Block(no_gpsimd_drain=True) as blk:
            em.flush(blk)
        with nc.psum_tensor("bc_psL1", [128, 512], F32) as bc_ps:
            em = mk_em(env)
            for sl in range(2):
                em.op("t", lambda t, s=sl: t.matmul(
                    bc_ps[:, :], ones1[:, :], ROW[:, s * 512:(s + 1) * 512],
                    start=True, stop=True), st)
                em.wait("v", st)
                em.op("v", lambda v, s=sl: v.tensor_copy(
                    BB[:, s * 512:(s + 1) * 512], bc_ps[:, :]), sv)
                em.wait("t", sv)
            with nc.Block(no_gpsimd_drain=True) as blk:
                em.flush(blk)

        with ExitStack() as _est:
            ent = _est.enter_context
            BAND = ent(nc.sbuf_tensor("BAND", [128, 1024], F32))
            S5 = ent(nc.sbuf_tensor("S5", [128, 1024], F32))
            E5 = ent(nc.sbuf_tensor("E5", [128, 1024], F32))
            Pb5 = ent(nc.sbuf_tensor("Pb5", [128, 1024], BF16))
            PT5 = ent(nc.sbuf_tensor("PT5", [128, 1024], BF16))
            OC = ent(nc.sbuf_tensor("OC", [128, 2 * D], BF16))
            S1C = ent(nc.sbuf_tensor("S1C", [128, 2 * D], F32))
            scp0 = ent(nc.psum_tensor("scp0L1", [128, 512], F32))
            scp1 = ent(nc.psum_tensor("scp1L1", [128, 512], F32))
            ptp0 = ent(nc.psum_tensor("ptp0L1", [128, 512], BF16))
            ptp1 = ent(nc.psum_tensor("ptp1L1", [128, 512], BF16))
            pvp0 = ent(nc.psum_tensor("pvp0L1", [64, 128], F32))
            pvp1 = ent(nc.psum_tensor("pvp1L1", [64, 128], F32))
            s1p0 = ent(nc.psum_tensor("s1p0L1", [128, 512], F32))
            s1p1 = ent(nc.psum_tensor("s1p1L1", [128, 512], F32))

            em = mk_em(env)
            for r in range(2):
                em.op("g", lambda g, rr=r: g.memset(
                    BAND[:, rr * 512:(rr + 1) * 512], 0.0))
                em.op("g", lambda g, rr=r: g.affine_select(
                    out=BAND[:, rr * 512:(rr + 1) * 512],
                    in_=BAND[:, rr * 512:(rr + 1) * 512],
                    compare_op=ALU.is_ge, fill=NEG,
                    base=-128 * rr, pattern=[[1, 512]], channel_multiplier=-1))
                em.op("g", lambda g, rr=r: g.affine_select(
                    out=BAND[:, rr * 512:(rr + 1) * 512],
                    in_=BAND[:, rr * 512:(rr + 1) * 512],
                    compare_op=ALU.is_ge, fill=NEG,
                    base=128 * rr + 255, pattern=[[-1, 512]],
                    channel_multiplier=1))
            with nc.Block(no_gpsimd_drain=True) as blk:
                em.flush(blk)

            em = mk_em(env)
            scs = (scp0, scp1)
            pts = (ptp0, ptp1)
            pvs = (pvp0, pvp1)
            maskmax_done = {0: 0, 1: 0}
            exp_done = {0: 0, 1: 0}
            pb_done = {0: 0, 1: 0}
            ptc_done = {0: 0, 1: 0}
            pv_done = {0: 0, 1: 0}
            ocevac_done = {0: 0, 1: 0}
            s1evac_done = {0: 0, 1: 0}
            spill_done = {0: 0, 1: 0}
            NH = NT * 16

            def emit_scores(k):
                c2 = k // 16
                h2 = k % 16
                par2 = k % 2
                g2 = h2 // 8
                i2 = h2 % 8
                n2 = c2 // 2
                if maskmax_done[par2]:
                    em.wait("t", sv, maskmax_done[par2])
                em.op("t", lambda t, gg=g2, ii=i2, p=scs[par2], cc=c2, nn=n2:
                      t.matmul(
                    p[:, :],
                    B[64 * gg:64 * (gg + 1),
                      ii * T + cc * 128:ii * T + (cc + 1) * 128],
                    Kp[64 * gg:64 * (gg + 1), 256 * nn:256 * nn + 512],
                    start=True, stop=True, tile_position=(64 * gg, 0)), st)
                scores_done[par2] = st.n

            scores_done = {0: 0, 1: 0}
            pbg_done = {0: 0, 1: 0}
            emit_scores(0)
            for k in range(NH):
                c = k // 16
                h = k % 16
                par = k % 2
                g = h // 8
                i = h % 8
                n = c // 2
                r = c % 2
                cpar = c % 2
                my_scores = scores_done[par]
                if k + 1 < NH:
                    emit_scores(k + 1)
                em.wait("v", st, my_scores)
                if exp_done[par]:
                    em.wait("v", ss, exp_done[par])
                em.op("v", lambda v, p=scs[par], rr=r, q=par:
                      v.tensor_tensor(
                    S5[:, q * 512:(q + 1) * 512], p[:, :],
                    BAND[:, rr * 512:(rr + 1) * 512], op=ALU.add), sv)
                em.wait("v", sv)
                em.op("v", lambda v, q=par: v.reduce_max(
                    STAT[:, 40 + q:41 + q], S5[:, q * 512:(q + 1) * 512],
                    axis=AX.X, negate=True), sv)
                em._fence("v")
                maskmax_done[par] = sv.n
                em.wait("s", sv, maskmax_done[par])
                em.op("s", lambda s, q=par: s.activation(
                    E5[:, q * 512:(q + 1) * 512],
                    S5[:, q * 512:(q + 1) * 512], AF.Exp,
                    bias=STAT[:, 40 + q:41 + q],
                    accum_out=STAT[:, 42 + q:43 + q]), ss)
                em._fence("s")
                exp_done[par] = ss.n
                em.wait("v", ss, exp_done[par])
                em.op("v", lambda v, q=par: v.reciprocal(
                    STAT[:, 44 + q:45 + q], STAT[:, 42 + q:43 + q]), sv)
                em.wait("v", sv)
                if pbg_done[par]:
                    em.wait("v", st, pbg_done[par])
                em.op("v", lambda v, q=par: v.tensor_scalar(
                    Pb5[:, q * 512:(q + 1) * 512],
                    E5[:, q * 512:(q + 1) * 512],
                    STAT[:, 44 + q:45 + q], None, op0=ALU.mult), sv)
                em._fence("v")
                pb_done[par] = sv.n
                em.wait("t", sv, pb_done[par])
                if ptc_done[par]:
                    em.wait("t", ss, ptc_done[par])
                for j in range(4):
                    em.op("t", lambda t, jj=j, q=par, p=pts[par]:
                          t.transpose(
                        p[:, jj * 128:(jj + 1) * 128],
                        Pb5[:, q * 512 + jj * 128:q * 512 + (jj + 1) * 128],
                        idb[:, :]), st if j == 3 else None)
                pbg_done[par] = st.n
                em.wait("s", st)
                em.op("s", lambda s, q=par, p=pts[par]: s.copy(
                    PT5[:, q * 512:(q + 1) * 512], p[:, :]), ss)
                em._fence("s")
                ptc_done[par] = ss.n
                em.wait("t", ss, ptc_done[par])
                if ocevac_done[par]:
                    em.wait("t", ss, ocevac_done[par])
                for j in range(4):
                    em.op("t", lambda t, jj=j, gg=g, q=par, nn=n,
                          p=pvs[par]: t.matmul(
                        p[:, :],
                        Vt[:, (2 * nn + jj) * 128 + 64 * gg:
                           (2 * nn + jj) * 128 + 64 * (gg + 1)],
                        PT5[:, q * 512 + jj * 128:q * 512 + (jj + 1) * 128],
                        start=(jj == 0), stop=(jj == 3)),
                        st if j == 3 else None)
                pv_done[par] = st.n
                em.wait("s", st, pv_done[par])
                em.op("s", lambda s, gg=g, ii=i, q=cpar, p=pvs[par]:
                      s.copy(
                    OC[64 * gg:64 * (gg + 1),
                       q * D + ii * 128:q * D + (ii + 1) * 128],
                    p[:, :]), ss)
                em._fence("s")
                ocevac_done[par] = ss.n

                if h == 15:
                    oc_ready = max(ocevac_done[0], ocevac_done[1])
                    s1s = (s1p0, s1p1)
                    for hh in range(2):
                        em.wait("t", ss, oc_ready)
                        if s1evac_done[hh]:
                            em.wait("t", sv, s1evac_done[hh])
                        for i2 in range(ND):
                            em.op("t", lambda t, h2=hh, ii=i2, q=cpar:
                                  t.matmul(
                                s1s[h2][:, :],
                                OC[:, q * D + ii * 128:q * D + (ii + 1) * 128],
                                WB1[:, ii * D + h2 * 512:ii * D + (h2 + 1) * 512],
                                start=(ii == 0), stop=(ii == ND - 1)),
                                st if i2 == ND - 1 else None)
                        s1mm = st.n
                        em.wait("v", st, s1mm)
                        if spill_done[cpar]:
                            em.wait("v", sdo, spill_done[cpar])
                        em.op("v", lambda v, h2=hh, q=cpar: v.tensor_tensor(
                            S1C[:, q * D + h2 * 512:q * D + (h2 + 1) * 512],
                            s1s[h2][:, :],
                            BB[:, h2 * 512:(h2 + 1) * 512], op=ALU.add), sv)
                        em._fence("v")
                        s1evac_done[hh] = sv.n
                    em.wait("y", sv, max(s1evac_done[0], s1evac_done[1]))
                    em.op("y", lambda y, cc=c, q=cpar: y.dma_start(
                        out=W["s1d"].rearrange("(n p) m -> p n m", p=128)
                        [:, cc, :],
                        in_=S1C[:, q * D:(q + 1) * D]), sdo, 16)
                    spill_done[cpar] = sdo.n
            with nc.Block(no_gpsimd_drain=True) as blk:
                em.flush(blk)

def build_l1_moe(env):
    nc = env["nc"]
    A, WB1, STAT, GATES = env["A"], env["WB1"], env["STAT"], env["GATES"]
    W = env["W"]
    sd, sd2, sd3, sd4, sdo, sv, ss, st, sg = (
        env["sd"], env["sd2"], env["sd3"], env["sd4"], env["sdo"],
        env["sv"], env["ss"], env["st"], env["sg"])
    out = env["out"]
    stg = env["STG"]
    NE = 8
    NM = 16
    TT = 2

    gdbg = None
    if os.environ.get("MK_DBG3"):
        gdbg = nc.dram_tensor("gdbg", [T, D], F32, kind="ExternalOutput").ap()

    with ExitStack() as _esm:
        ent = _esm.enter_context
        H = ent(nc.sbuf_tensor("H", [128, NM * 1024], BF16))
        OUTA = ent(nc.sbuf_tensor("OUTA", [128, 8 * D], F32))
        W13G0 = ent(nc.sbuf_tensor("W13G0", [128, D], BF16))
        W13G1 = ent(nc.sbuf_tensor("W13G1", [128, D], BF16))
        W13U0 = ent(nc.sbuf_tensor("W13U0", [128, D], BF16))
        W13U1 = ent(nc.sbuf_tensor("W13U1", [128, D], BF16))
        W2B0 = ent(nc.sbuf_tensor("W2B0", [128, 4 * D], BF16))
        W2B1 = ent(nc.sbuf_tensor("W2B1", [128, 4 * D], BF16))
        GSIL = ent(nc.sbuf_tensor("GSIL", [128, D], F32))
        XC = ent(nc.sbuf_tensor("XC", [128, D], F32))
        S1R = ent(nc.sbuf_tensor("S1R", [128, D], F32))
        STG2 = ent(nc.sbuf_tensor("STG2", [128, D], F32))
        STG3 = ent(nc.sbuf_tensor("STG3", [128, D], F32))
        YST0 = ent(nc.sbuf_tensor("YST0", [128, D], F32))
        YST1 = ent(nc.sbuf_tensor("YST1", [128, D], F32))
        gp0 = ent(nc.psum_tensor("gp0", [128, 512], F32))
        gp1 = ent(nc.psum_tensor("gp1", [128, 512], F32))
        up0 = ent(nc.psum_tensor("up0", [128, 512], F32))
        up1 = ent(nc.psum_tensor("up1", [128, 512], F32))
        yp0 = ent(nc.psum_tensor("yp0", [128, 512], F32))
        yp1 = ent(nc.psum_tensor("yp1", [128, 512], F32))

        w13g = (W13G0, W13G1)
        w13u = (W13U0, W13U1)
        w2b = (W2B0, W2B1)
        ysts = (YST0, YST1)

        for tt in range(TT):
            em = mk_em(env)
            em.op("g", lambda g: g.memset(OUTA[:, :], 0.0))
            with nc.Block(no_gpsimd_drain=True) as blk:
                em.flush(blk)

            em = mk_em(env)
            nexp = int(os.environ.get("MK_NEXP", NE))

            def h_dma(e, m):
                par = m % 2
                sgg = (stg[0], STG2)[par]
                sgu = (stg[1], STG3)[par]
                em.wait("y", sv)
                em.wait("g", sv)
                em.op("y", lambda y, ee=e, mm=m, b=sgg: y.dma_start(
                    out=b[:, :], in_=W["w13"][ee, mm]), sd, 16)
                em.op("g", lambda g, ee=e, mm=m, b=sgu: g.dma_start(
                    out=b[:, :], in_=W["w13"][ee, 16 + mm]), sd2, 16)

            def y_dma(e, j):
                b = ysts[j % 2]
                em.wait("y" if j % 2 == 0 else "g", ss)
                qe = "y" if j % 2 == 0 else "g"
                qs = sd if j % 2 == 0 else sd2
                em.op(qe, lambda q, ee=e, mm=j, bb=b: q.dma_start(
                    out=bb[:, :], in_=W["w2"][ee, mm * 128:(mm + 1) * 128, :]),
                    qs, 16)

            def y_cast(e, j):
                b = ysts[j % 2]
                dst = w2b[(j // 4) % 2]
                em.wait("s", st)
                em.wait("s", sd if j % 2 == 0 else sd2)
                em.op("s", lambda s, bb=b, dd=dst, k=j % 4: s.copy(
                    dd[:, k * D:(k + 1) * D], bb[:, :]), ss)

            hready = {}
            ycast_ready = {}
            evacdone = {0: 0, 1: 0}
            ttready = 0
            ssready = 0

            def h_cast2(e, m):
                par = m % 2
                sgg = (stg[0], STG2)[par]
                sgu = (stg[1], STG3)[par]
                em.wait("v", st)
                em.wait("v", sd)
                em.op("v", lambda v, p=par, b=sgg: v.tensor_copy(
                    w13g[p][:, :], b[:, :]), sv)
                em.wait("v", sd2)
                em.op("v", lambda v, p=par, b=sgu: v.tensor_copy(
                    w13u[p][:, :], b[:, :]), sv)
                em._fence("v")
                hready[(e, m)] = sv.n

            for e in range(nexp):
                if e == 0:
                    h_dma(e, 0)
                    h_cast2(e, 0)
                for m in range(NM):
                    if m + 1 < NM:
                        h_dma(e, m + 1)
                        h_cast2(e, m + 1)
                    elif e + 1 < nexp:
                        h_dma(e + 1, 0)
                        h_cast2(e + 1, 0)
                    if m >= 8:
                        j = m - 8
                        y_dma(e, j)
                        y_cast(e, j)
                        if j % 4 == 3:
                            em._fence("s")
                            ycast_ready[(e, j // 4)] = ss.n
                    em.wait("t", sv, max(hready[(e, m)], ttready))
                    if ssready:
                        em.wait("t", ss, ssready)
                    for th in range(2):
                        gps = (gp0, gp1)[th]
                        ups = (up0, up1)[th]
                        for dc in range(ND):
                            em.op("t", lambda t, d=dc, p=m % 2, h2=th, q=gps:
                                  t.matmul(
                                q[:, :], w13g[p][:, d * 128:(d + 1) * 128],
                                A[:, d * T + tt * 1024 + h2 * 512:
                                  d * T + tt * 1024 + (h2 + 1) * 512],
                                start=(d == 0), stop=(d == ND - 1)),
                                st if dc == ND - 1 else None)
                        for dc in range(ND):
                            em.op("t", lambda t, d=dc, p=m % 2, h2=th, q=ups:
                                  t.matmul(
                                q[:, :], w13u[p][:, d * 128:(d + 1) * 128],
                                A[:, d * T + tt * 1024 + h2 * 512:
                                  d * T + tt * 1024 + (h2 + 1) * 512],
                                start=(d == 0), stop=(d == ND - 1)),
                                st if dc == ND - 1 else None)
                    em.wait("s", st)
                    for th in range(2):
                        em.op("s", lambda s, h2=th, q=(gp0, gp1)[th]:
                              s.activation(
                            GSIL[:, h2 * 512:(h2 + 1) * 512], q[:, :],
                            AF.Silu), ss)
                    em._fence("s")
                    ssready = ss.n
                    em.wait("v", ss, ssready)
                    for th in range(2):
                        em.op("v", lambda v, mm=m, h2=th, q=(up0, up1)[th]:
                              v.tensor_tensor(
                            H[:, mm * 1024 + h2 * 512:mm * 1024 + (h2 + 1) * 512],
                            GSIL[:, h2 * 512:(h2 + 1) * 512], q[:, :],
                            op=ALU.mult), sv)
                    em._fence("v")
                    ttready = sv.n

                for s4 in range(4):
                    for tc in range(8):
                        cidx = tt * 8 + tc
                        for eb in range(2):
                            par = (tc * 2 + eb) % 2
                            yps = (yp0, yp1)[par]
                            w = evacdone[par]
                            if s4 == 0 and tc == 0 and eb == 0:
                                w = max(w, ttready)
                            if w:
                                em.wait("t", sv, w)
                            if tc == 0 and eb == 0:
                                em.wait("t", ss, ycast_ready[(e, s4)])
                            for k in range(4):
                                m = s4 * 4 + k
                                em.op("t", lambda t, mm=m, k2=k, t2=tc, b=eb,
                                      p=yps, dd=w2b[s4 % 2]: t.matmul(
                                    p[:, :],
                                    H[:, mm * 1024 + t2 * 128:
                                      mm * 1024 + (t2 + 1) * 128],
                                    dd[:, k2 * D + b * 512:
                                       k2 * D + (b + 1) * 512],
                                    start=(k2 == 0), stop=(k2 == 3)),
                                    st if k == 3 else None)
                            em.wait("v", st)
                            em.op("v", lambda v, t2=tc, b=eb, ci=cidx, ee=e,
                                  p=yps: v.scalar_tensor_tensor(
                                OUTA[:, t2 * D + b * 512:t2 * D + (b + 1) * 512],
                                p[:, :], GATES[:, ci * 8 + ee:ci * 8 + ee + 1],
                                OUTA[:, t2 * D + b * 512:t2 * D + (b + 1) * 512],
                                op0=ALU.mult, op1=ALU.add), sv)
                            em._fence("v")
                            evacdone[par] = sv.n
                        if 1 <= s4 <= 2 and tc < 4:
                            j = (s4 + 1) * 4 + tc
                            y_dma(e, j)
                            y_cast(e, j)
                            if j % 4 == 3:
                                em._fence("s")
                                ycast_ready[(e, s4 + 1)] = ss.n
            with nc.Block(no_gpsimd_drain=True) as blk:
                em.flush(blk)

            for tc in range(8):
                cidx = tt * 8 + tc
                em = mk_em(env)
                em.wait("s", sdo)
                em.wait("y", sv)
                em.wait("g", sv)
                em.op("y", lambda y, ci=cidx: y.dma_start(
                    out=XC[:, :],
                    in_=W["x1d"].rearrange("(n p) m -> p n m", p=128)[:, ci, :]),
                    sd, 16)
                em.op("g", lambda g, ci=cidx: g.dma_start(
                    out=S1R[:, :],
                    in_=W["s1d"].rearrange("(n p) m -> p n m", p=128)[:, ci, :]),
                    sd2, 16)
                em.op("s", lambda s, t2=tc: s.activation(
                    GSIL[:, :], OUTA[:, t2 * D:(t2 + 1) * D], AF.Silu), ss)
                env["fence_s"](em)
                if gdbg is not None:
                    em.wait("g", ss)
                    em.op("g", lambda g, ci=cidx: g.dma_start(
                        out=gdbg.rearrange("(n p) m -> p n m", p=128)[:, ci, :],
                        in_=GSIL[:, :]), sdo, 16)
                    em.wait("g", sdo)
                    em.wait("v", sdo)
                em.wait("v", ss)
                em.wait("v", sd)
                em.wait("v", sd2)
                em.op("v", lambda v: v.tensor_tensor(
                    GSIL[:, :], GSIL[:, :], S1R[:, :], op=ALU.mult), sv)
                em.wait("v", sv)
                em.op("v", lambda v: v.tensor_tensor(
                    GSIL[:, :], GSIL[:, :], XC[:, :], op=ALU.add), sv)
                env["fence_v"](em)
                em.wait("y", sv)
                em.op("y", lambda y, ci=cidx: y.dma_start(
                    out=out.rearrange("(n p) m -> p n m", p=128)[:, ci, :],
                    in_=GSIL[:, :]), sdo, 16)
                with nc.Block(no_gpsimd_drain=True) as blk:
                    em.flush(blk)


def kernel(**inputs):
    n = 8
    x = np.ascontiguousarray(np.asarray(inputs["L0_attn_inputs"],
                                        dtype=np.float32))
    wmap = {
        "wq0": "L0_attn_Wq", "bq0": "L0_attn_bq", "wr0": "L0_ffn_w_router",
        "wb0": "L0_ffn_w_block", "wq1": "L1_attn_Wq", "bq1": "L1_attn_bq",
        "wk1": "L1_attn_Wk", "bk1": "L1_attn_bk", "wv1": "L1_attn_Wv",
        "bv1": "L1_attn_bv", "wo1": "L1_attn_Wo", "bo1": "L1_attn_bo",
        "mr1": "L1_ffn_moe_router", "w13": "L1_ffn_moe_w13",
        "w2": "L1_ffn_moe_w2",
    }
    ws = {k: np.ascontiguousarray(np.asarray(inputs[v], dtype=np.float32))
          for k, v in wmap.items()}
    idx = (np.arange(2)[None, :, None] * 512 + np.arange(8)[:, None, None] * 64
           + np.arange(64)[None, None, :]).reshape(-1)
    ws["bq0t"] = np.ascontiguousarray(ws["bq0"].reshape(8, 128).T)
    ws["wq1"] = np.ascontiguousarray(ws["wq1"][:, idx])
    ws["bq1"] = np.ascontiguousarray(
        ws["bq1"][idx].reshape(8, 2, 64).transpose(1, 2, 0).reshape(128, 8))
    ws["wo1"] = np.ascontiguousarray(ws["wo1"][idx, :])
    ws["wk1"] = np.ascontiguousarray(
        ws["wk1"].reshape(8, 128, 128).transpose(1, 0, 2).reshape(128, 1024))
    ws["wv1"] = np.ascontiguousarray(
        ws["wv1"].reshape(8, 128, 128).transpose(1, 0, 2).reshape(128, 1024))
    ws["mr1"] = np.ascontiguousarray(
        ws["mr1"].reshape(8, 128, 8).transpose(1, 0, 2).reshape(128, 64))
    ws["bk1"] = np.ascontiguousarray(ws["bk1"].reshape(128, 1))
    ws["bv1"] = np.ascontiguousarray(ws["bv1"].reshape(128, 1))
    ws["w13"] = np.ascontiguousarray(
        ws["w13"].reshape(8, 8, 128, 32, 128).transpose(0, 3, 2, 1, 4)
        .reshape(8, 32, 128, 1024))
    nc = build(do_l0=True, do_l1=True)
    in_maps = []
    for c in range(n):
        m = {"x": np.ascontiguousarray(x[c])}
        m.update(ws)
        in_maps.append(m)
    res = run_bass_kernel_spmd(nc, in_maps, core_ids=list(range(n)))
    outs = [np.asarray(res.results[c]["out"]) for c in range(n)]
    return np.stack(outs, axis=0)


# version 10 (final)
# speedup vs baseline: 1.7163x; 1.0420x over previous
import os
import numpy as np
from contextlib import ExitStack
import concourse.bass as bass
import concourse.mybir as mybir
from concourse.bass_utils import run_bass_kernel_spmd

F32 = mybir.dt.float32
BF16 = mybir.dt.bfloat16
AF = mybir.ActivationFunctionType
ALU = mybir.AluOpType
AX = mybir.AxisListType

T = 2048
D = 1024
NT = 16
ND = 8
KSEL = 1536
GC1 = 0.7978845608028654
GC2 = 0.044715
EPS = 1e-5
NEG = -1e30


def f32(v):
    return np.float32(v)


class Sem:
    def __init__(self, h, is_dma=False):
        self.h = h
        self.n = 0
        self.is_dma = is_dma


class Em:
    ENGS = ("t", "v", "s", "g", "y")

    def __init__(self, nc):
        self.nc = nc
        self.L = {e: [] for e in self.ENGS}
        self._stat = None
        self._engsem = {}

    def set_fences(self, stat, sems_by_eng):
        self._stat = stat
        self._engsem = sems_by_eng

    def _fence(self, peng):
        stat = self._stat
        sem = self._engsem[peng]
        if peng == "v":
            self.op("v", lambda v: v.tensor_copy(stat[:, 61:62],
                                                 stat[:, 60:61]), sem)
        elif peng == "s":
            self.op("s", lambda s: s.copy(stat[:, 62:63],
                                          stat[:, 60:61]), sem)
        elif peng == "g":
            self.op("g", lambda g: g.tensor_copy(stat[:, 63:64],
                                                 stat[:, 60:61]), sem)

    def op(self, eng, fn, sem=None, k=1):
        if sem is not None:
            sem.n += k
            kk = k
            hh = sem.h
            self.L[eng].append(lambda e: fn(e).then_inc(hh, kk))
        else:
            self.L[eng].append(fn)

    DMA_SETTLE_CYC = 800

    def wait(self, eng, sem, val=None):
        if val is None and self._stat is not None:
            for peng, psem in self._engsem.items():
                if sem is psem and eng != peng:
                    self._fence(peng)
                    break
        v = sem.n if val is None else val
        hh = sem.h
        self.L[eng].append(lambda e: e.wait_ge(hh, v))
        if sem.is_dma:
            cyc = self.DMA_SETTLE_CYC
            self.L[eng].append(lambda e: e.nop(cycle_cnt=cyc))

    def flush(self, blk):
        if self._stat is not None:
            for peng in ("v", "s", "g"):
                if self.L[peng]:
                    self._fence(peng)
        names = {"t": "tensor", "v": "vector", "s": "scalar",
                 "g": "gpsimd", "y": "sync"}
        for e in self.ENGS:
            lst = self.L[e]
            if not lst:
                continue

            def body(eng, lst=lst):
                for f in lst:
                    f(eng)
            getattr(blk, names[e])(body)
        self.L = {e: [] for e in self.ENGS}


def mk_em(env):
    em = Em(env["nc"])
    em.set_fences(env["STAT"], {"v": env["sv"], "s": env["ss"],
                                "g": env["sg"]})
    return em


def build(do_l0=True, do_l1=True):
    nc = bass.Bass("TRN2", target_bir_lowering=False)

    xin = nc.dram_tensor("x", [T, D], F32, kind="ExternalInput").ap()
    out = nc.dram_tensor("out", [T, D], F32, kind="ExternalOutput").ap()
    W = {}
    if do_l0:
        W["wq0"] = nc.dram_tensor("wq0", [D, D], F32, kind="ExternalInput").ap()
        W["bq0"] = nc.dram_tensor("bq0", [D], F32, kind="ExternalInput").ap()
        if not do_l1:
            W["bq0t"] = nc.dram_tensor("bq0t", [128, 8], F32,
                                       kind="ExternalInput").ap()
        W["wr0"] = nc.dram_tensor("wr0", [D], F32, kind="ExternalInput").ap()
        W["wb0"] = nc.dram_tensor("wb0", [D, D], F32, kind="ExternalInput").ap()
    if do_l1:
        W["wq1"] = nc.dram_tensor("wq1", [D, D], F32, kind="ExternalInput").ap()
        W["bq1"] = nc.dram_tensor("bq1", [128, 8], F32, kind="ExternalInput").ap()
        W["wk1"] = nc.dram_tensor("wk1", [128, D], F32, kind="ExternalInput").ap()
        W["bk1"] = nc.dram_tensor("bk1", [128, 1], F32, kind="ExternalInput").ap()
        W["wv1"] = nc.dram_tensor("wv1", [128, D], F32, kind="ExternalInput").ap()
        W["bv1"] = nc.dram_tensor("bv1", [128, 1], F32, kind="ExternalInput").ap()
        W["wo1"] = nc.dram_tensor("wo1", [D, D], F32, kind="ExternalInput").ap()
        W["bo1"] = nc.dram_tensor("bo1", [D], F32, kind="ExternalInput").ap()
        W["mr1"] = nc.dram_tensor("mr1", [128, 64], F32, kind="ExternalInput").ap()
        W["bq0t"] = nc.dram_tensor("bq0t", [128, 8], F32, kind="ExternalInput").ap()
        W["w13"] = nc.dram_tensor("w13", [8, 32, 128, 1024], F32,
                                  kind="ExternalInput").ap()
        W["w2"] = nc.dram_tensor("w2", [8, T, D], F32, kind="ExternalInput").ap()
        dbg = "ExternalOutput" if os.environ.get("MK_DBG") else "Internal"
        W["x1d"] = nc.dram_tensor("x1d", [T, D], F32, kind=dbg).ap()
        W["s1d"] = nc.dram_tensor("s1d", [T, D], F32, kind=dbg).ap()

    with ExitStack() as _es1:
        ent = _es1.enter_context
        A = ent(nc.sbuf_tensor("A", [128, NT * D], BF16))
        WB1 = ent(nc.sbuf_tensor("WB1", [128, ND * D], BF16))
        STG0 = ent(nc.sbuf_tensor("STG0", [128, D], F32))
        STG1 = ent(nc.sbuf_tensor("STG1", [128, D], F32))
        idb = ent(nc.sbuf_tensor("idb", [128, 128], BF16))
        idf = ent(nc.sbuf_tensor("idf", [128, 128], F32))
        STAT = ent(nc.sbuf_tensor("STAT", [128, 64], F32))
        GATES = ent(nc.sbuf_tensor("GATES", [128, 128], F32))
        MASK = ent(nc.sbuf_tensor("MASK", [128, 16], F32))
        BB = ent(nc.sbuf_tensor("BB", [128, D], F32))
        bqp = ent(nc.sbuf_tensor("bqp", [128, 8], F32))
        ROW = ent(nc.sbuf_tensor("ROW", [1, D], F32))
        ones1 = ent(nc.sbuf_tensor("ones1", [1, 128], F32))
        _sd = ent(nc.semaphore("sd"))
        _sd2 = ent(nc.semaphore("sd2"))
        _sdo = ent(nc.semaphore("sdo"))
        _sv = ent(nc.semaphore("sv"))
        _ss = ent(nc.semaphore("ss"))
        _st = ent(nc.semaphore("st"))
        _sg = ent(nc.semaphore("sg"))
        _sd3 = ent(nc.semaphore("sd3"))
        _sd4 = ent(nc.semaphore("sd4"))
        env = dict(nc=nc, A=A, WB1=WB1, STG=(STG0, STG1), idb=idb, idf=idf,
                   STAT=STAT, GATES=GATES, MASK=MASK, BB=BB, bqp=bqp,
                   ROW=ROW, ones1=ones1, W=W, xin=xin, out=out,
                   sd=Sem(_sd, True), sd2=Sem(_sd2, True), sdo=Sem(_sdo, True),
                   sv=Sem(_sv), ss=Sem(_ss), st=Sem(_st), sg=Sem(_sg),
                   sd3=Sem(_sd3, True), sd4=Sem(_sd4, True))

        def fence_v(em):
            em.op("v", lambda v: v.tensor_copy(STAT[:, 61:62],
                                               STAT[:, 60:61]), env["sv"])

        def fence_s(em):
            em.op("s", lambda s: s.copy(STAT[:, 62:63], STAT[:, 60:61]),
                  env["ss"])
        env["fence_v"] = fence_v
        env["fence_s"] = fence_s

        em = mk_em(env)
        em.op("g", lambda g: g.memset(idb[:, :], 0.0))
        em.op("g", lambda g: g.affine_select(
            out=idb[:, :], in_=idb[:, :], compare_op=ALU.not_equal, fill=1.0,
            base=0, pattern=[[-1, 128]], channel_multiplier=1))
        em.op("g", lambda g: g.memset(idf[:, :], 0.0))
        em.op("g", lambda g: g.affine_select(
            out=idf[:, :], in_=idf[:, :], compare_op=ALU.not_equal, fill=1.0,
            base=0, pattern=[[-1, 128]], channel_multiplier=1))
        em.op("g", lambda g: g.memset(ones1[:, :], 1.0))
        with nc.Block(no_gpsimd_drain=True) as blk:
            em.flush(blk)

        with (
            nc.sbuf_tensor("X", [128, NT * D], F32) as X,
            nc.sbuf_tensor("B", [128, NT * D], BF16) as B,
        ):
            env["X"] = X
            env["B"] = B
            em = mk_em(env)
            em.op("y", lambda y: y.dma_start(
                out=X[:, :].rearrange("p (n m) -> p n m", n=NT),
                in_=xin.rearrange("(n p) m -> p n m", p=128)), env["sd"], 16)
            em.wait("y", env["sd"])
            with nc.Block(no_gpsimd_drain=True) as blk:
                em.flush(blk)

            if do_l0:
                build_l0(env)
            if do_l1:
                build_l1_part1(env)
            else:
                em = mk_em(env)
                em.op("y", lambda y: y.dma_start(
                    out=out.rearrange("(n p) m -> p n m", p=128),
                    in_=X[:, :].rearrange("p (n m) -> p n m", n=NT)),
                    env["sdo"], 16)
                with nc.Block(no_gpsimd_drain=True) as blk:
                    em.flush(blk)

        if do_l1:
            build_l1_moe(env)

        em = mk_em(env)
        em.wait("y", env["sdo"])
        with nc.Block(no_gpsimd_drain=True) as blk:
            em.flush(blk)

    return nc


def load_weight_dxe(env, em, src_ap, ncols, oview=None):
    nc = env["nc"]
    WB1, (STG0, STG1) = env["WB1"], env["STG"]
    sd, sv, ss = env["sd"], env["sv"], env["ss"]
    for dc in range(ND):
        stg = (STG0, STG1)[dc % 2]
        if dc >= 2:
            em.wait("y", sv if dc % 2 == 0 else ss)
        ov = (lambda a: a) if oview is None else oview
        em.op("y", lambda y, s=src_ap(dc), g=stg, nco=ncols, o=ov:
              y.dma_start(out=o(g[:, 0:nco]), in_=s), sd, 16)
        em.wait("v" if dc % 2 == 0 else "s", sd)
        eng = "v" if dc % 2 == 0 else "s"
        if eng == "v":
            em.op("v", lambda v, g=stg, dcc=dc, nco=ncols:
                  v.tensor_copy(WB1[:, dcc * nco:(dcc + 1) * nco], g[:, 0:nco]), sv)
        else:
            em.op("s", lambda s, g=stg, dcc=dc, nco=ncols:
                  s.copy(WB1[:, dcc * nco:(dcc + 1) * nco], g[:, 0:nco]), ss)


def emit_ln_stats(env, em, c, xn32, sq):
    X, STAT = env["X"], env["STAT"]
    sv, ss = env["sv"], env["ss"]
    xc = X[:, c * D:(c + 1) * D]
    em.op("v", lambda v: v.reduce_sum(STAT[:, 48:49], xc, axis=AX.X), sv)
    em.wait("v", sv)
    em.op("v", lambda v: v.tensor_scalar(
        STAT[:, c:c + 1], STAT[:, 48:49], 1.0 / D, None, op0=ALU.mult), sv)
    em.wait("v", sv)
    em.op("v", lambda v: v.tensor_scalar(
        xn32[:, :], xc, STAT[:, c:c + 1], None, op0=ALU.subtract), sv)
    em.wait("s", sv)
    em.op("s", lambda s: s.activation(sq[:, 0:D], xn32[:, :], AF.Square,
                                      accum_out=STAT[:, 50:51]), ss)
    em.wait("v", ss)
    em.op("v", lambda v: v.tensor_scalar(
        STAT[:, 51:52], STAT[:, 50:51], 1.0 / D, None, op0=ALU.mult), sv)
    em.wait("s", sv)
    em.op("s", lambda s: s.activation(STAT[:, 52:53], STAT[:, 51:52], AF.Ln,
                                      bias=f32(EPS)), ss)
    em.wait("s", ss)
    em.op("s", lambda s: s.activation(STAT[:, 16 + c:17 + c], STAT[:, 52:53],
                                      AF.Exp, scale=f32(-0.5)), ss)
    em.wait("v", ss)
    em.op("v", lambda v: v.tensor_scalar(
        xn32[:, :], xn32[:, :], STAT[:, 16 + c:17 + c], None, op0=ALU.mult), sv)
    em.wait("v", sv)


def build_l0(env):
    nc = env["nc"]
    A, B, X, WB1, STAT, MASK, BB, bqp, ROW, ones1, idf, idb = (
        env["A"], env["B"], env["X"], env["WB1"], env["STAT"], env["MASK"],
        env["BB"], env["bqp"], env["ROW"], env["ones1"], env["idf"], env["idb"])
    W = env["W"]
    sd, sv, ss, st, sg = env["sd"], env["sv"], env["ss"], env["st"], env["sg"]

    with ExitStack() as _es2:
        ent = _es2.enter_context
        srow = ent(nc.sbuf_tensor("srow", [1, T], F32))
        s_row = ent(nc.sbuf_tensor("s_row", [128, T], F32))
        XN32 = ent(nc.sbuf_tensor("XN32", [128, D], F32))
        SC = ent(nc.sbuf_tensor("SC", [128, D], F32))
        wrb = ent(nc.sbuf_tensor("wrb", [128, D], F32))
        em = mk_em(env)
        load_weight_dxe(env, em, lambda dc: W["wq0"][dc * 128:(dc + 1) * 128, :], D)
        em.op("y", lambda y: y.dma_start(out=bqp[:, :], in_=W["bq0t"]),
              sd, 16)
        em.op("y", lambda y: y.dma_start(out=ROW[:, :], in_=W["bq0"].rearrange("(a d) -> a d", a=1)),
              sd, 16)
        em.wait("y", sd)
        with nc.Block(no_gpsimd_drain=True) as blk:
            em.flush(blk)

        with nc.psum_tensor("bc_ps", [128, 512], F32) as bc_ps:
            em = mk_em(env)
            for sl in range(2):
                em.op("t", lambda t, s=sl: t.matmul(
                    bc_ps[:, :], ones1[:, :], ROW[:, s * 512:(s + 1) * 512],
                    start=True, stop=True), st)
                em.wait("v", st)
                em.op("v", lambda v, s=sl: v.tensor_copy(
                    BB[:, s * 512:(s + 1) * 512], bc_ps[:, :]), sv)
                em.wait("t", sv)
            with nc.Block(no_gpsimd_drain=True) as blk:
                em.flush(blk)

            em = mk_em(env)
            em.op("y", lambda y: y.dma_start(out=ROW[:, :], in_=W["wr0"].rearrange("(a d) -> a d", a=1)),
                  sd, 16)
            em.wait("t", sd)
            for sl in range(2):
                em.op("t", lambda t, s=sl: t.matmul(
                    bc_ps[:, :], ones1[:, :], ROW[:, s * 512:(s + 1) * 512],
                    start=True, stop=True), st)
                em.wait("v", st)
                em.op("v", lambda v, s=sl: v.tensor_copy(
                    wrb[:, s * 512:(s + 1) * 512], bc_ps[:, :]), sv)
                em.wait("t", sv)
            with nc.Block(no_gpsimd_drain=True) as blk:
                em.flush(blk)

        with ExitStack() as _es3:
            ent = _es3.enter_context
            XNF = ent(nc.sbuf_tensor("XNF", [128, D], BF16))
            tp0 = ent(nc.psum_tensor("tp0", [128, 128], F32))
            tp1 = ent(nc.psum_tensor("tp1", [128, 128], F32))
            qf0 = ent(nc.psum_tensor("qf0", [128, 128], F32))
            qf1 = ent(nc.psum_tensor("qf1", [128, 128], F32))
            qt0 = ent(nc.psum_tensor("qt0", [128, 512], F32))
            qt1 = ent(nc.psum_tensor("qt1", [128, 512], F32))
            for c in range(NT):
                em = mk_em(env)
                emit_ln_stats(env, em, c, XN32, s_row)
                em.op("v", lambda v: v.tensor_tensor(
                    s_row[:, 0:D], XN32[:, :], wrb[:, :], op=ALU.mult), sv)
                em.op("v", lambda v, cc=c: v.reduce_sum(
                    STAT[:, 32 + cc:33 + cc], s_row[:, 0:D], axis=AX.X), sv)
                em.wait("t", sv)
                tps = (tp0, tp1)
                for dc in range(ND):
                    if dc >= 2:
                        em.wait("t", sv)
                    em.op("t", lambda t, d=dc, p=tps[dc % 2]: t.transpose(
                        p[:, :], XN32[:, d * 128:(d + 1) * 128], idf[:, :]), st)
                    em.wait("v", st)
                    em.op("v", lambda v, d=dc, p=tps[dc % 2]: v.tensor_copy(
                        XNF[:, d * 128:(d + 1) * 128], p[:, :]), sv)
                env["fence_v"](em)
                em.wait("t", sv)
                qfs = (qf0, qf1)
                for i in range(ND):
                    ps = qfs[i % 2]
                    if i >= 2:
                        em.wait("t", sv, sv.n)
                    for dc in range(ND):
                        em.op("t", lambda t, ii=i, d=dc, p=ps: t.matmul(
                            p[:, :],
                            WB1[:, d * D + ii * 128:d * D + (ii + 1) * 128],
                            XNF[:, d * 128:(d + 1) * 128],
                            start=(d == 0), stop=(d == ND - 1)),
                            st if dc == ND - 1 else None)
                    em.wait("v", st)
                    em.op("v", lambda v, ii=i, cc=c, p=ps: v.tensor_scalar(
                        A[:, ii * T + cc * 128:ii * T + (cc + 1) * 128],
                        p[:, :], bqp[:, ii:ii + 1], None, op0=ALU.add), sv)
                qts = (qt0, qt1)
                for h in range(2):
                    ps = qts[h]
                    for dc in range(ND):
                        em.op("t", lambda t, hh=h, d=dc, p=ps: t.matmul(
                            p[:, :],
                            XNF[:, d * 128:(d + 1) * 128],
                            WB1[:, d * D + hh * 512:d * D + (hh + 1) * 512],
                            start=(d == 0), stop=(d == ND - 1)),
                            st if dc == ND - 1 else None)
                    em.wait("v", st)
                    em.op("v", lambda v, hh=h, cc=c, p=ps: v.tensor_tensor(
                        B[:, cc * D + hh * 512:cc * D + (hh + 1) * 512],
                        p[:, :], BB[:, hh * 512:(hh + 1) * 512], op=ALU.add), sv)
                with nc.Block(no_gpsimd_drain=True) as blk:
                    em.flush(blk)

        with (
            nc.psum_tensor("trp", [1, 128], F32) as trp,
            nc.psum_tensor("bcp", [128, 512], F32) as bcp,
        ):
            em = mk_em(env)
            for c in range(NT):
                em.op("t", lambda t, cc=c: t.transpose(
                    trp[:, :], STAT[:, 32 + cc:33 + cc], idf[:, :]), st)
                em.wait("v", st)
                em.op("v", lambda v, cc=c: v.tensor_copy(
                    srow[:, cc * 128:(cc + 1) * 128], trp[:, :]), sv)
                em.wait("t", sv)
            with nc.Block(no_gpsimd_drain=True) as blk:
                em.flush(blk)
            em = mk_em(env)
            for sl in range(4):
                em.op("t", lambda t, s=sl: t.matmul(
                    bcp[:, :], ones1[:, :], srow[:, s * 512:(s + 1) * 512],
                    start=True, stop=True), st)
                em.wait("v", st)
                em.op("v", lambda v, s=sl: v.tensor_copy(
                    s_row[:, s * 512:(s + 1) * 512], bcp[:, :]), sv)
                em.wait("t", sv)
            with nc.Block(no_gpsimd_drain=True) as blk:
                em.flush(blk)
            em = mk_em(env)
            for c in range(NT):
                for sl in range(4):
                    em.op("v", lambda v, cc=c, s=sl: v.tensor_scalar(
                        SC[:, 0:512], s_row[:, s * 512:(s + 1) * 512],
                        STAT[:, 32 + cc:33 + cc], 0.0, op0=ALU.is_gt,
                        op1=ALU.add, accum_out=STAT[:, 53 + s:54 + s]), sv)
                    em.wait("v", sv)
                em.op("v", lambda v: v.tensor_tensor(
                    STAT[:, 53:54], STAT[:, 53:54], STAT[:, 54:55], op=ALU.add), sv)
                em.wait("v", sv)
                em.op("v", lambda v: v.tensor_tensor(
                    STAT[:, 55:56], STAT[:, 55:56], STAT[:, 56:57], op=ALU.add), sv)
                em.wait("v", sv)
                em.op("v", lambda v: v.tensor_tensor(
                    STAT[:, 53:54], STAT[:, 53:54], STAT[:, 55:56], op=ALU.add), sv)
                em.wait("v", sv)
                em.op("v", lambda v, cc=c: v.tensor_scalar(
                    MASK[:, cc:cc + 1], STAT[:, 53:54], float(KSEL), None,
                    op0=ALU.is_lt), sv)
                em.wait("v", sv)
            with nc.Block(no_gpsimd_drain=True) as blk:
                em.flush(blk)

        if os.environ.get("MK_DBG6"):
            qdbg6 = nc.dram_tensor("qdbg6", [128, NT * D], F32,
                                   kind="ExternalOutput").ap()
            bdbg6 = nc.dram_tensor("bdbg6", [128, NT * D], F32,
                                   kind="ExternalOutput").ap()
            wdbg6 = nc.dram_tensor("wdbg6", [128, ND * D], F32,
                                   kind="ExternalOutput").ap()
            em = mk_em(env)
            for seg in range(ND):
                em.op("v", lambda v, s2=seg: v.tensor_copy(
                    XN32[:, :], WB1[:, s2 * D:(s2 + 1) * D]), sv)
                em.wait("y", sv)
                em.op("y", lambda y, s2=seg: y.dma_start(
                    out=wdbg6[:, s2 * D:(s2 + 1) * D], in_=XN32[:, :]),
                    env["sdo"], 16)
                em.wait("v", env["sdo"])
            for seg in range(NT):
                em.op("v", lambda v, s2=seg: v.tensor_copy(
                    XN32[:, :], A[:, s2 * D:(s2 + 1) * D]), sv)
                em.wait("y", sv)
                em.op("y", lambda y, s2=seg: y.dma_start(
                    out=qdbg6[:, s2 * D:(s2 + 1) * D], in_=XN32[:, :]),
                    env["sdo"], 16)
                em.wait("v", env["sdo"])
                em.op("v", lambda v, s2=seg: v.tensor_copy(
                    XN32[:, :], B[:, s2 * D:(s2 + 1) * D]), sv)
                em.wait("y", sv)
                em.op("y", lambda y, s2=seg: y.dma_start(
                    out=bdbg6[:, s2 * D:(s2 + 1) * D], in_=XN32[:, :]),
                    env["sdo"], 16)
                em.wait("v", env["sdo"])
            with nc.Block(no_gpsimd_drain=True) as blk:
                em.flush(blk)

        em = mk_em(env)
        load_weight_dxe(env, em, lambda dc: W["wb0"][dc * 128:(dc + 1) * 128, :], D)
        em.wait("y", env["sd"])
        with nc.Block(no_gpsimd_drain=True) as blk:
            em.flush(blk)

        with ExitStack() as _es4:
            ent = _es4.enter_context
            Pb = ent(nc.sbuf_tensor("Pb", [128, T], BF16))
            PT = ent(nc.sbuf_tensor("PT", [128, 512], BF16))
            Pg = ent(nc.sbuf_tensor("Pg", [128, D], F32))
            XNF3 = ent(nc.sbuf_tensor("XNF3", [128, D], BF16))
            scp0 = ent(nc.psum_tensor("scp0", [128, 512], F32))
            scp1 = ent(nc.psum_tensor("scp1", [128, 512], F32))
            ptp0 = ent(nc.psum_tensor("ptp0", [128, 128], BF16))
            ptp1 = ent(nc.psum_tensor("ptp1", [128, 128], BF16))
            pvp0 = ent(nc.psum_tensor("pvp0", [128, 512], F32))
            pvp1 = ent(nc.psum_tensor("pvp1", [128, 512], F32))
            tpf0 = ent(nc.psum_tensor("tpf0", [128, 128], F32))
            tpf1 = ent(nc.psum_tensor("tpf1", [128, 128], F32))
            pgp0, pgp1 = scp0, scp1
            for c in range(NT):
                em = mk_em(env)
                scs = (scp0, scp1)
                AB2 = os.environ.get("MK_AB", "") == "2"
                for sl in (range(4) if not AB2 else []):
                    ps = scs[sl % 2]
                    if sl >= 2:
                        em.wait("t", sv)
                    for i in range(ND):
                        em.op("t", lambda t, s=sl, ii=i, p=ps: t.matmul(
                            p[:, :],
                            A[:, ii * T + c * 128:ii * T + (c + 1) * 128],
                            A[:, ii * T + s * 512:ii * T + (s + 1) * 512],
                            start=(ii == 0), stop=(ii == ND - 1)),
                            st if i == ND - 1 else None)
                    em.wait("v", st)
                    em.op("v", lambda v, s=sl, p=ps: v.tensor_copy(
                        s_row[:, s * 512:(s + 1) * 512], p[:, :]), sv)
                if AB2:
                    em.op("v", lambda v: v.memset(SC[:, :], 1.0), sv)
                em.wait("v", sv)
                if not AB2:
                    em.op("v", lambda v: v.reduce_max(
                        STAT[:, 48:49], s_row[:, :], axis=AX.X, negate=True), sv)
                    em.wait("v", sv)
                    em.op("v", lambda v: v.tensor_scalar(
                        STAT[:, 49:50], STAT[:, 48:49], 1.0 / 32.0, None,
                        op0=ALU.mult), sv)
                    em.wait("s", sv)
                    em.op("s", lambda s: s.activation(
                        Pb[:, :], s_row[:, :], AF.Exp, bias=STAT[:, 49:50],
                        scale=f32(1.0 / 32.0), accum_out=STAT[:, 50:51]), ss)
                    em.wait("v", ss)
                    em.op("v", lambda v: v.reciprocal(STAT[:, 51:52], STAT[:, 50:51]),
                          sv)
                    pts = (ptp0, ptp1)
                    env["fence_s"](em)
                    em.wait("t", ss)
                    for jb in range(4):
                        for jo in range(4):
                            j = jb * 4 + jo
                            if j >= 2:
                                em.wait("t", ss)
                            em.op("t", lambda t, jj=j, p=pts[j % 2]: t.transpose(
                                p[:, :], Pb[:, jj * 128:(jj + 1) * 128],
                                idb[:, :]), st)
                            em.wait("s", st)
                            em.op("s", lambda s, jj=j, o=jo, p=pts[j % 2]: s.copy(
                                PT[:, o * 128:(o + 1) * 128], p[:, :]), ss)
                        env["fence_s"](em)
                        em.wait("t", ss)
                        for jo in range(4):
                            j = jb * 4 + jo
                            for h in range(2):
                                em.op("t", lambda t, jj=j, o=jo, hh=h,
                                      p=(pvp0, pvp1)[h]: t.matmul(
                                    p[:, :],
                                    PT[:, o * 128:(o + 1) * 128],
                                    B[:, jj * D + hh * 512:jj * D + (hh + 1) * 512],
                                    start=(jj == 0), stop=(jj == NT - 1)),
                                    st if (j == NT - 1 and h == 1) else None)
                    em.wait("v", st)
                    em.wait("v", sv, sv.n)
                    for h in range(2):
                        em.op("v", lambda v, hh=h, p=(pvp0, pvp1)[h]: v.tensor_scalar(
                            SC[:, hh * 512:(hh + 1) * 512], p[:, :],
                            STAT[:, 51:52], None, op0=ALU.mult), sv)
                em.op("v", lambda v: v.tensor_scalar(
                    XN32[:, :], X[:, c * D:(c + 1) * D], STAT[:, c:c + 1], None,
                    op0=ALU.subtract), sv)
                em.wait("v", sv)
                em.op("v", lambda v: v.tensor_scalar(
                    XN32[:, :], XN32[:, :], STAT[:, 16 + c:17 + c], None,
                    op0=ALU.mult), sv)
                AB = os.environ.get("MK_AB", "")
                em.wait("t", sv)
                tfs = (tpf0, tpf1)
                for dc in (range(ND) if AB not in ("1", "2") else []):
                    if dc >= 2:
                        em.wait("t", ss)
                    em.op("t", lambda t, d=dc, p=tfs[dc % 2]: t.transpose(
                        p[:, 0:128], XN32[:, d * 128:(d + 1) * 128],
                        idf[:, :]), st)
                    em.wait("s", st)
                    em.op("s", lambda s, d=dc, p=tfs[dc % 2]: s.copy(
                        XNF3[:, d * 128:(d + 1) * 128], p[:, 0:128]), ss)
                env["fence_s"](em)
                em.wait("t", ss)
                pgs = (pgp0, pgp1)
                for h in (range(2) if AB not in ("1", "2") else []):
                    for dc in range(ND):
                        em.op("t", lambda t, hh=h, d=dc, p=pgs[h]: t.matmul(
                            p[:, :],
                            XNF3[:, d * 128:(d + 1) * 128],
                            WB1[:, d * D + hh * 512:d * D + (hh + 1) * 512],
                            start=(d == 0), stop=(d == ND - 1)),
                            st if dc == ND - 1 else None)
                    em.wait("s", st)
                    em.op("s", lambda s, hh=h, p=pgs[h]: s.copy(
                        Pg[:, hh * 512:(hh + 1) * 512], p[:, :]), ss)
                if AB in ("1", "2"):
                    em.op("v", lambda v: v.tensor_copy(
                        s_row[:, D:2 * D], XN32[:, :]), sv)
                em.wait("s", ss)
                if AB not in ("1", "2"):
                    em.op("s", lambda s: s.activation(
                        s_row[:, 0:D], Pg[:, :], AF.Square), ss)
                em.wait("v", ss)
                if AB not in ("1", "2"):
                    em.op("v", lambda v: v.tensor_scalar(
                        s_row[:, 0:D], s_row[:, 0:D], GC1 * GC2, GC1,
                        op0=ALU.mult, op1=ALU.add), sv)
                    em.op("v", lambda v: v.tensor_tensor(
                        s_row[:, D:2 * D], s_row[:, 0:D], Pg[:, :], op=ALU.mult), sv)
                em.wait("s", sv)
                if AB not in ("1", "2"):
                    em.op("s", lambda s: s.activation(
                        s_row[:, 0:D], s_row[:, D:2 * D], AF.Tanh), ss)
                em.wait("v", ss)
                if AB not in ("1", "2"):
                    em.op("v", lambda v: v.scalar_tensor_tensor(
                        s_row[:, D:2 * D], s_row[:, 0:D], 1.0, Pg[:, :],
                        op0=ALU.add, op1=ALU.mult), sv)
                    em.op("v", lambda v: v.tensor_scalar(
                        s_row[:, D:2 * D], s_row[:, D:2 * D], 0.5, None,
                        op0=ALU.mult), sv)
                    em.op("v", lambda v: v.tensor_tensor(
                        Pg[:, :], s_row[:, D:2 * D], XN32[:, :], op=ALU.subtract), sv)
                    em.op("v", lambda v, cc=c: v.tensor_scalar(
                        Pg[:, :], Pg[:, :], MASK[:, cc:cc + 1], None,
                        op0=ALU.mult), sv)
                    em.op("v", lambda v: v.tensor_tensor(
                        s_row[:, D:2 * D], XN32[:, :], Pg[:, :], op=ALU.add), sv)
                em.wait("s", sv)
                em.op("s", lambda s: s.activation(
                    s_row[:, 0:D], s_row[:, D:2 * D], AF.Tanh,
                    scale=f32(0.5)), ss)
                em.wait("v", ss)
                em.op("v", lambda v: v.tensor_scalar(
                    s_row[:, 0:D], s_row[:, 0:D], 0.5, 0.5,
                    op0=ALU.mult, op1=ALU.add), sv)
                em.op("v", lambda v: v.tensor_tensor(
                    s_row[:, 0:D], s_row[:, 0:D], s_row[:, D:2 * D],
                    op=ALU.mult), sv)
                em.op("v", lambda v: v.tensor_tensor(
                    SC[:, :], SC[:, :], s_row[:, 0:D], op=ALU.mult), sv)
                em.op("v", lambda v, cc=c: v.tensor_tensor(
                    X[:, cc * D:(cc + 1) * D], X[:, cc * D:(cc + 1) * D],
                    SC[:, :], op=ALU.add), sv)
                with nc.Block(no_gpsimd_drain=True) as blk:
                    em.flush(blk)


def build_l1_part1(env):
    nc = env["nc"]
    A, B, X, WB1, STAT, GATES, BB, bqp, ROW, ones1, idf, idb = (
        env["A"], env["B"], env["X"], env["WB1"], env["STAT"], env["GATES"],
        env["BB"], env["bqp"], env["ROW"], env["ones1"], env["idf"], env["idb"])
    W = env["W"]
    sd, sd2, sdo, sv, ss, st, sg = (env["sd"], env["sd2"], env["sdo"],
                                    env["sv"], env["ss"], env["st"], env["sg"])

    with ExitStack() as _es5:
        ent = _es5.enter_context
        Kp = ent(nc.sbuf_tensor("Kp", [128, 2304], BF16))
        Vt = ent(nc.sbuf_tensor("Vt", [128, 2304], BF16))
        WKB = ent(nc.sbuf_tensor("WKB", [128, D], BF16))
        WVB = ent(nc.sbuf_tensor("WVB", [128, D], BF16))
        MRH = ent(nc.sbuf_tensor("MRH", [128, 64], BF16))
        MRL = ent(nc.sbuf_tensor("MRL", [128, 64], BF16))
        MRS = ent(nc.sbuf_tensor("MRS", [128, 64], F32))
        bqp1 = ent(nc.sbuf_tensor("bqp1", [128, 8], F32))
        bkvp = ent(nc.sbuf_tensor("bkvp", [128, 2], F32))
        em = mk_em(env)
        load_weight_dxe(env, em,
                        lambda dc: W["wq1"][dc * 128:(dc + 1) * 128, :], D)
        em.op("y", lambda y: y.dma_start(out=bqp1[:, :], in_=W["bq1"]), sd, 16)
        em.op("y", lambda y: y.dma_start(out=bkvp[:, 0:1], in_=W["bk1"]),
              sd, 16)
        em.op("y", lambda y: y.dma_start(out=bkvp[:, 1:2], in_=W["bv1"]),
              sd, 16)
        em.wait("g", sv)
        em.wait("g", ss)
        em.op("g", lambda g: g.dma_start(
            out=env["STG"][0][:, :], in_=W["wk1"]), sd2, 16)
        em.wait("v", sd2)
        em.op("v", lambda v: v.tensor_copy(WKB[:, :], env["STG"][0][:, :]), sv)
        em.op("g", lambda g: g.dma_start(
            out=env["STG"][1][:, :], in_=W["wv1"]), sd2, 16)
        em.wait("v", sd2)
        em.op("v", lambda v: v.tensor_copy(WVB[:, :], env["STG"][1][:, :]), sv)
        em.op("g", lambda g: g.dma_start(out=MRS[:, :], in_=W["mr1"]), sd2, 16)
        em.wait("v", sd2)
        em.op("v", lambda v: v.tensor_copy(MRH[:, :], MRS[:, :]), sv)
        em.wait("v", sv)
        em.op("v", lambda v: v.tensor_copy(env["STG"][0][:, 0:64], MRH[:, :]), sv)
        em.wait("v", sv)
        em.op("v", lambda v: v.tensor_tensor(
            MRS[:, :], MRS[:, :], env["STG"][0][:, 0:64], op=ALU.subtract), sv)
        em.wait("v", sv)
        em.op("v", lambda v: v.tensor_copy(MRL[:, :], MRS[:, :]), sv)
        em.op("g", lambda g: g.memset(Kp[:, 0:128], 0.0))
        em.op("g", lambda g: g.memset(Kp[:, 2176:2304], 0.0))
        em.op("g", lambda g: g.memset(Vt[:, 0:128], 0.0))
        em.op("g", lambda g: g.memset(Vt[:, 2176:2304], 0.0))
        em.wait("y", sd)
        em.wait("g", sd2)
        with nc.Block(no_gpsimd_drain=True) as blk:
            em.flush(blk)

        with ExitStack() as _es6:
            ent = _es6.enter_context
            XN32 = ent(nc.sbuf_tensor("XN32L1", [128, D], F32))
            XNB = ent(nc.sbuf_tensor("XNB", [128, D], BF16))
            HI32 = ent(nc.sbuf_tensor("HI32", [128, D], F32))
            LOB = ent(nc.sbuf_tensor("LOB", [128, D], BF16))
            LOF = ent(nc.sbuf_tensor("LOF", [128, D], BF16))
            VFC = ent(nc.sbuf_tensor("VFC", [128, 128], BF16))
            GS = ent(nc.sbuf_tensor("GS", [128, 48], F32))
            txp0 = ent(nc.psum_tensor("txp0", [128, 128], BF16))
            txp1 = ent(nc.psum_tensor("txp1", [128, 128], BF16))
            lgp = ent(nc.psum_tensor("lgp", [128, 8], F32))
            qfp0 = ent(nc.psum_tensor("qfp0", [128, 128], F32))
            qfp1 = ent(nc.psum_tensor("qfp1", [128, 128], F32))
            kvp0 = ent(nc.psum_tensor("kvp0", [128, 128], F32))
            kvp1 = ent(nc.psum_tensor("kvp1", [128, 128], F32))
            vtp = ent(nc.psum_tensor("vtp", [128, 128], BF16))
            for c in range(NT):
                em = mk_em(env)
                emit_ln_stats(env, em, c, XN32, HI32)
                em.op("v", lambda v: v.tensor_copy(XNB[:, :], XN32[:, :]), sv)
                em.wait("v", sv)
                em.op("v", lambda v: v.tensor_copy(HI32[:, :], XNB[:, :]), sv)
                em.wait("v", sv)
                em.op("v", lambda v: v.tensor_tensor(
                    XN32[:, :], XN32[:, :], HI32[:, :], op=ALU.subtract), sv)
                em.wait("v", sv)
                em.op("v", lambda v: v.tensor_copy(LOB[:, :], XN32[:, :]), sv)
                em.op("y", lambda y, cc=c: y.dma_start(
                    out=W["x1d"].rearrange("(n p) m -> p n m", p=128)[:, cc, :],
                    in_=X[:, cc * D:(cc + 1) * D]), sdo, 16)
                em.wait("t", sv)
                txs = (txp0, txp1)
                for dc in range(ND):
                    if dc >= 2:
                        em.wait("t", ss)
                    em.op("t", lambda t, d=dc, p=txs[dc % 2]: t.transpose(
                        p[:, :], XNB[:, d * 128:(d + 1) * 128], idb[:, :]), st)
                    em.wait("s", st)
                    em.op("s", lambda s, d=dc, cc=c, p=txs[dc % 2]: s.copy(
                        A[:, d * T + cc * 128:d * T + (cc + 1) * 128], p[:, :]),
                        ss)
                for dc in range(ND):
                    em.wait("t", ss)
                    em.op("t", lambda t, d=dc, p=txs[dc % 2]: t.transpose(
                        p[:, :], LOB[:, d * 128:(d + 1) * 128], idb[:, :]), st)
                    em.wait("s", st)
                    em.op("s", lambda s, d=dc, p=txs[dc % 2]: s.copy(
                        LOF[:, d * 128:(d + 1) * 128], p[:, :]), ss)
                env["fence_s"](em)
                em.wait("t", ss)
                for pi in range(3):
                    for dc in range(ND):
                        if pi == 0:
                            lh, rh = (
                                A[:, dc * T + c * 128:dc * T + (c + 1) * 128],
                                MRH[:, dc * 8:(dc + 1) * 8])
                        elif pi == 1:
                            lh, rh = (LOF[:, dc * 128:(dc + 1) * 128],
                                      MRH[:, dc * 8:(dc + 1) * 8])
                        else:
                            lh, rh = (
                                A[:, dc * T + c * 128:dc * T + (c + 1) * 128],
                                MRL[:, dc * 8:(dc + 1) * 8])
                        em.op("t", lambda t, l=lh, r=rh, pp=pi, d=dc: t.matmul(
                            lgp[:, :], l, r,
                            start=(pp == 0 and d == 0),
                            stop=(pp == 2 and d == ND - 1)),
                            st if (pi == 2 and dc == ND - 1) else None)
                em.wait("v", st)
                em.op("v", lambda v: v.tensor_copy(GS[:, 0:8], lgp[:, :]), sv)
                em.wait("v", sv)
                em.op("v", lambda v: v.reduce_max(GS[:, 8:9], GS[:, 0:8],
                                                  axis=AX.X), sv)
                em.wait("v", sv)
                em.op("v", lambda v: v.tensor_scalar(
                    GS[:, 16:24], GS[:, 0:8], GS[:, 8:9], None, op0=ALU.is_ge),
                    sv)
                em.wait("v", sv)
                em.op("v", lambda v: v.scalar_tensor_tensor(
                    GS[:, 24:32], GS[:, 16:24], NEG, GS[:, 0:8],
                    op0=ALU.mult, op1=ALU.add), sv)
                em.wait("v", sv)
                em.op("v", lambda v: v.reduce_max(GS[:, 9:10], GS[:, 24:32],
                                                  axis=AX.X), sv)
                em.wait("v", sv)
                em.op("v", lambda v: v.tensor_scalar(
                    GS[:, 10:11], GS[:, 8:9], -1.0, None, op0=ALU.mult), sv)
                em.wait("s", sv)
                em.op("s", lambda s: s.activation(
                    GS[:, 32:40], GS[:, 0:8], AF.Exp, bias=GS[:, 10:11]), ss)
                em.op("s", lambda s: s.activation(
                    GS[:, 11:12], GS[:, 9:10], AF.Exp, bias=GS[:, 10:11]), ss)
                em.wait("v", ss)
                em.op("v", lambda v: v.tensor_scalar(
                    GS[:, 12:13], GS[:, 11:12], 1.0, None, op0=ALU.add), sv)
                em.wait("v", sv)
                em.op("v", lambda v: v.reciprocal(GS[:, 13:14], GS[:, 12:13]), sv)
                em.wait("v", sv)
                em.op("v", lambda v: v.tensor_scalar(
                    GS[:, 40:48], GS[:, 0:8], GS[:, 9:10], None, op0=ALU.is_ge),
                    sv)
                em.wait("v", sv)
                em.op("v", lambda v: v.tensor_tensor(
                    GS[:, 40:48], GS[:, 40:48], GS[:, 32:40], op=ALU.mult), sv)
                em.wait("v", sv)
                em.op("v", lambda v, cc=c: v.tensor_scalar(
                    GATES[:, cc * 8:(cc + 1) * 8], GS[:, 40:48], GS[:, 13:14],
                    None, op0=ALU.mult), sv)
                qfs = (qfp0, qfp1)
                for i in range(ND):
                    ps = qfs[i % 2]
                    if i >= 2:
                        em.wait("t", sv)
                    for dc in range(ND):
                        em.op("t", lambda t, ii=i, d=dc, cc=c, p=ps: t.matmul(
                            p[:, :],
                            WB1[:, d * D + ii * 128:d * D + (ii + 1) * 128],
                            A[:, d * T + cc * 128:d * T + (cc + 1) * 128],
                            start=(d == 0), stop=(d == ND - 1)),
                            st if dc == ND - 1 else None)
                    em.wait("v", st)
                    em.op("v", lambda v, ii=i, cc=c, p=ps: v.tensor_scalar(
                        B[:, ii * T + cc * 128:ii * T + (cc + 1) * 128],
                        p[:, :], bqp1[:, ii:ii + 1], None, op0=ALU.add), sv)
                kvs = (kvp0, kvp1)
                for dc in range(ND):
                    em.op("t", lambda t, d=dc, cc=c: t.matmul(
                        kvs[0][:, :], WKB[:, d * 128:(d + 1) * 128],
                        A[:, d * T + cc * 128:d * T + (cc + 1) * 128],
                        start=(d == 0), stop=(d == ND - 1)),
                        st if dc == ND - 1 else None)
                em.wait("v", st)
                em.op("v", lambda v, cc=c: v.tensor_scalar(
                    Kp[:, 128 + cc * 128:128 + (cc + 1) * 128], kvs[0][:, :],
                    bkvp[:, 0:1], None, op0=ALU.add), sv)
                for dc in range(ND):
                    em.op("t", lambda t, d=dc, cc=c: t.matmul(
                        kvs[1][:, :], WVB[:, d * 128:(d + 1) * 128],
                        A[:, d * T + cc * 128:d * T + (cc + 1) * 128],
                        start=(d == 0), stop=(d == ND - 1)),
                        st if dc == ND - 1 else None)
                em.wait("v", st)
                em.op("v", lambda v: v.tensor_scalar(
                    VFC[:, :], kvs[1][:, :], bkvp[:, 1:2], None, op0=ALU.add),
                    sv)
                em.op("v", lambda v: v.tensor_copy(STAT[:, 61:62],
                                                   STAT[:, 60:61]), sv)
                em.wait("t", sv)
                em.op("t", lambda t: t.transpose(vtp[:, :], VFC[:, :],
                                                 idb[:, :]), st)
                em.wait("v", st)
                em.op("v", lambda v, cc=c: v.tensor_copy(
                    Vt[:, 128 + cc * 128:128 + (cc + 1) * 128], vtp[:, :]), sv)
                with nc.Block(no_gpsimd_drain=True) as blk:
                    em.flush(blk)

        if os.environ.get("MK_DBG2"):
            qdbg = nc.dram_tensor("qdbg", [128, NT * D], F32,
                                  kind="ExternalOutput").ap()
            kdbg = nc.dram_tensor("kdbg", [128, 2304], F32,
                                  kind="ExternalOutput").ap()
            vdbg = nc.dram_tensor("vdbg", [128, 2304], F32,
                                  kind="ExternalOutput").ap()
            adbg = nc.dram_tensor("adbg", [128, NT * D], F32,
                                  kind="ExternalOutput").ap()
            wdbg = nc.dram_tensor("wdbg", [128, 2 * D], F32,
                                  kind="ExternalOutput").ap()
            em = mk_em(env)
            em.op("v", lambda v: v.tensor_copy(env["STG"][0][:, :],
                                               WKB[:, :]), sv)
            em.op("v", lambda v: v.tensor_copy(env["STG"][1][:, :],
                                               WB1[:, 0:D]), sv)
            em.wait("y", sv)
            em.op("y", lambda y: y.dma_start(out=wdbg[:, 0:D],
                                             in_=env["STG"][0][:, :]),
                  env["sdo"], 16)
            em.op("y", lambda y: y.dma_start(out=wdbg[:, D:2 * D],
                                             in_=env["STG"][1][:, :]),
                  env["sdo"], 16)
            em.wait("y", env["sdo"])
            with nc.Block(no_gpsimd_drain=True) as blk:
                em.flush(blk)
            em = mk_em(env)
            for seg in range(NT):
                em.op("v", lambda v, s2=seg: v.tensor_copy(
                    env["STG"][0][:, :], A[:, s2 * D:(s2 + 1) * D]), sv)
                em.wait("y", sv)
                em.op("y", lambda y, s2=seg: y.dma_start(
                    out=adbg[:, s2 * D:(s2 + 1) * D],
                    in_=env["STG"][0][:, :]), env["sdo"], 16)
                em.wait("v", env["sdo"])
            with nc.Block(no_gpsimd_drain=True) as blk:
                em.flush(blk)
            em = mk_em(env)
            for seg in range(NT):
                em.op("v", lambda v, s2=seg: v.tensor_copy(
                    env["STG"][0][:, :], B[:, s2 * D:(s2 + 1) * D]), sv)
                em.wait("y", sv)
                em.op("y", lambda y, s2=seg: y.dma_start(
                    out=qdbg[:, s2 * D:(s2 + 1) * D],
                    in_=env["STG"][0][:, :]), env["sdo"], 16)
                em.wait("v", env["sdo"])
            for seg in range(3):
                w = 768
                em.op("v", lambda v, s2=seg, w2=w: v.tensor_copy(
                    env["STG"][0][:, 0:w2], Kp[:, s2 * w2:(s2 + 1) * w2]), sv)
                em.wait("y", sv)
                em.op("y", lambda y, s2=seg, w2=w: y.dma_start(
                    out=kdbg[:, s2 * w2:(s2 + 1) * w2],
                    in_=env["STG"][0][:, 0:w2]), env["sdo"], 16)
                em.wait("v", env["sdo"])
                em.op("v", lambda v, s2=seg, w2=w: v.tensor_copy(
                    env["STG"][0][:, 0:w2], Vt[:, s2 * w2:(s2 + 1) * w2]), sv)
                em.wait("y", sv)
                em.op("y", lambda y, s2=seg, w2=w: y.dma_start(
                    out=vdbg[:, s2 * w2:(s2 + 1) * w2],
                    in_=env["STG"][0][:, 0:w2]), env["sdo"], 16)
                em.wait("v", env["sdo"])
            with nc.Block(no_gpsimd_drain=True) as blk:
                em.flush(blk)

        em = mk_em(env)
        load_weight_dxe(env, em,
                        lambda dc: W["wo1"][dc * 128:(dc + 1) * 128, :], D)
        em.op("y", lambda y: y.dma_start(out=ROW[:, :], in_=W["bo1"].rearrange("(a d) -> a d", a=1)),
              sd, 16)
        em.wait("y", sd)
        with nc.Block(no_gpsimd_drain=True) as blk:
            em.flush(blk)
        with nc.psum_tensor("bc_psL1", [128, 512], F32) as bc_ps:
            em = mk_em(env)
            for sl in range(2):
                em.op("t", lambda t, s=sl: t.matmul(
                    bc_ps[:, :], ones1[:, :], ROW[:, s * 512:(s + 1) * 512],
                    start=True, stop=True), st)
                em.wait("v", st)
                em.op("v", lambda v, s=sl: v.tensor_copy(
                    BB[:, s * 512:(s + 1) * 512], bc_ps[:, :]), sv)
                em.wait("t", sv)
            with nc.Block(no_gpsimd_drain=True) as blk:
                em.flush(blk)

        with ExitStack() as _est:
            ent = _est.enter_context
            BAND = ent(nc.sbuf_tensor("BAND", [128, 1024], F32))
            S5 = ent(nc.sbuf_tensor("S5", [128, 1024], F32))
            E5 = ent(nc.sbuf_tensor("E5", [128, 1024], F32))
            Pb5 = ent(nc.sbuf_tensor("Pb5", [128, 1024], BF16))
            PT5 = ent(nc.sbuf_tensor("PT5", [128, 1024], BF16))
            OC = ent(nc.sbuf_tensor("OC", [128, 2 * D], BF16))
            S1C = ent(nc.sbuf_tensor("S1C", [128, 2 * D], F32))
            scp0 = ent(nc.psum_tensor("scp0L1", [128, 512], F32))
            scp1 = ent(nc.psum_tensor("scp1L1", [128, 512], F32))
            ptp0 = ent(nc.psum_tensor("ptp0L1", [128, 512], BF16))
            ptp1 = ent(nc.psum_tensor("ptp1L1", [128, 512], BF16))
            pvp0 = ent(nc.psum_tensor("pvp0L1", [64, 128], F32))
            pvp1 = ent(nc.psum_tensor("pvp1L1", [64, 128], F32))
            s1p0 = ent(nc.psum_tensor("s1p0L1", [128, 512], F32))
            s1p1 = ent(nc.psum_tensor("s1p1L1", [128, 512], F32))

            em = mk_em(env)
            for r in range(2):
                em.op("g", lambda g, rr=r: g.memset(
                    BAND[:, rr * 512:(rr + 1) * 512], 0.0))
                em.op("g", lambda g, rr=r: g.affine_select(
                    out=BAND[:, rr * 512:(rr + 1) * 512],
                    in_=BAND[:, rr * 512:(rr + 1) * 512],
                    compare_op=ALU.is_ge, fill=NEG,
                    base=-128 * rr, pattern=[[1, 512]], channel_multiplier=-1))
                em.op("g", lambda g, rr=r: g.affine_select(
                    out=BAND[:, rr * 512:(rr + 1) * 512],
                    in_=BAND[:, rr * 512:(rr + 1) * 512],
                    compare_op=ALU.is_ge, fill=NEG,
                    base=128 * rr + 255, pattern=[[-1, 512]],
                    channel_multiplier=1))
            with nc.Block(no_gpsimd_drain=True) as blk:
                em.flush(blk)

            em = mk_em(env)
            scs = (scp0, scp1)
            pts = (ptp0, ptp1)
            pvs = (pvp0, pvp1)
            maskmax_done = {0: 0, 1: 0}
            exp_done = {0: 0, 1: 0}
            pb_done = {0: 0, 1: 0}
            ptc_done = {0: 0, 1: 0}
            pv_done = {0: 0, 1: 0}
            ocevac_done = {0: 0, 1: 0}
            s1evac_done = {0: 0, 1: 0}
            spill_done = {0: 0, 1: 0}
            NH = NT * 16

            def emit_scores(k):
                c2 = k // 16
                h2 = k % 16
                par2 = k % 2
                g2 = h2 // 8
                i2 = h2 % 8
                n2 = c2 // 2
                if maskmax_done[par2]:
                    em.wait("t", sv, maskmax_done[par2])
                em.op("t", lambda t, gg=g2, ii=i2, p=scs[par2], cc=c2, nn=n2:
                      t.matmul(
                    p[:, :],
                    B[64 * gg:64 * (gg + 1),
                      ii * T + cc * 128:ii * T + (cc + 1) * 128],
                    Kp[64 * gg:64 * (gg + 1), 256 * nn:256 * nn + 512],
                    start=True, stop=True, tile_position=(64 * gg, 0)), st)
                scores_done[par2] = st.n

            scores_done = {0: 0, 1: 0}
            pbg_done = {0: 0, 1: 0}
            emit_scores(0)
            for k in range(NH):
                c = k // 16
                h = k % 16
                par = k % 2
                g = h // 8
                i = h % 8
                n = c // 2
                r = c % 2
                cpar = c % 2
                my_scores = scores_done[par]
                if k + 1 < NH:
                    emit_scores(k + 1)
                em.wait("v", st, my_scores)
                if exp_done[par]:
                    em.wait("v", ss, exp_done[par])
                em.op("v", lambda v, p=scs[par], rr=r, q=par:
                      v.tensor_tensor(
                    S5[:, q * 512:(q + 1) * 512], p[:, :],
                    BAND[:, rr * 512:(rr + 1) * 512], op=ALU.add), sv)
                em.wait("v", sv)
                em.op("v", lambda v, q=par: v.reduce_max(
                    STAT[:, 40 + q:41 + q], S5[:, q * 512:(q + 1) * 512],
                    axis=AX.X, negate=True), sv)
                em._fence("v")
                maskmax_done[par] = sv.n
                em.wait("s", sv, maskmax_done[par])
                em.op("s", lambda s, q=par: s.activation(
                    E5[:, q * 512:(q + 1) * 512],
                    S5[:, q * 512:(q + 1) * 512], AF.Exp,
                    bias=STAT[:, 40 + q:41 + q],
                    accum_out=STAT[:, 42 + q:43 + q]), ss)
                em._fence("s")
                exp_done[par] = ss.n
                em.wait("v", ss, exp_done[par])
                em.op("v", lambda v, q=par: v.reciprocal(
                    STAT[:, 44 + q:45 + q], STAT[:, 42 + q:43 + q]), sv)
                em.wait("v", sv)
                if pbg_done[par]:
                    em.wait("v", st, pbg_done[par])
                em.op("v", lambda v, q=par: v.tensor_scalar(
                    Pb5[:, q * 512:(q + 1) * 512],
                    E5[:, q * 512:(q + 1) * 512],
                    STAT[:, 44 + q:45 + q], None, op0=ALU.mult), sv)
                em._fence("v")
                pb_done[par] = sv.n
                em.wait("t", sv, pb_done[par])
                if ptc_done[par]:
                    em.wait("t", ss, ptc_done[par])
                for j in range(4):
                    em.op("t", lambda t, jj=j, q=par, p=pts[par]:
                          t.transpose(
                        p[:, jj * 128:(jj + 1) * 128],
                        Pb5[:, q * 512 + jj * 128:q * 512 + (jj + 1) * 128],
                        idb[:, :]), st if j == 3 else None)
                pbg_done[par] = st.n
                em.wait("s", st)
                em.op("s", lambda s, q=par, p=pts[par]: s.copy(
                    PT5[:, q * 512:(q + 1) * 512], p[:, :]), ss)
                em._fence("s")
                ptc_done[par] = ss.n
                em.wait("t", ss, ptc_done[par])
                if ocevac_done[par]:
                    em.wait("t", ss, ocevac_done[par])
                for j in range(4):
                    em.op("t", lambda t, jj=j, gg=g, q=par, nn=n,
                          p=pvs[par]: t.matmul(
                        p[:, :],
                        Vt[:, (2 * nn + jj) * 128 + 64 * gg:
                           (2 * nn + jj) * 128 + 64 * (gg + 1)],
                        PT5[:, q * 512 + jj * 128:q * 512 + (jj + 1) * 128],
                        start=(jj == 0), stop=(jj == 3)),
                        st if j == 3 else None)
                pv_done[par] = st.n
                em.wait("s", st, pv_done[par])
                em.op("s", lambda s, gg=g, ii=i, q=cpar, p=pvs[par]:
                      s.copy(
                    OC[64 * gg:64 * (gg + 1),
                       q * D + ii * 128:q * D + (ii + 1) * 128],
                    p[:, :]), ss)
                em._fence("s")
                ocevac_done[par] = ss.n

                if h == 15:
                    oc_ready = max(ocevac_done[0], ocevac_done[1])
                    s1s = (s1p0, s1p1)
                    for hh in range(2):
                        em.wait("t", ss, oc_ready)
                        if s1evac_done[hh]:
                            em.wait("t", sv, s1evac_done[hh])
                        for i2 in range(ND):
                            em.op("t", lambda t, h2=hh, ii=i2, q=cpar:
                                  t.matmul(
                                s1s[h2][:, :],
                                OC[:, q * D + ii * 128:q * D + (ii + 1) * 128],
                                WB1[:, ii * D + h2 * 512:ii * D + (h2 + 1) * 512],
                                start=(ii == 0), stop=(ii == ND - 1)),
                                st if i2 == ND - 1 else None)
                        s1mm = st.n
                        em.wait("v", st, s1mm)
                        if spill_done[cpar]:
                            em.wait("v", sdo, spill_done[cpar])
                        em.op("v", lambda v, h2=hh, q=cpar: v.tensor_tensor(
                            S1C[:, q * D + h2 * 512:q * D + (h2 + 1) * 512],
                            s1s[h2][:, :],
                            BB[:, h2 * 512:(h2 + 1) * 512], op=ALU.add), sv)
                        em._fence("v")
                        s1evac_done[hh] = sv.n
                    em.wait("y", sv, max(s1evac_done[0], s1evac_done[1]))
                    em.op("y", lambda y, cc=c, q=cpar: y.dma_start(
                        out=W["s1d"].rearrange("(n p) m -> p n m", p=128)
                        [:, cc, :],
                        in_=S1C[:, q * D:(q + 1) * D]), sdo, 16)
                    spill_done[cpar] = sdo.n
            with nc.Block(no_gpsimd_drain=True) as blk:
                em.flush(blk)

def build_l1_moe(env):
    nc = env["nc"]
    A, WB1, STAT, GATES = env["A"], env["WB1"], env["STAT"], env["GATES"]
    W = env["W"]
    sd, sd2, sd3, sd4, sdo, sv, ss, st, sg = (
        env["sd"], env["sd2"], env["sd3"], env["sd4"], env["sdo"],
        env["sv"], env["ss"], env["st"], env["sg"])
    out = env["out"]
    stg = env["STG"]
    NE = 8
    NM = 16
    TT = 2

    gdbg = None
    if os.environ.get("MK_DBG3"):
        gdbg = nc.dram_tensor("gdbg", [T, D], F32, kind="ExternalOutput").ap()

    with ExitStack() as _esm:
        ent = _esm.enter_context
        H = ent(nc.sbuf_tensor("H", [128, NM * 1024], BF16))
        OUTA = ent(nc.sbuf_tensor("OUTA", [128, 8 * D], F32))
        W13G0 = ent(nc.sbuf_tensor("W13G0", [128, D], BF16))
        W13G1 = ent(nc.sbuf_tensor("W13G1", [128, D], BF16))
        W13U0 = ent(nc.sbuf_tensor("W13U0", [128, D], BF16))
        W13U1 = ent(nc.sbuf_tensor("W13U1", [128, D], BF16))
        W2B0 = ent(nc.sbuf_tensor("W2B0", [128, 4 * D], BF16))
        W2B1 = ent(nc.sbuf_tensor("W2B1", [128, 4 * D], BF16))
        GSIL = ent(nc.sbuf_tensor("GSIL", [128, D], F32))
        XC = ent(nc.sbuf_tensor("XC", [128, D], F32))
        S1R = ent(nc.sbuf_tensor("S1R", [128, D], F32))
        STG2 = ent(nc.sbuf_tensor("STG2", [128, D], F32))
        STG3 = ent(nc.sbuf_tensor("STG3", [128, D], F32))
        YST0 = ent(nc.sbuf_tensor("YST0", [128, D], F32))
        YST1 = ent(nc.sbuf_tensor("YST1", [128, D], F32))
        gp0 = ent(nc.psum_tensor("gp0", [128, 512], F32))
        gp1 = ent(nc.psum_tensor("gp1", [128, 512], F32))
        up0 = ent(nc.psum_tensor("up0", [128, 512], F32))
        up1 = ent(nc.psum_tensor("up1", [128, 512], F32))
        yp0 = ent(nc.psum_tensor("yp0", [128, 512], F32))
        yp1 = ent(nc.psum_tensor("yp1", [128, 512], F32))

        w13g = (W13G0, W13G1)
        w13u = (W13U0, W13U1)
        w2b = (W2B0, W2B1)
        ysts = (YST0, YST1)

        for tt in range(TT):
            em = mk_em(env)
            em.op("g", lambda g: g.memset(OUTA[:, :], 0.0))
            with nc.Block(no_gpsimd_drain=True) as blk:
                em.flush(blk)

            em = mk_em(env)
            nexp = int(os.environ.get("MK_NEXP", NE))

            def h_dma(e, m):
                par = m % 2
                sgg = (stg[0], STG2)[par]
                sgu = (stg[1], STG3)[par]
                em.wait("y", sv)
                em.wait("g", sv)
                em.op("y", lambda y, ee=e, mm=m, b=sgg: y.dma_start(
                    out=b[:, :], in_=W["w13"][ee, mm]), sd, 16)
                em.op("g", lambda g, ee=e, mm=m, b=sgu: g.dma_start(
                    out=b[:, :], in_=W["w13"][ee, 16 + mm]), sd2, 16)

            def y_dma(e, j):
                b = ysts[j % 2]
                em.wait("y" if j % 2 == 0 else "g", ss)
                qe = "y" if j % 2 == 0 else "g"
                qs = sd if j % 2 == 0 else sd2
                em.op(qe, lambda q, ee=e, mm=j, bb=b: q.dma_start(
                    out=bb[:, :], in_=W["w2"][ee, mm * 128:(mm + 1) * 128, :]),
                    qs, 16)

            def y_cast(e, j):
                b = ysts[j % 2]
                dst = w2b[(j // 4) % 2]
                em.wait("s", st)
                em.wait("s", sd if j % 2 == 0 else sd2)
                em.op("s", lambda s, bb=b, dd=dst, k=j % 4: s.copy(
                    dd[:, k * D:(k + 1) * D], bb[:, :]), ss)

            hready = {}
            ycast_ready = {}
            evacdone = {0: 0, 1: 0}
            ttready_p = {0: 0, 1: 0}
            ssready = 0

            def h_cast2(e, m):
                par = m % 2
                sgg = (stg[0], STG2)[par]
                sgu = (stg[1], STG3)[par]
                em.wait("v", st)
                em.wait("v", sd)
                em.op("v", lambda v, p=par, b=sgg: v.tensor_copy(
                    w13g[p][:, :], b[:, :]), sv)
                em.wait("v", sd2)
                em.op("v", lambda v, p=par, b=sgu: v.tensor_copy(
                    w13u[p][:, :], b[:, :]), sv)
                em._fence("v")
                hready[(e, m)] = sv.n

            for e in range(nexp):
                if e == 0:
                    h_dma(e, 0)
                    h_cast2(e, 0)
                for m in range(NM):
                    if m + 1 < NM:
                        h_dma(e, m + 1)
                        h_cast2(e, m + 1)
                    elif e + 1 < nexp:
                        h_dma(e + 1, 0)
                        h_cast2(e + 1, 0)
                    if m >= 8:
                        j = m - 8
                        y_dma(e, j)
                        y_cast(e, j)
                        if j % 4 == 3:
                            em._fence("s")
                            ycast_ready[(e, j // 4)] = ss.n
                    em.wait("t", sv, max(hready[(e, m)], ttready_p[m % 2]))
                    if ssready:
                        em.wait("t", ss, ssready)
                    for th in range(2):
                        gps = (gp0, gp1)[th]
                        ups = ((up0, up1) if m % 2 == 0 else (yp0, yp1))[th]
                        for dc in range(ND):
                            em.op("t", lambda t, d=dc, p=m % 2, h2=th, q=gps:
                                  t.matmul(
                                q[:, :], w13g[p][:, d * 128:(d + 1) * 128],
                                A[:, d * T + tt * 1024 + h2 * 512:
                                  d * T + tt * 1024 + (h2 + 1) * 512],
                                start=(d == 0), stop=(d == ND - 1)),
                                st if dc == ND - 1 else None)
                        for dc in range(ND):
                            em.op("t", lambda t, d=dc, p=m % 2, h2=th, q=ups:
                                  t.matmul(
                                q[:, :], w13u[p][:, d * 128:(d + 1) * 128],
                                A[:, d * T + tt * 1024 + h2 * 512:
                                  d * T + tt * 1024 + (h2 + 1) * 512],
                                start=(d == 0), stop=(d == ND - 1)),
                                st if dc == ND - 1 else None)
                    em.wait("s", st)
                    for th in range(2):
                        em.op("s", lambda s, h2=th, q=(gp0, gp1)[th]:
                              s.activation(
                            GSIL[:, h2 * 512:(h2 + 1) * 512], q[:, :],
                            AF.Silu), ss)
                    em._fence("s")
                    ssready = ss.n
                    em.wait("v", ss, ssready)
                    for th in range(2):
                        em.op("v", lambda v, mm=m, h2=th,
                              q=((up0, up1) if m % 2 == 0 else (yp0, yp1))[th]:
                              v.tensor_tensor(
                            H[:, mm * 1024 + h2 * 512:mm * 1024 + (h2 + 1) * 512],
                            GSIL[:, h2 * 512:(h2 + 1) * 512], q[:, :],
                            op=ALU.mult), sv)
                    em._fence("v")
                    ttready_p[m % 2] = sv.n

                for s4 in range(4):
                    for tc in range(8):
                        cidx = tt * 8 + tc
                        for eb in range(2):
                            par = (tc * 2 + eb) % 2
                            yps = (yp0, yp1)[par]
                            w = evacdone[par]
                            if s4 == 0 and tc == 0 and eb == 0:
                                w = max(w, ttready_p[0], ttready_p[1])
                            if w:
                                em.wait("t", sv, w)
                            if tc == 0 and eb == 0:
                                em.wait("t", ss, ycast_ready[(e, s4)])
                            for k in range(4):
                                m = s4 * 4 + k
                                em.op("t", lambda t, mm=m, k2=k, t2=tc, b=eb,
                                      p=yps, dd=w2b[s4 % 2]: t.matmul(
                                    p[:, :],
                                    H[:, mm * 1024 + t2 * 128:
                                      mm * 1024 + (t2 + 1) * 128],
                                    dd[:, k2 * D + b * 512:
                                       k2 * D + (b + 1) * 512],
                                    start=(k2 == 0), stop=(k2 == 3)),
                                    st if k == 3 else None)
                            em.wait("v", st)
                            em.op("v", lambda v, t2=tc, b=eb, ci=cidx, ee=e,
                                  p=yps: v.scalar_tensor_tensor(
                                OUTA[:, t2 * D + b * 512:t2 * D + (b + 1) * 512],
                                p[:, :], GATES[:, ci * 8 + ee:ci * 8 + ee + 1],
                                OUTA[:, t2 * D + b * 512:t2 * D + (b + 1) * 512],
                                op0=ALU.mult, op1=ALU.add), sv)
                            em._fence("v")
                            evacdone[par] = sv.n
                        if 1 <= s4 <= 2 and tc < 4:
                            j = (s4 + 1) * 4 + tc
                            y_dma(e, j)
                            y_cast(e, j)
                            if j % 4 == 3:
                                em._fence("s")
                                ycast_ready[(e, s4 + 1)] = ss.n
            with nc.Block(no_gpsimd_drain=True) as blk:
                em.flush(blk)

            for tc in range(8):
                cidx = tt * 8 + tc
                em = mk_em(env)
                em.wait("s", sdo)
                em.wait("y", sv)
                em.wait("g", sv)
                em.op("y", lambda y, ci=cidx: y.dma_start(
                    out=XC[:, :],
                    in_=W["x1d"].rearrange("(n p) m -> p n m", p=128)[:, ci, :]),
                    sd, 16)
                em.op("g", lambda g, ci=cidx: g.dma_start(
                    out=S1R[:, :],
                    in_=W["s1d"].rearrange("(n p) m -> p n m", p=128)[:, ci, :]),
                    sd2, 16)
                em.op("s", lambda s, t2=tc: s.activation(
                    GSIL[:, :], OUTA[:, t2 * D:(t2 + 1) * D], AF.Silu), ss)
                env["fence_s"](em)
                if gdbg is not None:
                    em.wait("g", ss)
                    em.op("g", lambda g, ci=cidx: g.dma_start(
                        out=gdbg.rearrange("(n p) m -> p n m", p=128)[:, ci, :],
                        in_=GSIL[:, :]), sdo, 16)
                    em.wait("g", sdo)
                    em.wait("v", sdo)
                em.wait("v", ss)
                em.wait("v", sd)
                em.wait("v", sd2)
                em.op("v", lambda v: v.tensor_tensor(
                    GSIL[:, :], GSIL[:, :], S1R[:, :], op=ALU.mult), sv)
                em.wait("v", sv)
                em.op("v", lambda v: v.tensor_tensor(
                    GSIL[:, :], GSIL[:, :], XC[:, :], op=ALU.add), sv)
                env["fence_v"](em)
                em.wait("y", sv)
                em.op("y", lambda y, ci=cidx: y.dma_start(
                    out=out.rearrange("(n p) m -> p n m", p=128)[:, ci, :],
                    in_=GSIL[:, :]), sdo, 16)
                with nc.Block(no_gpsimd_drain=True) as blk:
                    em.flush(blk)


def kernel(**inputs):
    n = 8
    x = np.ascontiguousarray(np.asarray(inputs["L0_attn_inputs"],
                                        dtype=np.float32))
    wmap = {
        "wq0": "L0_attn_Wq", "bq0": "L0_attn_bq", "wr0": "L0_ffn_w_router",
        "wb0": "L0_ffn_w_block", "wq1": "L1_attn_Wq", "bq1": "L1_attn_bq",
        "wk1": "L1_attn_Wk", "bk1": "L1_attn_bk", "wv1": "L1_attn_Wv",
        "bv1": "L1_attn_bv", "wo1": "L1_attn_Wo", "bo1": "L1_attn_bo",
        "mr1": "L1_ffn_moe_router", "w13": "L1_ffn_moe_w13",
        "w2": "L1_ffn_moe_w2",
    }
    ws = {k: np.ascontiguousarray(np.asarray(inputs[v], dtype=np.float32))
          for k, v in wmap.items()}
    idx = (np.arange(2)[None, :, None] * 512 + np.arange(8)[:, None, None] * 64
           + np.arange(64)[None, None, :]).reshape(-1)
    ws["bq0t"] = np.ascontiguousarray(ws["bq0"].reshape(8, 128).T)
    ws["wq1"] = np.ascontiguousarray(ws["wq1"][:, idx])
    ws["bq1"] = np.ascontiguousarray(
        ws["bq1"][idx].reshape(8, 2, 64).transpose(1, 2, 0).reshape(128, 8))
    ws["wo1"] = np.ascontiguousarray(ws["wo1"][idx, :])
    ws["wk1"] = np.ascontiguousarray(
        ws["wk1"].reshape(8, 128, 128).transpose(1, 0, 2).reshape(128, 1024))
    ws["wv1"] = np.ascontiguousarray(
        ws["wv1"].reshape(8, 128, 128).transpose(1, 0, 2).reshape(128, 1024))
    ws["mr1"] = np.ascontiguousarray(
        ws["mr1"].reshape(8, 128, 8).transpose(1, 0, 2).reshape(128, 64))
    ws["bk1"] = np.ascontiguousarray(ws["bk1"].reshape(128, 1))
    ws["bv1"] = np.ascontiguousarray(ws["bv1"].reshape(128, 1))
    ws["w13"] = np.ascontiguousarray(
        ws["w13"].reshape(8, 8, 128, 32, 128).transpose(0, 3, 2, 1, 4)
        .reshape(8, 32, 128, 1024))
    nc = build(do_l0=True, do_l1=True)
    in_maps = []
    for c in range(n):
        m = {"x": np.ascontiguousarray(x[c])}
        m.update(ws)
        in_maps.append(m)
    res = run_bass_kernel_spmd(nc, in_maps, core_ids=list(range(n)))
    outs = [np.asarray(res.results[c]["out"]) for c in range(n)]
    return np.stack(outs, axis=0)
